# version 14
# baseline (speedup 1.0000x reference)
from contextlib import ExitStack
import numpy as np
import concourse.bass as bass
import concourse.mybir as mybir

F32 = mybir.dt.float32
BF16 = mybir.dt.bfloat16
I16 = mybir.dt.int16
U16 = mybir.dt.uint16
ALU = mybir.AluOpType
AF = mybir.ActivationFunctionType
AX = mybir.AxisListType

MAGIC = 12582912.0
C0 = float(np.float32(np.log(np.float32(1e-10))))


class Sem:
    __slots__ = ("h", "v")
    def __init__(self, nc, name):
        self.h = nc._stack.enter_context(nc.semaphore(name))
        self.v = 0
    def inc(self, n=1):
        self.v += n
        return self.h, n
    def tail(self):
        return self.h, self.v


class Pipe:
    def __init__(self, b, name, nbuf, pq=1, cq=1, dma_p=False, dma_c=False):
        self.nbuf = nbuf
        self.pq, self.cq = pq, cq
        self.ps = [b.sem(f"{name}_p{i}" if dma_p else f"{name}_p")
                   for i in range(nbuf if dma_p else 1)]
        self.cs = [b.sem(f"{name}_c{i}" if dma_c else f"{name}_c")
                   for i in range(nbuf if dma_c else 1)]
        if not dma_p:
            self.ps = self.ps * nbuf if False else self.ps
        self.pb = [s.v for s in self.ps]
        self.cb = [s.v for s in self.cs]
        self._pn = len(self.ps)
        self._cn = len(self.cs)
    def pwait(self, eng, i):
        if self._cn == 1:
            eng.wait_ge(self.cs[0].h, self.cb[0] + max(0, i - self.nbuf + 1) * self.cq)
        else:
            j = i % self._cn
            rounds = max(0, (i - self.nbuf) // self._cn + 1)
            eng.wait_ge(self.cs[j].h, self.cb[j] + rounds * self.cq)
    def pinc(self, i):
        return self.ps[i % self._pn].inc(self.pq)
    def pinc_part(self, i, n=16):
        return self.ps[i % self._pn].inc(n)
    def cwait(self, eng, i):
        j = i % self._pn
        eng.wait_ge(self.ps[j].h, self.pb[j] + (i // self._pn + 1) * self.pq)
    def cinc(self, i):
        return self.cs[i % self._cn].inc(self.cq)


class B:
    def __init__(self, cfg):
        self.cfg = cfg
        nc = bass.Bass("TRN2", target_bir_lowering=False, debug=False)
        nc._stack = ExitStack()
        self.nc = nc
        self.sems = {}
    def sem(self, name):
        if name not in self.sems:
            self.sems[name] = Sem(self.nc, name)
        return self.sems[name]
    def sbuf(self, name, shape, dtype):
        return self.nc._stack.enter_context(self.nc.sbuf_tensor(name, shape, dtype))
    def psum(self, name, shape, dtype=F32):
        return self.nc._stack.enter_context(self.nc.psum_tensor(name, shape, dtype))
    def dram(self, name, shape, dtype, kind="Internal"):
        return self.nc.dram_tensor(name, shape, dtype, kind=kind)


def default_cfg(T=4096):
    return dict(E=1024, T=T, G=512, H=16, D=64, ACH=128, SCH=64, I=3072, n_cores=8)


from concourse.bass import BassScalarEngine as _BSE
if not hasattr(_BSE, "engine_nop"):
    _BSE.engine_nop = lambda self: self.drain()
try:
    from concourse.bass import BassGpSimd as _BGS
    if not hasattr(_BGS, "engine_nop"):
        _BGS.engine_nop = lambda self: self.drain()
except Exception:
    pass


from contextlib import ExitStack



class Scope:
    def __init__(self, b):
        self.b = b
        self.es = ExitStack()
    def sbuf(self, name, shape, dtype):
        return self.es.enter_context(self.b.nc.sbuf_tensor(name, shape, dtype))
    def psum(self, name, shape, dtype=F32):
        return self.es.enter_context(self.b.nc.psum_tensor(name, shape, dtype))
    def banks(self, name, n):
        return [self.psum(f"{name}{i}", [128, 512], F32) for i in range(n)]
    def close(self):
        self.es.close()


def emit_weight_prep(b, w, wT_out, ident_b, tag):
    nc = b.nc
    O, Iin = w.shape
    OT, ITn = O // 128, Iin // 128
    nb = Iin // 32
    sc = Scope(b)
    wtmp = sc.sbuf(f"{tag}_tmp", [128, Iin], F32)
    absb = sc.sbuf(f"{tag}_abs", [128, nb], F32)
    sblk = sc.sbuf(f"{tag}_s", [128, nb], F32)
    rblk = sc.sbuf(f"{tag}_rs", [128, nb], F32)
    q16 = sc.sbuf(f"{tag}_q", [128, Iin], I16)
    qf = sc.sbuf(f"{tag}_qf", [128, Iin], F32)
    wq = sc.sbuf(f"{tag}_wq", [128, Iin], BF16)
    pst = [sc.psum(f"{tag}_pst{i}", [128, 1024], BF16) for i in range(2)]
    wtr = sc.sbuf(f"{tag}_tr", [128, ITn * 128], BF16)
    ld = b.sem("g_ldx")
    for ot in range(OT):
        with nc.Block() as blk:
            @blk.sync
            def _(s):
                s.dma_start(out=wtmp[:, :], in_=w[ot * 128:(ot + 1) * 128, :]).then_inc(*ld.inc(16))
                s.wait_ge(*ld.tail())

            @blk.vector
            def _(v):
                v.wait_ge(*ld.tail())
                v.tensor_reduce(absb.ap().rearrange("p (bk o) -> p bk o", o=1),
                                wtmp.ap().rearrange("p (bk k) -> p bk k", k=32),
                                AX.X, ALU.max, apply_absolute_value=True)
                v.drain()
                v.tensor_scalar(sblk[:, :], absb[:, :], 1.0 / 7.0, 1e-12, ALU.mult, ALU.max)
                v.drain()
                v.reciprocal(rblk[:, :], sblk[:, :])
                v.drain()
                v.tensor_tensor(qf.ap().rearrange("p (bk k) -> p bk k", k=32),
                                wtmp.ap().rearrange("p (bk k) -> p bk k", k=32),
                                bass.AP(rblk, 0, [[nb, 128], [1, nb], [0, 32]]), ALU.mult)
                v.drain()
                v.tensor_single_scalar(qf[:, :], qf[:, :], MAGIC, ALU.add)
                v.drain()
                v.tensor_single_scalar(q16[:, :], qf[:, :], MAGIC, ALU.subtract)
                v.drain()
                v.tensor_tensor(wq.ap().rearrange("p (bk k) -> p bk k", k=32),
                                q16.ap().rearrange("p (bk k) -> p bk k", k=32),
                                bass.AP(sblk, 0, [[nb, 128], [1, nb], [0, 32]]), ALU.mult)

        with nc.Block() as blk:
            tp = Pipe(b, "g_tp", 2)
            ep = Pipe(b, "wp_ep", 1, cq=16, dma_c=True)

            @blk.tensor
            def _(te):
                for it in range(ITn):
                    tp.pwait(te, it)
                    te.transpose(pst[it % 2][:, 0:128],
                                 wq[:, it * 128:(it + 1) * 128], ident_b[:, :]).then_inc(*tp.pinc(it))

            @blk.vector
            def _(v):
                ep.pwait(v, 0)
                for it in range(ITn):
                    tp.cwait(v, it)
                    v.tensor_copy(wtr[:, it * 128:(it + 1) * 128],
                                  pst[it % 2][:, 0:128]).then_inc(*tp.cinc(it))
                v.engine_nop().then_inc(*ep.pinc(0))

            @blk.sync
            def _(s):
                ep.cwait(s, 0)
                s.dma_start(out=bass.AP(wT_out, ot * 128, [[O, 128], [128 * O, ITn], [1, 128]]),
                            in_=bass.AP(wtr, 0, [[ITn * 128, 128], [128, ITn], [1, 128]])).then_inc(*ep.cinc(0))
    with nc.Block() as blk:
        @blk.sync
        def _(s):
            for sm in b.sem("wp_ep_c0"), b.sem("wp_ep_c1"):
                s.wait_ge(*sm.tail())
    sc.close()


def emit_cast_bf16(b, w, w_bf, tag):
    nc = b.nc
    R, Ccols = w.shape
    RT = R // 128
    sc = Scope(b)
    stg = [sc.sbuf(f"{tag}_stg{i}", [128, Ccols], F32) for i in range(2)]
    ob = [sc.sbuf(f"{tag}_ob{i}", [128, Ccols], BF16) for i in range(2)]
    with nc.Block() as blk:
        lp = Pipe(b, "g_bl", 2, pq=16, dma_p=True)
        epipe = Pipe(b, "g_bo", 2, cq=16, dma_c=True)

        @blk.sync
        def _(s):
            for rt in range(RT):
                lp.pwait(s, rt)
                s.dma_start(out=stg[rt % 2][:, :], in_=w[rt * 128:(rt + 1) * 128, :]).then_inc(*lp.pinc(rt))

        @blk.vector
        def _(v):
            for rt in range(RT):
                lp.cwait(v, rt)
                epipe.pwait(v, rt)
                v.tensor_copy(ob[rt % 2][:, :], stg[rt % 2][:, :])
                v.engine_nop().then_inc(*lp.cinc(rt))
                v.engine_nop().then_inc(*epipe.pinc(rt))

        @blk.gpsimd
        def _(g):
            for rt in range(RT):
                epipe.cwait(g, rt)
                g.dma_start(out=w_bf[rt * 128:(rt + 1) * 128, :], in_=ob[rt % 2][:, :]).then_inc(*epipe.cinc(rt))
            for sm in epipe.cs:
                g.wait_ge(*sm.tail())
    sc.close()


def emit_ln_group(b, blk, st, banks, xg, xb, sqb, xnc_sb, rb_sb, rcol_sb, want_rcol,
                  x2n_out=None):
    cfg = b.cfg
    E, G = cfg["E"], cfg["G"]
    ET, TPG = E // 128, G // 128
    s_mm = b.sem("ln_mm")
    s_ev = b.sem("ln_ev")
    ps_st, ps_st2, ps_mb, ps_rb, ps_rc = banks[0], banks[1], banks[2], banks[3], banks[4]

    lnc = Pipe(b, "ln_c", 2)

    @blk.scalar
    def _(s):
        for e in range(ET):
            lnc.pwait(s, e)
            sl = slice((e % 2) * G, (e % 2) * G + G)
            s.activation(xb[:, sl], xg[:, e * G:(e + 1) * G], AF.Copy)
            s.drain()
            s.activation(sqb[:, sl], xg[:, e * G:(e + 1) * G], AF.Square)
            s.engine_nop().then_inc(*lnc.pinc(e))

    @blk.tensor
    def _(te):
        for e in range(ET):
            lnc.cwait(te, e)
            sl = slice((e % 2) * G, (e % 2) * G + G)
            mm1 = te.matmul(ps_st[0:1, 0:G], st["ones_col_b"][:, :], xb[:, sl],
                            start=(e == 0), stop=(e == ET - 1))
            mm2 = te.matmul(ps_st2[0:1, 0:G], st["ones_col_b"][:, :], sqb[:, sl],
                            start=(e == 0), stop=(e == ET - 1))
            if e == ET - 1:
                mm1.then_inc(*lnc.cinc(e))
                mm2.then_inc(*s_mm.inc(2))
            else:
                mm2.then_inc(*lnc.cinc(e))
    k_stat = s_mm.v

    @blk.scalar
    def _(s):
        s.wait_ge(s_mm.h, k_stat)
        s.activation(st["msq_row"][0:1, 0:G], ps_st[0:1, 0:G], AF.Square, scale=1.0 / E)
        s.drain()
        s.activation(st["m_row"][0:1, 0:G], ps_st[0:1, 0:G], AF.Copy, scale=1.0 / E)
        s.engine_nop().then_inc(*s_ev.inc())
    k_msq = s_ev.v

    @blk.vector
    def _(v):
        v.wait_ge(s_ev.h, k_msq)
        v.scalar_tensor_tensor(st["var_row"][0:1, 0:G], ps_st2[0:1, 0:G], 1.0 / E,
                               st["msq_row"][0:1, 0:G], ALU.mult, ALU.subtract)
        v.engine_nop().then_inc(*s_ev.inc())
    k_var = s_ev.v

    @blk.scalar
    def _(s):
        s.wait_ge(s_ev.h, k_var)
        s.activation(st["lnv_row"][0:1, 0:G], st["var_row"][0:1, 0:G], AF.Ln, bias=st["eps_t"][0:1, 0:1])
        s.drain()
        s.activation(st["r_row"][0:1, 0:G], st["lnv_row"][0:1, 0:G], AF.Exp, scale=-0.5)
        s.engine_nop().then_inc(*s_ev.inc())
    k_r = s_ev.v

    @blk.tensor
    def _(te):
        te.wait_ge(s_ev.h, k_r)
        mm3 = te.matmul(ps_mb[:, 0:G], st["ones_row_f"][0:1, :], st["m_row"][0:1, 0:G], start=True, stop=True)
        mm3 = te.matmul(ps_rb[:, 0:G], st["ones_row_f"][0:1, :], st["r_row"][0:1, 0:G], start=True, stop=True)
        if want_rcol:
            for tt in range(TPG):
                mm3 = te.matmul(ps_rc[:, tt:tt + 1], st["r_row"][0:1, tt * 128:(tt + 1) * 128],
                                st["one_f"][0:1, 0:1], start=True, stop=True)
        mm3.then_inc(*s_mm.inc())
    k_bc = s_mm.v

    @blk.vector
    def _(v):
        v.wait_ge(s_mm.h, k_bc)
        v.tensor_copy(rb_sb[:, 0:G], ps_rb[:, 0:G])
        if want_rcol:
            v.tensor_copy(rcol_sb[:, 0:TPG], ps_rc[:, 0:TPG])
        if xnc_sb is not None:
            for e in range(ET):
                v.tensor_tensor(xnc_sb[:, e * G:(e + 1) * G], xg[:, e * G:(e + 1) * G],
                                ps_mb[:, 0:G], ALU.subtract)
            v.drain()
        if x2n_out is not None:
            for e in range(ET):
                v.tensor_tensor(x2n_out[:, e * G:(e + 1) * G], xg[:, e * G:(e + 1) * G],
                                ps_mb[:, 0:G], ALU.subtract)
            v.drain()
            for e in range(ET):
                v.tensor_tensor(x2n_out[:, e * G:(e + 1) * G], x2n_out[:, e * G:(e + 1) * G],
                                rb_sb[:, 0:G], ALU.mult)
            v.drain()
        v.engine_nop().then_inc(*s_ev.inc())
    return s_ev.v


def make_stats(b):
    G = b.cfg["G"]
    return dict(
        m_row=b.sbuf("st_m", [1, G], F32),
        msq_row=b.sbuf("st_msq", [1, G], F32),
        var_row=b.sbuf("st_var", [1, G], F32),
        lnv_row=b.sbuf("st_lnv", [1, G], F32),
        r_row=b.sbuf("st_r", [1, G], F32),
        magic_col=b.sbuf("st_mag", [128, 1], F32),
    )


def emit_attn_a(b, st, src, dst, wa_bf, wp_bf, ident_f, maskT_b):
    cfg = b.cfg
    nc = b.nc
    E, T, G, H, D, ACH = cfg["E"], cfg["T"], cfg["G"], cfg["H"], cfg["D"], cfg["ACH"]
    ET, TPG, NG, CPG = E // 128, G // 128, T // G, G // ACH
    HT = H // 2
    assert ACH == 128 and TPG == CPG and D == 64

    sc = Scope(b)
    xg = [sc.sbuf(f"a_xg{i}", [128, ET * G], F32) for i in range(2)]
    ubuf = sc.sbuf("a_ubuf", [128, 2 * ET * G], BF16)
    lnb = sc.sbuf("a_lnb", [128, 4 * G], BF16)
    xnc = sc.sbuf("a_xnc", [128, ET * G], BF16)
    rb_sb = sc.sbuf("a_rb", [128, G], F32)
    rcol = sc.sbuf("a_rcol", [128, TPG], F32)
    wstr = sc.sbuf("a_wstr", [128, 2 * ET * 128], BF16)
    wvstr = sc.sbuf("a_wvstr", [128, 2 * ET * 512], BF16)
    f32a = sc.sbuf("a_f32a", [128, 2 * G], F32)
    f32b = sc.sbuf("a_f32b", [128, 2 * G], F32)
    f32c = sc.sbuf("a_f32c", [128, 2 * G], F32)
    vbuf = sc.sbuf("a_vbuf", [128, TPG * H * 65], BF16)
    kbuf = sc.sbuf("a_kbuf", [128, TPG * E], BF16)
    scT = sc.sbuf("a_scT", [128, H * ACH], BF16)
    den2 = sc.sbuf("a_den2", [128, H], F32)
    Ssb = sc.sbuf("a_Ssb", [128, HT * 65], BF16)
    ysb = sc.sbuf("a_ysb", [128, TPG * E], F32)
    yT = sc.sbuf("a_yT", [128, ET * G], BF16)
    resg = sc.sbuf("a_resg", [128, 2 * G], F32)
    bk = sc.banks("a_bk", 7)
    qkb = ubuf

    def num_slot(h):
        if h < 7:
            return bk[2], h * 65
        if h < 14:
            return bk[3], (h - 7) * 65
        return bk[4], (h - 14) * 65

    def S_slot(h):
        hp = h // 2
        bank = bk[5] if hp < 7 else bk[6]
        return bank, (hp % 7) * 65, (h % 2) * 64

    ld_x = [b.sem("g_ldxA"), b.sem("g_ldxB")]
    s_st = b.sem("g_stres")

    with nc.Block() as blk:
        @blk.vector
        def _(v):
            v.memset(bass.AP(vbuf, 64, [[TPG * H * 65, 128], [65, TPG * H], [1, 1]]), 1.0)
            v.memset(bk[5][:, :], 0.0)
            v.memset(bk[6][:, :], 0.0)

    for g in range(NG):
        with nc.Block() as blk:
            @blk.sync
            def _(s):
                if g == 0:
                    s.dma_start(out=bass.AP(xg[0], 0, [[ET * G, 128], [G, ET], [1, G]]),
                                in_=bass.AP(src, 0, [[T, 128], [128 * T, ET], [1, G]])).then_inc(*ld_x[0].inc(16))
                if g + 1 < NG:
                    s.dma_start(out=bass.AP(xg[(g + 1) % 2], 0, [[ET * G, 128], [G, ET], [1, G]]),
                                in_=bass.AP(src, (g + 1) * G, [[T, 128], [128 * T, ET], [1, G]])).then_inc(*ld_x[(g + 1) % 2].inc(16))
                s.wait_ge(*ld_x[g % 2].tail())
        cur = xg[g % 2]

        with nc.Block() as blk:
            emit_ln_group(b, blk, st, bk, cur, lnb[:, 0:2 * G], lnb[:, 2 * G:4 * G],
                          xnc, rb_sb, rcol, True)

        with nc.Block() as blk:
            wpipe = Pipe(b, "g_w", 2, pq=16, dma_p=True)
            mpipe = Pipe(b, "g_m", 2)
            spipe = Pipe(b, "g_s", 2)
            NQK = 2 * ET

            @blk.sync
            def _(s):
                for i in range(NQK):
                    s.wait_ge(mpipe.cs[0].h, mpipe.cb[0] + max(0, i - 1))
                    sec, f = (0, i) if i < ET else (1, i - ET)
                    s.dma_start(out=bass.AP(wstr, (i % 2) * ET * 128,
                                            [[2 * ET * 128, 128], [128, ET], [1, 128]]),
                                in_=bass.AP(wa_bf, sec * E + f * 128,
                                            [[3 * E, 128], [128 * 3 * E, ET], [1, 128]])).then_inc(*wpipe.pinc_part(i))

            @blk.tensor
            def _(te):
                for i in range(NQK):
                    wpipe.cwait(te, i)
                    mpipe.pwait(te, i)
                    buf = wstr[:, (i % 2) * ET * 128:(i % 2) * ET * 128 + ET * 128]
                    for e in range(ET):
                        mm = te.matmul(bk[i % 2][:, 0:G], buf[:, e * 128:(e + 1) * 128],
                                       xnc[:, e * G:(e + 1) * G],
                                       start=(e == 0), stop=(e == ET - 1))
                    mm.then_inc(*mpipe.pinc(i))

            @blk.scalar
            def _(s):
                for i in range(NQK):
                    spipe.cwait(s, i)
                    sl = slice((i % 2) * G, (i % 2) * G + G)
                    s.activation(f32c[:, sl], f32b[:, sl], AF.Exp).then_inc(*spipe.cinc(i))

            @blk.vector
            def _(v):
                for i in range(NQK + 1):
                    if i < NQK:
                        mpipe.cwait(v, i)
                        sl = slice((i % 2) * G, (i % 2) * G + G)
                        v.tensor_tensor(f32a[:, sl], bk[i % 2][:, 0:G], rb_sb[:, 0:G], ALU.mult)
                        v.engine_nop().then_inc(*mpipe.cinc(i))
                        v.drain()
                        v.tensor_single_scalar(f32b[:, sl], f32a[:, sl], 0.0, ALU.min)
                        v.engine_nop().then_inc(*spipe.pinc(i))
                    if i >= 1:
                        j = i - 1
                        v.wait_ge(spipe.cs[0].h, spipe.cb[0] + j + 1)
                        sj = slice((j % 2) * G, (j % 2) * G + G)
                        v.scalar_tensor_tensor(qkb[:, j * G:(j + 1) * G], f32a[:, sj], 0.0,
                                               f32c[:, sj], ALU.max, ALU.add)
                        v.drain()

        HW_ = min(512, E)
        NH = E // HW_
        with nc.Block() as blk:
            wv = Pipe(b, "g_wv", 2, pq=16, dma_p=True)
            mv = Pipe(b, "g_mv", 1)
            kp = Pipe(b, "g_kp", 2)
            ke = Pipe(b, "g_ke", 2)

            @blk.sync
            def _(s):
                for r in range(2 * NH):
                    sec = 2 if r < NH else 1
                    half = r % NH
                    s.wait_ge(mv.cs[0].h, mv.cb[0] + max(0, r - 1))
                    s.dma_start(out=bass.AP(wvstr, (r % 2) * ET * 512,
                                            [[2 * ET * 512, 128], [512, ET], [1, HW_]]),
                                in_=bass.AP(wa_bf, sec * E + half * HW_,
                                            [[3 * E, 128], [128 * 3 * E, ET], [1, HW_]])).then_inc(*wv.pinc_part(r))

            @blk.tensor
            def _(te):
                for r in range(2 * NH):
                    mv.pwait(te, r)
                    wv.cwait(te, r)
                    for e in range(ET):
                        for tt in range(TPG):
                            mm = te.matmul(bk[tt][:, 0:HW_],
                                           xnc[:, e * G + tt * 128:e * G + (tt + 1) * 128],
                                           wvstr[:, ((r % 2) * ET + e) * 512:((r % 2) * ET + e) * 512 + HW_],
                                           start=(e == 0), stop=(e == ET - 1))
                    mm.then_inc(*mv.pinc(r))

            @blk.scalar
            def _(s):
                NJ = NH * TPG
                jx = 0
                for r in range(2 * NH):
                    mv.cwait(s, r)
                    half = r % NH
                    hph = HW_ // 64
                    for tt in range(TPG):
                        if r < NH:
                            out_ap = bass.AP(vbuf, tt * H * 65 + half * hph * 65,
                                             [[TPG * H * 65, 128], [65, hph], [1, 64]])
                            s.activation(out_ap, bk[tt][:, 0:HW_], AF.Copy, scale=rcol[:, tt:tt + 1])
                        else:
                            j = (r - NH) * TPG + tt
                            kp.pwait(s, j)
                            s.activation(f32a[:, (j % 2) * G:(j % 2) * G + HW_],
                                         bk[tt][:, 0:HW_], AF.Copy, scale=rcol[:, tt:tt + 1])
                            s.engine_nop().then_inc(*kp.pinc(j))
                            if j >= 1:
                                ke.cwait(s, j - 1)
                                slx = slice(((j - 1) % 2) * G, ((j - 1) % 2) * G + HW_)
                                s.activation(f32c[:, slx], f32b[:, slx], AF.Exp).then_inc(*ke.cinc(j - 1))
                    s.engine_nop().then_inc(*mv.cinc(r))
                ke.cwait(s, NJ - 1)
                slx = slice(((NJ - 1) % 2) * G, ((NJ - 1) % 2) * G + HW_)
                s.activation(f32c[:, slx], f32b[:, slx], AF.Exp).then_inc(*ke.cinc(NJ - 1))

            @blk.vector
            def _(v):
                for j in range(NH * TPG + 1):
                    if j < NH * TPG:
                        kp.cwait(v, j)
                        ke.pwait(v, j)
                        sl = slice((j % 2) * G, (j % 2) * G + HW_)
                        v.tensor_single_scalar(f32b[:, sl], f32a[:, sl], 0.0, ALU.min)
                        v.engine_nop().then_inc(*ke.pinc(j))
                    if j >= 1:
                        jj = j - 1
                        v.wait_ge(ke.cs[0].h, ke.cb[0] + jj + 1)
                        sjl = slice((jj % 2) * G, (jj % 2) * G + HW_)
                        half2, tt2 = divmod(jj, TPG)
                        v.scalar_tensor_tensor(
                            kbuf[:, tt2 * E + half2 * HW_:tt2 * E + (half2 + 1) * HW_],
                            f32a[:, sjl], 0.0, f32c[:, sjl], ALU.max, ALU.add)
                        v.drain()
                        v.engine_nop().then_inc(*kp.cinc(jj))

        with nc.Block() as blk:
            scp = Pipe(b, "g_scp", 1)
            nmp = Pipe(b, "g_nmp", 1)
            sup = Pipe(b, "g_sup", 1)
            msk = Pipe(b, "g_msk", 1)
            dnp = Pipe(b, "g_dn", 1)

            @blk.tensor
            def _(te):
                for c in range(CPG):
                    gc = g * CPG + c
                    for hp in range(HT):
                        i = c * HT + hp
                        scp.pwait(te, i)
                        for par in range(2):
                            kT = qkb[64 * par:64 * par + 64,
                                     (ET + hp) * G + c * 128:(ET + hp) * G + (c + 1) * 128]
                            qT = qkb[64 * par:64 * par + 64,
                                     hp * G + c * 128:hp * G + (c + 1) * 128]
                            mm = te.matmul(bk[par][:, 0:128], kT, qT, start=True, stop=True,
                                           tile_position=(64 * par, 0))
                        mm.then_inc(*scp.pinc(i))
                    msk.cwait(te, c)
                    nmp.pwait(te, c)
                    te.wait_ge(sup.cs[0].h, sup.cb[0] + c)
                    for h in range(H):
                        bankn, off = num_slot(h)
                        mmn = te.matmul(bankn[:, off:off + 65],
                                        scT[:, h * 128:(h + 1) * 128],
                                        vbuf[:, c * H * 65 + h * 65:c * H * 65 + (h + 1) * 65],
                                        start=True, stop=(gc == 0))
                        if gc > 0:
                            par = h % 2
                            mmn = te.matmul(bankn[:, off:off + 65],
                                            qkb[64 * par:64 * par + 64,
                                                (h // 2) * G + c * 128:(h // 2) * G + (c + 1) * 128],
                                            Ssb[64 * par:64 * par + 64,
                                                (h // 2) * 65:(h // 2) * 65 + 65],
                                            start=False, stop=True, tile_position=(64 * par, 0))
                    mmn.then_inc(*nmp.pinc(c))
                    sup.pwait(te, c)
                    for h in range(H):
                        bs, off, pbase = S_slot(h)
                        mms = te.matmul(bs[pbase:pbase + 64, off:off + 65],
                                        kbuf[:, c * E + h * 64:c * E + (h + 1) * 64],
                                        vbuf[:, c * H * 65 + h * 65:c * H * 65 + (h + 1) * 65],
                                        start=False, stop=(gc == NG * CPG - 1),
                                        skip_group_check=True, tile_position=(0, pbase))
                    mms.then_inc(*sup.pinc(c))

            @blk.vector
            def _(v):
                for c in range(CPG):
                    v.wait_ge(nmp.ps[0].h, nmp.pb[0] + c)
                    for hp in range(HT):
                        i = c * HT + hp
                        scp.cwait(v, i)
                        for par in range(2):
                            h = 2 * hp + par
                            v.tensor_tensor(scT[:, h * 128:(h + 1) * 128], bk[par][:, 0:128],
                                            maskT_b[:, :], ALU.mult)
                        v.engine_nop().then_inc(*scp.cinc(i))
                    v.engine_nop().then_inc(*msk.pinc(c))
                    nmp.cwait(v, c)
                    dnp.pwait(v, c)
                    nh = min(H, 7)
                    v.tensor_single_scalar(den2[:, 0:nh], bass.AP(bk[2], 64, [[512, 128], [65, nh]]),
                                           1e-6, ALU.add)
                    if H > 7:
                        v.tensor_single_scalar(den2[:, 7:14], bass.AP(bk[3], 64, [[512, 128], [65, 7]]),
                                               1e-6, ALU.add)
                    if H > 14:
                        v.tensor_single_scalar(den2[:, 14:16], bass.AP(bk[4], 64, [[512, 128], [65, 2]]),
                                               1e-6, ALU.add)
                    v.drain()
                    v.reciprocal(den2[:, 0:H], den2[:, 0:H])
                    v.drain()
                    v.engine_nop().then_inc(*dnp.pinc(c))

            @blk.scalar
            def _(s):
                for c in range(CPG):
                    dnp.cwait(s, c)
                    for h in range(H):
                        bankn, off = num_slot(h)
                        s.activation(ysb[:, c * E + h * 64:c * E + (h + 1) * 64],
                                     bankn[:, off:off + 64], AF.Copy, scale=den2[:, h:h + 1])
                    s.engine_nop().then_inc(*dnp.cinc(c))
                    s.engine_nop().then_inc(*nmp.cinc(c))
                    sup.cwait(s, c)
                    s.activation(Ssb[:, 0:min(HT, 7) * 65], bk[5][:, 0:min(HT, 7) * 65], AF.Copy)
                    if HT > 7:
                        s.activation(Ssb[:, 455:455 + (HT - 7) * 65], bk[6][:, 0:(HT - 7) * 65], AF.Copy)
                    s.engine_nop().then_inc(*sup.cinc(c))

        with nc.Block() as blk:
            tp = Pipe(b, "g_tp", 2)
            pj = Pipe(b, "g_pj", 2)
            rs = Pipe(b, "g_rs", 2, cq=16, dma_c=True)
            wp2 = Pipe(b, "g_w2", 2, pq=16, dma_p=True)

            @blk.sync
            def _(s):
                for eo in range(ET):
                    s.wait_ge(pj.cs[0].h, pj.cb[0] + max(0, eo - 1))
                    s.dma_start(out=bass.AP(wstr, (eo % 2) * ET * 128,
                                            [[2 * ET * 128, 128], [128, ET], [1, 128]]),
                                in_=bass.AP(wp_bf, eo * 128,
                                            [[E, 128], [128 * E, ET], [1, 128]])).then_inc(*wp2.pinc_part(eo))

            @blk.tensor
            def _(te):
                for i in range(TPG * ET):
                    tt, e = divmod(i, ET)
                    tp.pwait(te, i)
                    te.transpose(bk[i % 2][:, 0:128],
                                 ysb[:, tt * E + e * 128:tt * E + (e + 1) * 128],
                                 ident_f[:, :]).then_inc(*tp.pinc(i))
                te.wait_ge(tp.cs[0].h, tp.cb[0] + TPG * ET)
                for eo in range(ET):
                    wp2.cwait(te, eo)
                    pj.pwait(te, eo)
                    buf = wstr[:, (eo % 2) * ET * 128:(eo % 2) * ET * 128 + ET * 128]
                    for e in range(ET):
                        mm = te.matmul(bk[eo % 2][:, 0:G], buf[:, e * 128:(e + 1) * 128],
                                       yT[:, e * G:(e + 1) * G],
                                       start=(e == 0), stop=(e == ET - 1))
                    mm.then_inc(*pj.pinc(eo))

            @blk.vector
            def _(v):
                for i in range(TPG * ET):
                    tt, e = divmod(i, ET)
                    tp.cwait(v, i)
                    v.tensor_copy(yT[:, e * G + tt * 128:e * G + (tt + 1) * 128],
                                  bk[i % 2][:, 0:128]).then_inc(*tp.cinc(i))
                for eo in range(ET):
                    pj.cwait(v, eo)
                    rs.pwait(v, eo)
                    v.tensor_tensor(resg[:, (eo % 2) * G:(eo % 2) * G + G], bk[eo % 2][:, 0:G],
                                    cur[:, eo * G:(eo + 1) * G], ALU.add)
                    v.engine_nop().then_inc(*pj.cinc(eo))
                    v.engine_nop().then_inc(*rs.pinc(eo))

            @blk.gpsimd
            def _(gp):
                for eo in range(ET):
                    rs.cwait(gp, eo)
                    gp.dma_start(out=dst[eo * 128:(eo + 1) * 128, g * G:(g + 1) * G],
                                 in_=resg[:, (eo % 2) * G:(eo % 2) * G + G]).then_inc(*rs.cinc(eo))

    with nc.Block() as blk:
        @blk.sync
        def _(s):
            for sm in b.sem("g_rs_c0"), b.sem("g_rs_c1"):
                s.wait_ge(*sm.tail())
    sc.close()


def emit_allreduce_max(b, maxv, nv, grow, bank, ident_f, cc_in, cc_out, replica, tag):
    nc = b.nc
    red = b.sem("g_r")
    dm = b.sem("ar_dm")
    with nc.Block() as blk:
        @blk.tensor
        def _(te):
            te.matmul(bank[0:nv, 0:128], maxv[:, 0:nv], ident_f[:, :], start=True,
                      stop=True).then_inc(*red.inc())
        k_tr = red.v

        @blk.vector
        def _(v):
            v.wait_ge(red.h, k_tr)
            v.tensor_reduce(maxv[0:nv, nv:nv + 1], bank[0:nv, 0:128], AX.X, ALU.max)
            v.engine_nop().then_inc(*red.inc())
        k_red = red.v

        @blk.gpsimd
        def _(g):
            ccs = b.sem("ar_cc")
            g.wait_ge(red.h, k_red)
            g.dma_start(out=cc_in[0:1, 0:nv], in_=maxv[0:nv, nv:nv + 1]).then_inc(*dm.inc(16))
            g.wait_ge(*dm.tail())
            g.collective_compute("AllReduce", ALU.max, replica_groups=replica,
                                 ins=[cc_in[0:1, 0:nv]], outs=[cc_out[0:1, 0:nv]]).then_inc(*ccs.inc())
            g.wait_ge(*ccs.tail())
            g.dma_start(out=grow[0:1, 0:nv], in_=cc_out[0:1, 0:nv]).then_inc(*dm.inc(16))
            g.wait_ge(*dm.tail())


def emit_scales(b, st, grow, nv, scal, c0, bank, tag):
    nc = b.nc
    sv = b.sem("g_r")
    with nc.Block() as blk:
        @blk.vector
        def _(v):
            v.tensor_scalar(grow[0:1, 0:nv], grow[0:1, 0:nv], 1.0 / 32767.0, 1e-12,
                            ALU.mult, ALU.max)
            v.drain()
            v.reciprocal(grow[0:1, nv:2 * nv], grow[0:1, 0:nv])
            v.engine_nop().then_inc(*sv.inc())
        k_v = sv.v

        @blk.tensor
        def _(te):
            te.wait_ge(sv.h, k_v)
            te.matmul(bank[:, 0:2 * nv], st["ones_row_f"][0:1, :], grow[0:1, 0:2 * nv],
                      start=True, stop=True).then_inc(*sv.inc())
        k_t = sv.v

        @blk.vector
        def _(v):
            v.wait_ge(sv.h, k_t)
            for k in range(nv):
                v.tensor_copy(scal[:, c0 + 2 * k:c0 + 2 * k + 1], bank[:, k:k + 1])
                v.tensor_copy(scal[:, c0 + 2 * k + 1:c0 + 2 * k + 2], bank[:, nv + k:nv + k + 1])
            v.drain()


def emit_ffn(b, st, src, dst, wu_bf, wg_bf, wd_bf, sp_x2n, sp_up, sp_gate, sp_t,
             sp_opre, scal, cc_in, cc_out, ident_f, replica, tag):
    cfg = b.cfg
    nc = b.nc
    E, T, G, I = cfg["E"], cfg["T"], cfg["G"], cfg["I"]
    ET, IT, NG, TPG = E // 128, I // 128, T // G, G // 128
    CH = min(512, T)
    NCH = T // CH

    N = IT * NCH
    ITG = IT * G

    sc = Scope(b)
    xg = [sc.sbuf(f"{tag}_xg{i}", [128, ET * G], F32) for i in range(2)]
    maxb = sc.sbuf(f"{tag}_maxb", [128, 2 * IT * NG], F32)
    maxv = sc.sbuf(f"{tag}_maxv", [128, 8], F32)
    grow = sc.sbuf(f"{tag}_grow", [1, 8], F32)
    resg = sc.sbuf(f"{tag}_resg", [128, 2 * G], F32)
    bk = sc.banks(f"{tag}_bk", 8)

    sc1 = Scope(b)
    lnb = sc1.sbuf(f"{tag}_lnb", [128, 4 * G], BF16)
    rb_sb = sc1.sbuf(f"{tag}_rb", [128, G], F32)
    x2ng = sc1.sbuf(f"{tag}_x2n", [128, ET * G], F32)
    iq = sc1.sbuf(f"{tag}_iq", [128, ET * G], I16)
    xq = sc1.sbuf(f"{tag}_xq", [128, ET * T], BF16)
    wstr = sc1.sbuf(f"{tag}_wstr", [128, 2 * ET * 128], BF16)
    spst = sc1.sbuf(f"{tag}_spst", [128, 4 * 512], BF16)

    ld_x = [b.sem("g_ldxA"), b.sem("g_ldxB")]
    st_sp = [b.sem("g_stspA"), b.sem("g_stspB")]
    s_st = b.sem("g_stres")

    for g in range(NG):
        with nc.Block() as blk:
            @blk.sync
            def _(s):
                if g == 0:
                    s.dma_start(out=bass.AP(xg[0], 0, [[ET * G, 128], [G, ET], [1, G]]),
                                in_=bass.AP(src, 0, [[T, 128], [128 * T, ET], [1, G]])).then_inc(*ld_x[0].inc(16))
                if g + 1 < NG:
                    s.dma_start(out=bass.AP(xg[(g + 1) % 2], 0, [[ET * G, 128], [G, ET], [1, G]]),
                                in_=bass.AP(src, (g + 1) * G, [[T, 128], [128 * T, ET], [1, G]])).then_inc(*ld_x[(g + 1) % 2].inc(16))
                s.wait_ge(*ld_x[g % 2].tail())
        cur = xg[g % 2]
        x2c = x2ng[:, 0:ET * G]
        spill_guard = (st_sp[0].v, st_sp[1].v)
        with nc.Block() as blk:
            @blk.vector
            def _(v):
                v.wait_ge(st_sp[0].h, spill_guard[0])
                v.wait_ge(st_sp[1].h, spill_guard[1])
            emit_ln_group(b, blk, st, bk, cur, lnb[:, 0:2 * G], lnb[:, 2 * G:4 * G],
                          None, rb_sb, None, False, x2n_out=x2c)
            s_r = b.sem("g_r")

            @blk.vector
            def _(v):
                for e in range(ET):
                    v.tensor_reduce(maxv[:, 0:1] if False else maxb[:, g * ET + e:g * ET + e + 1],
                                    x2c[:, e * G:(e + 1) * G], AX.X, ALU.max,
                                    apply_absolute_value=True)
                v.engine_nop().then_inc(*s_r.inc())
            k_r = s_r.v

            @blk.gpsimd
            def _(gp):
                gp.wait_ge(s_r.h, k_r)
                gp.dma_start(out=bass.AP(sp_x2n, g * G, [[T, 128], [128 * T, ET], [1, G]]),
                             in_=bass.AP(x2ng, 0, [[ET * G, 128], [G, ET], [1, G]])).then_inc(*st_sp[g % 2].inc(16))
    with nc.Block() as blk:
        @blk.vector
        def _(v):
            v.tensor_reduce(maxv[:, 0:1], maxb[:, 0:NG * ET], AX.X, ALU.max)
            v.drain()
    emit_allreduce_max(b, maxv, 1, grow, bk[5], ident_f, cc_in, cc_out, replica, tag)
    emit_scales(b, st, grow, 1, scal, 0, bk[6], tag)

    with nc.Block() as blk:
        lp = Pipe(b, "g_lsp", 1, pq=16, dma_p=True)
        qp = Pipe(b, "g_qp", 1)

        @blk.sync
        def _(s):
            s.wait_ge(*st_sp[0].tail())
            s.wait_ge(*st_sp[1].tail())
            for g in range(NG):
                lp.pwait(s, g)
                s.dma_start(out=bass.AP(x2ng, 0, [[ET * G, 128], [G, ET], [1, G]]),
                            in_=bass.AP(sp_x2n, g * G, [[T, 128], [128 * T, ET], [1, G]])).then_inc(*lp.pinc(g))

        @blk.vector
        def _(v):
            for g in range(NG):
                lp.cwait(v, g)
                qp.pwait(v, g)
                v.tensor_scalar(x2ng[:, :], x2ng[:, :], scal[:, 1:2], MAGIC, ALU.mult, ALU.add)
                v.drain()
                v.tensor_single_scalar(iq[:, :], x2ng[:, :], MAGIC, ALU.subtract)
                v.engine_nop().then_inc(*qp.pinc(g))
                v.engine_nop().then_inc(*lp.cinc(g))

        @blk.scalar
        def _(s):
            for g in range(NG):
                qp.cwait(s, g)
                for e in range(ET):
                    s.activation(xq[:, e * T + g * G:e * T + (g + 1) * G],
                                 iq[:, e * G:(e + 1) * G], AF.Copy, scale=scal[:, 0:1])
                s.engine_nop().then_inc(*qp.cinc(g))

    with nc.Block() as blk:
        wp = Pipe(b, "g_w", 2, pq=16, dma_p=True)
        mp = Pipe(b, "g_mp", NG)
        ep = Pipe(b, "g_ep", 4, cq=16, dma_c=True)

        @blk.sync
        def _(s):
            for ft in range(2 * IT):
                s.wait_ge(mp.cs[0].h, mp.cb[0] + max(0, ft - 1) * NG)
                wsrc = wu_bf if ft < IT else wg_bf
                f = ft % IT
                s.dma_start(out=bass.AP(wstr, (ft % 2) * ET * 128,
                                        [[2 * ET * 128, 128], [128, ET], [1, 128]]),
                            in_=bass.AP(wsrc, f * 128,
                                        [[I, 128], [128 * I, ET], [1, 128]])).then_inc(*wp.pinc_part(ft))

        @blk.tensor
        def _(te):
            for ft in range(2 * IT):
                wp.cwait(te, ft)
                buf = wstr[:, (ft % 2) * ET * 128:(ft % 2) * ET * 128 + ET * 128]
                for e in range(ET):
                    for g in range(NG):
                        i = ft * NG + g
                        if e == 0:
                            mp.pwait(te, i)
                        mm = te.matmul(bk[g][:, 0:G],
                                       buf[:, e * 128:(e + 1) * 128],
                                       xq[:, e * T + g * G:e * T + (g + 1) * G],
                                       start=(e == 0), stop=(e == ET - 1))
                        if e == ET - 1:
                            mm.then_inc(*mp.pinc(i))

        @blk.scalar
        def _(s):
            srd = b.sem("g_rd")
            for i in range(2 * IT * NG):
                mp.cwait(s, i)
                ep.pwait(s, i)
                s.activation(spst[:, (i % 4) * 512:(i % 4) * 512 + G], bk[i % NG][:, 0:G],
                             AF.Copy)
                s.engine_nop().then_inc(*mp.cinc(i))
                s.engine_nop().then_inc(*srd.inc())

        @blk.vector
        def _(v):
            srd = b.sem("g_rd")
            base = srd.v - 2 * IT * NG
            for i in range(2 * IT * NG):
                v.wait_ge(srd.h, base + i + 1)
                v.tensor_reduce(maxb[:, i:i + 1], spst[:, (i % 4) * 512:(i % 4) * 512 + G],
                                AX.X, ALU.max, apply_absolute_value=True)
                v.engine_nop().then_inc(*ep.pinc(i))

        @blk.gpsimd
        def _(gp):
            for i in range(2 * IT * NG):
                ft, g = divmod(i, NG)
                spdst = sp_up if ft < IT else sp_gate
                f = ft % IT
                ep.cwait(gp, i)
                gp.dma_start(out=spdst[f * 128:(f + 1) * 128, g * G:(g + 1) * G],
                             in_=spst[:, (i % 4) * 512:(i % 4) * 512 + G]).then_inc(*ep.cinc(i))

    with nc.Block() as blk:
        @blk.vector
        def _(v):
            v.tensor_reduce(maxv[:, 0:1], maxb[:, 0:IT * NG], AX.X, ALU.max)
            v.tensor_reduce(maxv[:, 1:2], maxb[:, IT * NG:2 * IT * NG], AX.X, ALU.max)
            v.drain()
    emit_allreduce_max(b, maxv, 2, grow, bk[5], ident_f, cc_in, cc_out, replica, tag)
    emit_scales(b, st, grow, 2, scal, 2, bk[6], tag)
    with nc.Block() as blk:
        @blk.vector
        def _(v):
            v.tensor_single_scalar(scal[:, 10:11], scal[:, 4:5], 1.0 / 65536.0, ALU.mult)
            v.drain()

    sc1.close()
    sc2 = Scope(b)
    wstrD = sc2.sbuf(f"{tag}_wd", [128, ET * IT * 128], BF16)
    oT2 = sc2.sbuf(f"{tag}_oT", [128, 2 * ITG], BF16)
    fb1 = sc2.sbuf(f"{tag}_fb1", [128, 2 * CH], F32)
    fb2 = sc2.sbuf(f"{tag}_fb2", [128, 2 * CH], F32)
    fc1 = sc2.sbuf(f"{tag}_fc1", [128, 2 * CH], F32)
    prb = sc2.sbuf(f"{tag}_pr", [128, 2 * CH], F32)
    ib1 = sc2.sbuf(f"{tag}_ib1", [128, 4 * CH], I16)
    ib2 = sc2.sbuf(f"{tag}_ib2", [128, 2 * CH], I16)
    ub1 = sc2.sbuf(f"{tag}_ub1", [128, 2 * CH], U16)
    bb1 = sc2.sbuf(f"{tag}_bb1", [128, 3 * CH], BF16)
    gld = sc2.sbuf(f"{tag}_gld", [128, 3 * CH], BF16)
    uld = sc2.sbuf(f"{tag}_uld", [128, 3 * CH], BF16)
    magic = st["magic_col"]

    with nc.Block() as blk:
        cs = [b.sem(f"g_c{k}") for k in range(8)]
        cb = [x.v for x in cs]
        ld = [b.sem(n) for n in ("g_ldxA", "g_ldxB", "g_ldx")]
        ldb = [x.v for x in ld]
        sw = [b.sem(n) for n in ("g_ep_c0", "g_ep_c1", "g_ep_c2")]
        swb = [x.v for x in sw]

        @blk.sync
        def _(s):
            for k in range(4):
                s.wait_ge(*b.sem(f"g_ep_c{k}").tail())
            for i in range(N):
                it, ch = divmod(i, NCH)
                if i >= 3:
                    s.wait_ge(cs[0].h, cb[0] + i - 2)
                s.dma_start(out=gld[:, (i % 3) * CH:(i % 3) * CH + CH],
                            in_=sp_gate[it * 128:(it + 1) * 128, ch * CH:(ch + 1) * CH]).then_inc(*ld[i % 3].inc(16))

        @blk.scalar
        def _(s):
            def S1(i):
                s.wait_ge(ld[i % 3].h, ldb[i % 3] + (i // 3 + 1) * 16)
                if i >= 2:
                    s.wait_ge(cs[1].h, cb[1] + i - 1)
                s.activation(fb1[:, (i % 2) * CH:(i % 2) * CH + CH],
                             gld[:, (i % 3) * CH:(i % 3) * CH + CH],
                             AF.Identity, bias=magic[:, 0:1], scale=scal[:, 5:6]).then_inc(*cs[0].inc())
            def S3(j):
                s.wait_ge(cs[1].h, cb[1] + j + 1)
                if j >= 2:
                    s.wait_ge(cs[3].h, cb[3] + j - 1)
                s.activation(fb2[:, (j % 2) * CH:(j % 2) * CH + CH],
                             ib1[:, (j % 4) * CH:(j % 4) * CH + CH],
                             AF.Sigmoid, scale=scal[:, 4:5]).then_inc(*cs[2].inc())
            for i in range(N + 1):
                if i < N:
                    S1(i)
                if 0 <= i - 1 < N:
                    S3(i - 1)

        @blk.vector
        def _(v):
            def st2(i):
                v.wait_ge(cs[0].h, cb[0] + i + 1)
                if i >= 4:
                    v.wait_ge(cs[5].h, cb[5] + i - 3)
                v.tensor_single_scalar(ib1[:, (i % 4) * CH:(i % 4) * CH + CH],
                                       fb1[:, (i % 2) * CH:(i % 2) * CH + CH],
                                       MAGIC, ALU.subtract).then_inc(*cs[1].inc())
            def st4(j):
                v.wait_ge(cs[2].h, cb[2] + j + 1)
                if j >= 2:
                    v.wait_ge(cs[4].h, cb[4] + j - 1)
                v.tensor_scalar(fc1[:, (j % 2) * CH:(j % 2) * CH + CH],
                                fb2[:, (j % 2) * CH:(j % 2) * CH + CH],
                                65536.0, MAGIC, ALU.mult, ALU.add).then_inc(*cs[3].inc())
            def st5(k):
                v.wait_ge(cs[3].h, cb[3] + k + 1)
                if k >= 2:
                    v.wait_ge(cs[5].h, cb[5] + k - 1)
                v.tensor_scalar(ub1[:, (k % 2) * CH:(k % 2) * CH + CH],
                                fc1[:, (k % 2) * CH:(k % 2) * CH + CH],
                                MAGIC, 65535.0, ALU.subtract, ALU.min).then_inc(*cs[4].inc())
            def st6(l):
                v.wait_ge(cs[4].h, cb[4] + l + 1)
                if l >= 2:
                    v.wait_ge(cs[6].h, cb[6] + l - 1)
                v.tensor_tensor(prb[:, (l % 2) * CH:(l % 2) * CH + CH],
                                ib1[:, (l % 4) * CH:(l % 4) * CH + CH],
                                ub1[:, (l % 2) * CH:(l % 2) * CH + CH], ALU.mult).then_inc(*cs[5].inc())
            def st7(m):
                v.wait_ge(cs[5].h, cb[5] + m + 1)
                if m >= 3:
                    v.wait_ge(cs[7].h, cb[7] + m - 2)
                    v.wait_ge(sw[m % 3].h, swb[m % 3] + (m // 3) * 16)
                v.tensor_scalar(bb1[:, (m % 3) * CH:(m % 3) * CH + CH],
                                prb[:, (m % 2) * CH:(m % 2) * CH + CH],
                                scal[:, 10:11], None, ALU.mult).then_inc(*cs[6].inc())
            def st8(n):
                v.wait_ge(cs[6].h, cb[6] + n + 1)
                v.tensor_reduce(maxb[:, n:n + 1], bb1[:, (n % 3) * CH:(n % 3) * CH + CH],
                                AX.X, ALU.max, apply_absolute_value=True).then_inc(*cs[7].inc())
            for i in range(N + 5):
                if i < N:
                    st2(i)
                if 0 <= i - 1 < N:
                    st4(i - 1)
                if 0 <= i - 2 < N:
                    st5(i - 2)
                if 0 <= i - 3 < N:
                    st6(i - 3)
                if 0 <= i - 4 < N:
                    st7(i - 4)
                if 0 <= i - 5 < N:
                    st8(i - 5)

        @blk.gpsimd
        def _(gp):
            for i in range(N):
                it, ch = divmod(i, NCH)
                gp.wait_ge(cs[6].h, cb[6] + i + 1)
                gp.dma_start(out=sp_t[it * 128:(it + 1) * 128, ch * CH:(ch + 1) * CH],
                             in_=bb1[:, (i % 3) * CH:(i % 3) * CH + CH]).then_inc(*sw[i % 3].inc(16))

    with nc.Block() as blk:
        @blk.vector
        def _(v):
            v.tensor_reduce(maxv[:, 0:1], maxb[:, 0:IT * NCH], AX.X, ALU.max)
            v.drain()
    emit_allreduce_max(b, maxv, 1, grow, bk[5], ident_f, cc_in, cc_out, replica, tag)
    emit_scales(b, st, grow, 1, scal, 6, bk[6], tag)
    with nc.Block() as blk:
        @blk.vector
        def _(v):
            v.tensor_tensor(scal[:, 11:12], scal[:, 6:7], scal[:, 2:3], ALU.mult)
            v.drain()

    with nc.Block() as blk:
        cs = [b.sem(f"g_c{k}") for k in range(8)]
        cb = [x.v for x in cs]
        ld = [b.sem(n) for n in ("g_ldxA", "g_ldxB", "g_ldx")]
        ldb = [x.v for x in ld]
        lu = [b.sem(n) for n in ("g_w_p0", "g_w_p1", "g_w2_p0")]
        lub = [x.v for x in lu]
        sw = [b.sem(n) for n in ("g_ep_c0", "g_ep_c1", "g_ep_c2")]
        swb = [x.v for x in sw]

        @blk.sync
        def _(s):
            for k in range(3):
                s.wait_ge(sw[k].h, swb[k])
            for i in range(N):
                it, ch = divmod(i, NCH)
                if i >= 3:
                    s.wait_ge(cs[0].h, cb[0] + i - 2)
                s.dma_start(out=gld[:, (i % 3) * CH:(i % 3) * CH + CH],
                            in_=sp_t[it * 128:(it + 1) * 128, ch * CH:(ch + 1) * CH]).then_inc(*ld[i % 3].inc(16))
                if i >= 3:
                    s.wait_ge(cs[2].h, cb[2] + i - 2)
                s.dma_start(out=uld[:, (i % 3) * CH:(i % 3) * CH + CH],
                            in_=sp_up[it * 128:(it + 1) * 128, ch * CH:(ch + 1) * CH]).then_inc(*lu[i % 3].inc(16))

        @blk.scalar
        def _(s):
            for i in range(N):
                s.wait_ge(ld[i % 3].h, ldb[i % 3] + (i // 3 + 1) * 16)
                if i >= 2:
                    s.wait_ge(cs[1].h, cb[1] + i - 1)
                s.activation(fb1[:, (i % 2) * CH:(i % 2) * CH + CH],
                             gld[:, (i % 3) * CH:(i % 3) * CH + CH],
                             AF.Identity, bias=magic[:, 0:1], scale=scal[:, 7:8]).then_inc(*cs[0].inc())
                s.wait_ge(lu[i % 3].h, lub[i % 3] + (i // 3 + 1) * 16)
                if i >= 2:
                    s.wait_ge(cs[3].h, cb[3] + i - 1)
                s.activation(fb2[:, (i % 2) * CH:(i % 2) * CH + CH],
                             uld[:, (i % 3) * CH:(i % 3) * CH + CH],
                             AF.Identity, bias=magic[:, 0:1], scale=scal[:, 3:4]).then_inc(*cs[2].inc())

        @blk.vector
        def _(v):
            def st2(i):
                v.wait_ge(cs[0].h, cb[0] + i + 1)
                if i >= 2:
                    v.wait_ge(cs[4].h, cb[4] + i - 1)
                v.tensor_single_scalar(ib1[:, (i % 2) * CH:(i % 2) * CH + CH],
                                       fb1[:, (i % 2) * CH:(i % 2) * CH + CH],
                                       MAGIC, ALU.subtract).then_inc(*cs[1].inc())
            def st4(i):
                v.wait_ge(cs[2].h, cb[2] + i + 1)
                if i >= 2:
                    v.wait_ge(cs[4].h, cb[4] + i - 1)
                v.tensor_single_scalar(ib2[:, (i % 2) * CH:(i % 2) * CH + CH],
                                       fb2[:, (i % 2) * CH:(i % 2) * CH + CH],
                                       MAGIC, ALU.subtract).then_inc(*cs[3].inc())
            def st6(k):
                v.wait_ge(cs[4].h, cb[4] + k + 1)
                if k >= 3:
                    v.wait_ge(cs[6].h, cb[6] + k - 2)
                    v.wait_ge(sw[k % 3].h, swb[k % 3] + (k // 3) * 16)
                v.tensor_scalar(bb1[:, (k % 3) * CH:(k % 3) * CH + CH],
                                prb[:, (k % 2) * CH:(k % 2) * CH + CH],
                                scal[:, 11:12], None, ALU.mult).then_inc(*cs[5].inc())
            def st7(l):
                v.wait_ge(cs[5].h, cb[5] + l + 1)
                v.tensor_reduce(maxb[:, l:l + 1], bb1[:, (l % 3) * CH:(l % 3) * CH + CH],
                                AX.X, ALU.max, apply_absolute_value=True).then_inc(*cs[6].inc())
            def st5(j):
                v.wait_ge(cs[1].h, cb[1] + j + 1)
                v.wait_ge(cs[3].h, cb[3] + j + 1)
                if j >= 2:
                    v.wait_ge(cs[5].h, cb[5] + j - 1)
                v.tensor_tensor(prb[:, (j % 2) * CH:(j % 2) * CH + CH],
                                ib1[:, (j % 2) * CH:(j % 2) * CH + CH],
                                ib2[:, (j % 2) * CH:(j % 2) * CH + CH], ALU.mult).then_inc(*cs[4].inc())
            for i in range(N + 3):
                if i < N:
                    st2(i)
                    st4(i)
                if 0 <= i - 1 < N:
                    st5(i - 1)
                if 0 <= i - 2 < N:
                    st6(i - 2)
                if 0 <= i - 3 < N:
                    st7(i - 3)

        @blk.gpsimd
        def _(gp):
            for i in range(N):
                it, ch = divmod(i, NCH)
                gp.wait_ge(cs[5].h, cb[5] + i + 1)
                gp.dma_start(out=sp_opre[it * 128:(it + 1) * 128, ch * CH:(ch + 1) * CH],
                             in_=bb1[:, (i % 3) * CH:(i % 3) * CH + CH]).then_inc(*sw[i % 3].inc(16))

    with nc.Block() as blk:
        @blk.vector
        def _(v):
            v.tensor_reduce(maxv[:, 0:1], maxb[:, 0:IT * NCH], AX.X, ALU.max)
            v.drain()
    emit_allreduce_max(b, maxv, 1, grow, bk[5], ident_f, cc_in, cc_out, replica, tag)
    emit_scales(b, st, grow, 1, scal, 8, bk[6], tag)

    NB = NG * ET
    NN = NG * IT
    with nc.Block() as blk:
        cs = [b.sem(f"g_c{k}") for k in range(8)]
        cb = [x.v for x in cs]
        ld = [b.sem(n) for n in ("g_ldxA", "g_ldxB", "g_ldx")]
        ldb = [x.v for x in ld]
        sw = [b.sem(n) for n in ("g_ep_c0", "g_ep_c1", "g_ep_c2")]
        swb = [x.v for x in sw]
        rl = [b.sem(n) for n in ("g_wv_p0", "g_wv_p1")]
        rlb = [x.v for x in rl]
        wdl = [b.sem("g_w2_p1")]
        wdlb = [x.v for x in wdl]
        do = [b.sem(n) for n in ("g_rs_c0", "g_rs_c1")]
        dob = [x.v for x in do]
        sev = b.sem("g_ev")
        sevb = sev.v
        smm = b.sem("g_mm")
        smmb = smm.v

        @blk.sync
        def _(s):
            for k in range(3):
                s.wait_ge(sw[k].h, swb[k])
            s.dma_start(out=bass.AP(wstrD, 0,
                                    [[ET * IT * 128, 128], [IT * 128, ET], [128, IT], [1, 128]]),
                        in_=bass.AP(wd_bf, 0,
                                    [[E, 128], [128, ET], [128 * E, IT], [1, 128]])).then_inc(*wdl[0].inc(16))
            for j in range(NN):
                g, it = divmod(j, IT)
                if it == 0:
                    if g >= 2:
                        s.wait_ge(sev.h, sevb + (g - 1) * ET)
                    s.dma_start(out=bass.AP(xg[g % 2], 0, [[ET * G, 128], [G, ET], [1, G]]),
                                in_=bass.AP(src, g * G, [[T, 128], [128 * T, ET], [1, G]])).then_inc(*rl[g % 2].inc(16))
                if j >= 3:
                    s.wait_ge(cs[0].h, cb[0] + j - 2)
                s.dma_start(out=gld[:, (j % 3) * CH:(j % 3) * CH + G],
                            in_=sp_opre[it * 128:(it + 1) * 128, g * G:(g + 1) * G]).then_inc(*ld[j % 3].inc(16))

        @blk.scalar
        def _(s):
            def S3(j):
                g, it = divmod(j, IT)
                s.wait_ge(cs[1].h, cb[1] + j + 1)
                if it == 0 and g >= 2:
                    s.wait_ge(sev.h, sevb + (g - 1) * ET)
                s.activation(oT2[:, (g % 2) * ITG + it * G:(g % 2) * ITG + it * G + G],
                             ib1[:, (j % 2) * CH:(j % 2) * CH + G],
                             AF.Copy, scale=scal[:, 8:9]).then_inc(*cs[2].inc())
            for j in range(NN):
                S3(j)

        @blk.vector
        def _(v):
            def st1(j):
                v.wait_ge(ld[j % 3].h, ldb[j % 3] + (j // 3 + 1) * 16)
                if j >= 2:
                    v.wait_ge(cs[1].h, cb[1] + j - 1)
                v.tensor_scalar(fb1[:, (j % 2) * CH:(j % 2) * CH + G],
                                gld[:, (j % 3) * CH:(j % 3) * CH + G],
                                scal[:, 9:10], MAGIC, ALU.mult, ALU.add).then_inc(*cs[0].inc())
            def st2(j):
                v.wait_ge(cs[0].h, cb[0] + j + 1)
                if j >= 2:
                    v.wait_ge(cs[2].h, cb[2] + j - 1)
                v.tensor_single_scalar(ib1[:, (j % 2) * CH:(j % 2) * CH + G],
                                       fb1[:, (j % 2) * CH:(j % 2) * CH + G],
                                       MAGIC, ALU.subtract).then_inc(*cs[1].inc())
            def evac(idx):
                g, eo = divmod(idx, ET)
                v.wait_ge(smm.h, smmb + idx + 1)
                v.wait_ge(rl[g % 2].h, rlb[g % 2] + (g // 2 + 1) * 16)
                if idx >= 2:
                    v.wait_ge(do[idx % 2].h, dob[idx % 2] + (idx // 2) * 16)
                v.tensor_tensor(resg[:, (idx % 2) * G:(idx % 2) * G + G], bk[idx % 4][:, 0:G],
                                xg[g % 2][:, eo * G:(eo + 1) * G], ALU.add).then_inc(*sev.inc())
            nev = 0
            for j in range(NN + 1):
                if j < NN:
                    st1(j)
                if 0 <= j - 1 < NN:
                    st2(j - 1)
                if j >= IT and (j - IT) % 3 == 2 and nev < NB:
                    evac(nev)
                    nev += 1
            while nev < NB:
                evac(nev)
                nev += 1

        @blk.tensor
        def _(te):
            for g in range(NG):
                for eo in range(ET):
                    idx = g * ET + eo
                    if idx == 0:
                        te.wait_ge(wdl[0].h, wdlb[0] + 16)
                    if idx >= 4:
                        te.wait_ge(sev.h, sevb + idx - 3)
                    for it in range(IT):
                        if eo == 0:
                            te.wait_ge(cs[2].h, cb[2] + g * IT + it + 1)
                        mm = te.matmul(bk[idx % 4][:, 0:G],
                                       wstrD[:, (eo * IT + it) * 128:(eo * IT + it + 1) * 128],
                                       oT2[:, (g % 2) * ITG + it * G:(g % 2) * ITG + it * G + G],
                                       start=(it == 0), stop=(it == IT - 1))
                    mm.then_inc(*smm.inc())

        @blk.gpsimd
        def _(gp):
            for idx in range(NB):
                g, eo = divmod(idx, ET)
                gp.wait_ge(sev.h, sevb + idx + 1)
                gp.dma_start(out=dst[eo * 128:(eo + 1) * 128, g * G:(g + 1) * G],
                             in_=resg[:, (idx % 2) * G:(idx % 2) * G + G]).then_inc(*do[idx % 2].inc(16))

    with nc.Block() as blk:
        @blk.sync
        def _(s):
            for nm in ("g_rs_c0", "g_rs_c1"):
                s.wait_ge(*b.sem(nm).tail())
    sc2.close()
    sc.close()


def emit_attn_b(b, st, src, dst, waiv_bf, wmix_bf, db_sb, bmix_sb, sp_csw, sp_cd,
                ident_f, onesf):
    cfg = b.cfg
    nc = b.nc
    E, T, G, SCH = cfg["E"], cfg["T"], cfg["G"], cfg["SCH"]
    I3 = 3 * E
    ET, IT3, NG, TPG = E // 128, I3 // 128, T // G, G // 128
    SPG = G // SCH
    NSC = T // SCH

    sc = Scope(b)
    xg = [sc.sbuf(f"b_xg{i}", [128, ET * G], F32) for i in range(2)]
    lnb = sc.sbuf("b_lnb", [128, 4 * G], BF16)
    xnc = sc.sbuf("b_xnc", [128, ET * G], BF16)
    rb_sb = sc.sbuf("b_rb", [128, G], F32)
    wstr = sc.sbuf("b_wstr", [128, 2 * ET * 128], BF16)
    t_a = sc.sbuf("b_ta", [128, ET * G], F32)
    t_i = sc.sbuf("b_ti", [128, ET * G], F32)
    t_v = sc.sbuf("b_tv", [128, ET * G], F32)
    g1 = sc.sbuf("b_g1", [128, ET * G], F32)
    ctd = sc.sbuf("b_ctd", [128, ET * NSC], F32)
    cfs = sc.sbuf("b_cfs", [128, ET * NSC], F32)
    ccd = sc.sbuf("b_ccd", [128, ET * NSC], F32)
    wr2 = sc.sbuf("b_wr2", [128, ET * NSC], F32)
    cwc = sc.sbuf("b_cwc", [128, ET * NSC], F32)
    inc = sc.sbuf("b_inc", [128, ET * NSC], F32)
    hb2 = sc.sbuf("b_h", [128, 2 * ET * G], BF16)
    wmr = sc.sbuf("b_wmr", [128, ET * ET * 128], BF16)
    resg = sc.sbuf("b_resg", [128, 2 * G], F32)
    rstmask = sc.sbuf("b_rst", [128, G], F32)
    bk = sc.banks("b_bk", 6)

    with nc.Block() as blk:
        @blk.vector
        def _(v):
            v.memset(rstmask[:, :], 1.0)
            v.drain()
            v.memset(bass.AP(rstmask, 0, [[G, 128], [SCH, SPG]]), 0.0)

    ld_x = [b.sem("g_ldxA"), b.sem("g_ldxB")]
    st_sp = b.sem("g_stspA")
    s_st = b.sem("g_stres")

    for g in range(NG):
        with nc.Block() as blk:
            @blk.sync
            def _(s):
                if g == 0:
                    s.dma_start(out=bass.AP(xg[0], 0, [[ET * G, 128], [G, ET], [1, G]]),
                                in_=bass.AP(src, 0, [[T, 128], [128 * T, ET], [1, G]])).then_inc(*ld_x[0].inc(16))
                if g + 1 < NG:
                    s.dma_start(out=bass.AP(xg[(g + 1) % 2], 0, [[ET * G, 128], [G, ET], [1, G]]),
                                in_=bass.AP(src, (g + 1) * G, [[T, 128], [128 * T, ET], [1, G]])).then_inc(*ld_x[(g + 1) % 2].inc(16))
                s.wait_ge(*ld_x[g % 2].tail())
        cur = xg[g % 2]

        with nc.Block() as blk:
            emit_ln_group(b, blk, st, bk, cur, lnb[:, 0:2 * G], lnb[:, 2 * G:4 * G],
                          xnc, rb_sb, None, False)

        with nc.Block() as blk:
            wp = Pipe(b, "g_w", 2, pq=16, dma_p=True)
            mp = Pipe(b, "g_mp", 4)
            sg = Pipe(b, "g_sg", 4)

            @blk.sync
            def _(s):
                for ft in range(IT3):
                    s.wait_ge(mp.cs[0].h, mp.cb[0] + max(0, ft - 1))
                    s.dma_start(out=bass.AP(wstr, (ft % 2) * ET * 128,
                                            [[2 * ET * 128, 128], [128, ET], [1, 128]]),
                                in_=bass.AP(waiv_bf, ft * 128,
                                            [[3 * E, 128], [128 * 3 * E, ET], [1, 128]])).then_inc(*wp.pinc_part(ft))

            @blk.tensor
            def _(te):
                for ft in range(IT3):
                    wp.cwait(te, ft)
                    mp.pwait(te, ft)
                    buf = wstr[:, (ft % 2) * ET * 128:(ft % 2) * ET * 128 + ET * 128]
                    for e in range(ET):
                        mm = te.matmul(bk[ft % 4][:, 0:G], buf[:, e * 128:(e + 1) * 128],
                                       xnc[:, e * G:(e + 1) * G],
                                       start=(e == 0), stop=(e == ET - 1))
                    mm.then_inc(*mp.pinc(ft))

            @blk.vector
            def _(v):
                v.wait_ge(*st_sp.tail())
                for ft in range(IT3):
                    mp.cwait(v, ft)
                    if ft < ET:
                        dstb = t_a[:, ft * G:(ft + 1) * G]
                    elif ft < 2 * ET:
                        dstb = t_i[:, (ft - ET) * G:(ft - ET + 1) * G]
                    else:
                        dstb = t_v[:, (ft - 2 * ET) * G:(ft - 2 * ET + 1) * G]
                    v.tensor_tensor(dstb, bk[ft % 4][:, 0:G], rb_sb[:, 0:G], ALU.mult)
                    if ft < 2 * ET:
                        sg.pwait(v, ft)
                        v.engine_nop().then_inc(*sg.pinc(ft))
                    v.engine_nop().then_inc(*mp.cinc(ft))

            @blk.scalar
            def _(s):
                for ft in range(2 * ET):
                    sg.cwait(s, ft)
                    if ft < ET:
                        s.activation(t_a[:, ft * G:(ft + 1) * G], t_a[:, ft * G:(ft + 1) * G],
                                     AF.Sigmoid, bias=db_sb[:, ft:ft + 1])
                    else:
                        e = ft - ET
                        s.activation(t_i[:, e * G:(e + 1) * G], t_i[:, e * G:(e + 1) * G],
                                     AF.Sigmoid)
                    s.engine_nop().then_inc(*sg.cinc(ft))

        with nc.Block() as blk:
            R = ET
            import os as _os2
            _FRESH = "b" if _os2.environ.get("CHAIN_FRESH", "0") == "1" else "g"
            p1 = b.sem(f"{_FRESH}_c0"); v2 = b.sem(f"{_FRESH}_c1"); v3 = b.sem(f"{_FRESH}_c2")
            s4 = b.sem(f"{_FRESH}_c3"); p5 = b.sem(f"{_FRESH}_c4"); p6 = b.sem(f"{_FRESH}_c5")
            v7 = b.sem(f"{_FRESH}_s2"); s8 = b.sem(f"{_FRESH}_c7")
            v9 = b.sem(f"{_FRESH}_s1"); v10 = b.sem(f"{_FRESH}_c6")
            s11 = b.sem(f"{_FRESH}_l1"); s12 = b.sem(f"{_FRESH}_l2")
            p13 = b.sem(f"{_FRESH}_mv_p"); v14 = b.sem(f"{_FRESH}_mv_c")
            bs = {x: x.v for x in (p1, v2, v3, s4, p5, p6, v7, s8, v9, v10,
                                   s11, s12, p13, v14)}
            spb = st_sp.v
            def sl(r):
                return slice(r * G, (r + 1) * G)

            import os as _os
            _POOL = _os.environ.get("CHAIN_POOL", "1") == "1"

            @blk.gpsimd
            def _(gp):
                def R1(r):
                    gp.tensor_tensor(g1[:, sl(r)], t_a[:, sl(r)], t_a[:, sl(r)],
                                     ALU.mult).then_inc(*p1.inc())
                def R5(r):
                    gp.tensor_tensor(t_v[:, sl(r)], t_v[:, sl(r)], t_i[:, sl(r)],
                                     ALU.mult).then_inc(*p5.inc())
                def R6(r):
                    gp.wait_ge(p5.h, bs[p5] + r + 1)
                    gp.wait_ge(s4.h, bs[s4] + r + 1)
                    gp.tensor_tensor(t_v[:, sl(r)], t_v[:, sl(r)], g1[:, sl(r)],
                                     ALU.mult).then_inc(*p6.inc())
                def R13(r):
                    gp.wait_ge(p6.h, bs[p6] + r + 1)
                    gp.wait_ge(s12.h, bs[s12] + r + 1)
                    gp.tensor_tensor(t_v[:, sl(r)], t_v[:, sl(r)], t_a[:, sl(r)],
                                     ALU.mult).then_inc(*p13.inc())
                if _POOL:
                    for r in range(R):
                        R1(r)
                        R5(r)
                    for r in range(R):
                        R6(r)
                    for r in range(R):
                        R13(r)
                else:
                    for r in range(R):
                        R5(r)
                gp.wait_ge(v14.h, bs[v14] + R)
                gp.dma_start(out=bass.AP(sp_csw, g * G, [[T, 128], [128 * T, ET], [1, G]]),
                             in_=bass.AP(t_i, 0, [[ET * G, 128], [G, ET], [1, G]])).then_inc(*st_sp.inc(16))
                gp.wait_ge(s11.h, bs[s11] + R)
                gp.dma_start(out=bass.AP(sp_cd, g * G, [[T, 128], [128 * T, ET], [1, G]]),
                             in_=bass.AP(g1, 0, [[ET * G, 128], [G, ET], [1, G]])).then_inc(*st_sp.inc(16))

            @blk.vector
            def _(v):
                v.wait_ge(st_sp.h, spb)
                def R2(r):
                    v.wait_ge(p1.h, bs[p1] + r + 1)
                    v.tensor_scalar(g1[:, sl(r)], g1[:, sl(r)], -1.0, 1.0,
                                    ALU.mult, ALU.add).then_inc(*v2.inc())
                def R7(r):
                    v.wait_ge(p1.h, bs[p1] + r + 1)
                    v.tensor_single_scalar(t_a[:, sl(r)], t_a[:, sl(r)], 1e-10,
                                           ALU.max).then_inc(*v7.inc())
                def R3(r):
                    v.wait_ge(v2.h, bs[v2] + r + 1)
                    v.tensor_single_scalar(g1[:, sl(r)], g1[:, sl(r)], 1e-8,
                                           ALU.max).then_inc(*v3.inc())
                def R9(r):
                    v.wait_ge(s8.h, bs[s8] + r + 1)
                    v.wait_ge(p5.h, bs[p5] + r + 1)
                    v.tensor_tensor_scan(t_i[:, sl(r)], rstmask[:, 0:G], t_a[:, sl(r)],
                                         0.0, ALU.mult, ALU.add).then_inc(*v9.inc())
                def R10(r):
                    v.wait_ge(v9.h, bs[v9] + r + 1)
                    v.tensor_single_scalar(t_a[:, sl(r)], t_i[:, sl(r)], C0,
                                           ALU.max).then_inc(*v10.inc())
                def R14(r):
                    v.wait_ge(p13.h, bs[p13] + r + 1)
                    v.wait_ge(s11.h, bs[s11] + r + 1)
                    v.tensor_tensor_scan(t_i[:, sl(r)], rstmask[:, 0:G], t_v[:, sl(r)],
                                         0.0, ALU.mult, ALU.add).then_inc(*v14.inc())
                def R15(r):
                    v.wait_ge(v14.h, bs[v14] + r + 1)
                    v.wait_ge(s11.h, bs[s11] + r + 1)
                    v.tensor_copy(ctd[:, r * NSC + g * SPG:r * NSC + g * SPG + SPG],
                                  bass.AP(g1, r * G + SCH - 1, [[ET * G, 128], [SCH, SPG]]))
                    v.tensor_tensor(cfs[:, r * NSC + g * SPG:r * NSC + g * SPG + SPG],
                                    bass.AP(g1, r * G + SCH - 1, [[ET * G, 128], [SCH, SPG]]),
                                    bass.AP(t_i, r * G + SCH - 1, [[ET * G, 128], [SCH, SPG]]),
                                    ALU.mult)
                def R1v(r):
                    v.tensor_tensor(g1[:, sl(r)], t_a[:, sl(r)], t_a[:, sl(r)],
                                    ALU.mult).then_inc(*p1.inc())
                def R6v(r):
                    v.wait_ge(p5.h, bs[p5] + r + 1)
                    v.wait_ge(s4.h, bs[s4] + r + 1)
                    v.tensor_tensor(t_v[:, sl(r)], t_v[:, sl(r)], g1[:, sl(r)],
                                    ALU.mult).then_inc(*p6.inc())
                def R13v(r):
                    v.wait_ge(p6.h, bs[p6] + r + 1)
                    v.wait_ge(s12.h, bs[s12] + r + 1)
                    v.tensor_tensor(t_v[:, sl(r)], t_v[:, sl(r)], t_a[:, sl(r)],
                                    ALU.mult).then_inc(*p13.inc())
                for r in range(R + 4):
                    if r < R:
                        if not _POOL:
                            R1v(r)
                        R2(r)
                        R7(r)
                    if 0 <= r - 1 < R: R3(r - 1)
                    if not _POOL and 0 <= r - 2 < R: R6v(r - 2)
                    if 0 <= r - 2 < R: R9(r - 2)
                    if 0 <= r - 3 < R: R10(r - 3)
                for r in range(R):
                    if not _POOL:
                        R13v(r)
                    R14(r)
                    if r >= 1:
                        R15(r - 1)
                R15(R - 1)

            @blk.scalar
            def _(s):
                s4a = b.sem(f"{_FRESH}_r")
                b4a = s4a.v
                def R4a(r):
                    s.wait_ge(v3.h, bs[v3] + r + 1)
                    s.activation(g1[:, sl(r)], g1[:, sl(r)], AF.Ln).then_inc(*s4a.inc())
                def R4b(r):
                    s.wait_ge(s4a.h, b4a + r + 1)
                    s.activation(g1[:, sl(r)], g1[:, sl(r)], AF.Exp,
                                 scale=0.5).then_inc(*s4.inc())
                def R8(r):
                    s.wait_ge(v7.h, bs[v7] + r + 1)
                    s.activation(t_a[:, sl(r)], t_a[:, sl(r)], AF.Ln).then_inc(*s8.inc())
                def R11(r):
                    s.wait_ge(v9.h, bs[v9] + r + 1)
                    s.wait_ge(p6.h, bs[p6] + r + 1)
                    s.activation(g1[:, sl(r)], t_i[:, sl(r)], AF.Exp).then_inc(*s11.inc())
                def R12(r):
                    s.wait_ge(v10.h, bs[v10] + r + 1)
                    s.activation(t_a[:, sl(r)], t_a[:, sl(r)], AF.Exp,
                                 scale=-1.0).then_inc(*s12.inc())
                for r in range(R + 8):
                    if 0 <= r - 2 < R:
                        R4a(r - 2)
                        R8(r - 2)
                    if 0 <= r - 3 < R:
                        R4b(r - 3)
                    if 0 <= r - 8 < R:
                        R11(r - 8)
                        R12(r - 8)

    with nc.Block() as blk:
        l1 = b.sem("g_l1")
        l2 = b.sem("g_l2")

        @blk.vector
        def _(v):
            for e in range(ET):
                sl = slice(e * NSC, (e + 1) * NSC)
                v.tensor_single_scalar(wr2[:, sl], ctd[:, sl], 1e-10, ALU.max)
            v.engine_nop().then_inc(*l1.inc())
        k1 = l1.v

        @blk.scalar
        def _(s):
            s.wait_ge(l1.h, k1)
            for e in range(ET):
                sl = slice(e * NSC, (e + 1) * NSC)
                s.activation(wr2[:, sl], wr2[:, sl], AF.Ln)
            s.engine_nop().then_inc(*l2.inc())
        k2 = l2.v

        @blk.vector
        def _(v):
            v.wait_ge(l2.h, k2)
            for e in range(ET):
                sl = slice(e * NSC, (e + 1) * NSC)
                v.tensor_tensor_scan(ccd[:, sl], onesf[:, 0:NSC], wr2[:, sl], 0.0,
                                     ALU.mult, ALU.add)
            v.drain()
            for e in range(ET):
                sl = slice(e * NSC, (e + 1) * NSC)
                v.tensor_single_scalar(wr2[:, sl], ccd[:, sl], C0, ALU.max)
            v.engine_nop().then_inc(*l1.inc())
        k3 = l1.v

        @blk.scalar
        def _(s):
            s.wait_ge(l1.h, k3)
            for e in range(ET):
                sl = slice(e * NSC, (e + 1) * NSC)
                s.activation(ccd[:, sl], ccd[:, sl], AF.Exp)
            s.drain()
            for e in range(ET):
                sl = slice(e * NSC, (e + 1) * NSC)
                s.activation(wr2[:, sl], wr2[:, sl], AF.Exp, scale=-1.0)
            s.engine_nop().then_inc(*l2.inc())
        k4 = l2.v

        @blk.vector
        def _(v):
            v.wait_ge(l2.h, k4)
            for e in range(ET):
                sl = slice(e * NSC, (e + 1) * NSC)
                v.tensor_tensor(wr2[:, sl], cfs[:, sl], wr2[:, sl], ALU.mult)
            v.drain()
            for e in range(ET):
                sl = slice(e * NSC, (e + 1) * NSC)
                v.tensor_tensor_scan(cwc[:, sl], onesf[:, 0:NSC], wr2[:, sl], 0.0,
                                     ALU.mult, ALU.add)
            v.drain()
            v.memset(inc[:, :], 0.0)
            v.drain()
            for e in range(ET):
                v.tensor_tensor(inc[:, e * NSC + 1:(e + 1) * NSC],
                                ccd[:, e * NSC:(e + 1) * NSC - 1],
                                cwc[:, e * NSC:(e + 1) * NSC - 1], ALU.mult)
            v.drain()

    with nc.Block() as blk:
        lc = [b.sem("g_ldc"), b.sem("g_lsp_p0")]
        lcb = [x.v for x in lc]
        wl = b.sem("g_w2_p0")
        wlb = wl.v
        cst = b.sem("g_s1")
        cstb = cst.v
        cmm = b.sem("g_mm")
        cmmb = cmm.v
        cev = b.sem("g_ev")
        cevb = cev.v
        douts = [b.sem("g_rs_c0"), b.sem("g_rs_c1")]
        doutb = [x.v for x in douts]
        cdb = [t_a, t_v]
        cwb = [t_i, g1]
        NBm = NG * ET

        @blk.sync
        def _(s):
            s.wait_ge(*st_sp.tail())
            s.dma_start(out=bass.AP(wmr, 0,
                                    [[ET * ET * 128, 128], [ET * 128, ET], [128, ET], [1, 128]]),
                        in_=bass.AP(wmix_bf, 0,
                                    [[E, 128], [128, ET], [128 * E, ET], [1, 128]])).then_inc(*wl.inc(16))
            for g in range(NG):
                if g >= 2:
                    s.wait_ge(cst.h, cstb + g - 1)
                    s.wait_ge(cev.h, cevb + (g - 1) * ET)
                s.dma_start(out=bass.AP(cdb[g % 2], 0, [[ET * G, 128], [G, ET], [1, G]]),
                            in_=bass.AP(sp_cd, g * G, [[T, 128], [128 * T, ET], [1, G]])).then_inc(*lc[g % 2].inc(16))
                s.dma_start(out=bass.AP(cwb[g % 2], 0, [[ET * G, 128], [G, ET], [1, G]]),
                            in_=bass.AP(sp_csw, g * G, [[T, 128], [128 * T, ET], [1, G]])).then_inc(*lc[g % 2].inc(16))
                s.dma_start(out=bass.AP(xg[g % 2], 0, [[ET * G, 128], [G, ET], [1, G]]),
                            in_=bass.AP(src, g * G, [[T, 128], [128 * T, ET], [1, G]])).then_inc(*lc[g % 2].inc(16))

        @blk.vector
        def _(v):
            def states(g):
                v.wait_ge(lc[g % 2].h, lcb[g % 2] + (g // 2 + 1) * 48)
                if g >= 2:
                    v.wait_ge(cmm.h, cmmb + (g - 1) * ET)
                op = None
                for e in range(ET):
                    for ch in range(SPG):
                        sl = slice(e * G + ch * SCH, e * G + (ch + 1) * SCH)
                        hsl = slice((g % 2) * ET * G + e * G + ch * SCH,
                                    (g % 2) * ET * G + e * G + (ch + 1) * SCH)
                        op = v.scalar_tensor_tensor(hb2[:, hsl], cwb[g % 2][:, sl],
                                                    inc[:, e * NSC + g * SPG + ch:e * NSC + g * SPG + ch + 1],
                                                    cdb[g % 2][:, sl], ALU.add, ALU.mult)
                op.then_inc(*cst.inc())
            def evac(idx):
                g, eo = divmod(idx, ET)
                v.wait_ge(cmm.h, cmmb + idx + 1)
                if idx >= 2:
                    v.wait_ge(douts[idx % 2].h, doutb[idx % 2] + (idx // 2) * 16)
                v.scalar_tensor_tensor(resg[:, (idx % 2) * G:(idx % 2) * G + G],
                                       bk[idx % 4][:, 0:G], bmix_sb[:, eo:eo + 1],
                                       xg[g % 2][:, eo * G:(eo + 1) * G],
                                       ALU.add, ALU.add).then_inc(*cev.inc())
            for g in range(NG):
                states(g)
                if g >= 1:
                    for eo in range(ET):
                        evac((g - 1) * ET + eo)
            for eo in range(ET):
                evac((NG - 1) * ET + eo)

        @blk.tensor
        def _(te):
            for g in range(NG):
                for eo in range(ET):
                    idx = g * ET + eo
                    if idx == 0:
                        te.wait_ge(wl.h, wlb + 16)
                    if eo == 0:
                        te.wait_ge(cst.h, cstb + g + 1)
                    if idx >= 4:
                        te.wait_ge(cev.h, cevb + idx - 3)
                    for e in range(ET):
                        mm = te.matmul(bk[idx % 4][:, 0:G],
                                       wmr[:, (eo * ET + e) * 128:(eo * ET + e + 1) * 128],
                                       hb2[:, (g % 2) * ET * G + e * G:(g % 2) * ET * G + (e + 1) * G],
                                       start=(e == 0), stop=(e == ET - 1))
                    mm.then_inc(*cmm.inc())

        @blk.gpsimd
        def _(gp):
            for idx in range(NBm):
                g, eo = divmod(idx, ET)
                gp.wait_ge(cev.h, cevb + idx + 1)
                gp.dma_start(out=dst[eo * 128:(eo + 1) * 128, g * G:(g + 1) * G],
                             in_=resg[:, (idx % 2) * G:(idx % 2) * G + G]).then_inc(*douts[idx % 2].inc(16))

    with nc.Block() as blk:
        @blk.sync
        def _(s):
            for sm in b.sem("g_rs_c0"), b.sem("g_rs_c1"):
                s.wait_ge(*sm.tail())
    sc.close()




def build_program(cfg, phases=('attn_a', 'ffn1', 'attn_b', 'ffn2')):
    E, T, G, I = cfg["E"], cfg["T"], cfg["G"], cfg["I"]
    ET = E // 128
    b = B(cfg)
    nc = b.nc
    replica = [list(range(cfg["n_cores"]))]

    xT = b.dram("xT", [E, T], F32, kind="ExternalInput")
    w_attn_T = b.dram("w_attn_T", [E, 3 * E], F32, kind="ExternalInput")
    w_proj_T = b.dram("w_proj_T", [E, E], F32, kind="ExternalInput")
    W_aiv = b.dram("W_aiv", [E, 3 * E], F32, kind="ExternalInput")
    W_mix = b.dram("W_mix", [E, E], F32, kind="ExternalInput")
    decay_bias = b.dram("decay_bias", [E, 1], F32, kind="ExternalInput")
    b_mix = b.dram("b_mix", [E, 1], F32, kind="ExternalInput")
    w_gate = b.dram("w_gate", [I, E], F32, kind="ExternalInput")
    w_up = b.dram("w_up", [I, E], F32, kind="ExternalInput")
    w_down = b.dram("w_down", [E, I], F32, kind="ExternalInput")
    identc_b = b.dram("identc_b", [128, 128], BF16, kind="ExternalInput")
    identc_f = b.dram("identc_f", [128, 128], F32, kind="ExternalInput")
    maskTc = b.dram("maskTc", [128, 128], BF16, kind="ExternalInput")
    outT = b.dram("outT", [E, T], F32, kind="ExternalOutput")

    res1 = b.dram("res1", [E, T], F32)
    res2 = b.dram("res2", [E, T], F32)
    res3 = b.dram("res3", [E, T], F32)
    sp_x2n = b.dram("sp_x2n", [E, T], F32)
    sp_up = b.dram("sp_up", [I, T], BF16)
    sp_gate = b.dram("sp_gate", [I, T], BF16)
    sp_t = b.dram("sp_t", [I, T], BF16)
    sp_opre = b.dram("sp_opre", [I, T], BF16)
    sp_csw = b.dram("sp_csw", [E, T], F32)
    sp_cd = b.dram("sp_cd", [E, T], F32)
    wa_bf = b.dram("wa_bf", [E, 3 * E], BF16)
    wp_bf = b.dram("wp_bf", [E, E], BF16)
    waiv_bf = b.dram("waiv_bf", [E, 3 * E], BF16)
    wmix_bf = b.dram("wmix_bf", [E, E], BF16)
    wdq_upT = b.dram("wdq_upT", [E, I], BF16)
    wdq_gateT = b.dram("wdq_gateT", [E, I], BF16)
    wdq_downT = b.dram("wdq_downT", [I, E], BF16)
    cc_in = b.dram("cc_in", [1, 2], F32)
    cc_out = b.dram("cc_out", [1, 2], F32)

    ones_col_b = b.sbuf("ones_col_b", [128, 1], BF16)
    ones_row_f = b.sbuf("ones_row_f", [1, 128], F32)
    one_f = b.sbuf("one_f", [1, 1], F32)
    onesf = b.sbuf("onesf", [128, 512], F32)
    ident_b = b.sbuf("ident_b", [128, 128], BF16)
    ident_f = b.sbuf("ident_f", [128, 128], F32)
    maskT_b = b.sbuf("maskT_b", [128, 128], BF16)
    db_sb = b.sbuf("db_sb", [128, ET], F32)
    bmix_sb = b.sbuf("bmix_sb", [128, ET], F32)
    eps_t = b.sbuf("eps_t", [1, 1], F32)
    scal = b.sbuf("scal", [128, 16], F32)
    st = make_stats(b)
    st.update(dict(ones_col_b=ones_col_b, ones_row_f=ones_row_f, one_f=one_f, eps_t=eps_t))

    with nc.Block() as blk:
        @blk.sync
        def _(s):
            sd = b.sem("c0")
            s.dma_start(out=ident_b[:, :], in_=identc_b[:, :]).then_inc(*sd.inc(16))
            s.dma_start(out=ident_f[:, :], in_=identc_f[:, :]).then_inc(*sd.inc(16))
            s.dma_start(out=maskT_b[:, :], in_=maskTc[:, :]).then_inc(*sd.inc(16))
            for e in range(ET):
                s.dma_start(out=db_sb[:, e:e + 1], in_=decay_bias[e * 128:(e + 1) * 128, :]).then_inc(*sd.inc(16))
                s.dma_start(out=bmix_sb[:, e:e + 1], in_=b_mix[e * 128:(e + 1) * 128, :]).then_inc(*sd.inc(16))
            s.wait_ge(*sd.tail())

        @blk.vector
        def _(v):
            v.memset(ones_col_b[:, :], 1.0)
            v.memset(ones_row_f[:, :], 1.0)
            v.memset(one_f[:, :], 1.0)
            v.memset(onesf[:, :], 1.0)
            v.memset(eps_t[:, :], 1e-5)
            v.memset(scal[:, :], 0.0)
            v.memset(st["magic_col"][:, :], MAGIC)

    emit_cast_bf16(b, w_attn_T, wa_bf, "ca")
    emit_cast_bf16(b, w_proj_T, wp_bf, "cp")
    emit_cast_bf16(b, W_aiv, waiv_bf, "cv")
    emit_cast_bf16(b, W_mix, wmix_bf, "cm")
    emit_weight_prep(b, w_up, wdq_upT, ident_b, "wu")
    emit_weight_prep(b, w_gate, wdq_gateT, ident_b, "wg")
    emit_weight_prep(b, w_down, wdq_downT, ident_b, "wd")

    chain = {"attn_a": res1, "ffn1": res2, "attn_b": res3, "ffn2": outT}
    last = None
    cur_src = xT
    for ph in phases:
        dst = chain[ph] if ph != phases[-1] else outT
        if ph == "attn_a":
            emit_attn_a(b, st, cur_src, dst, wa_bf, wp_bf, ident_f, maskT_b)
        elif ph in ("ffn1", "ffn2"):
            emit_ffn(b, st, cur_src, dst, wdq_upT, wdq_gateT, wdq_downT, sp_x2n, sp_up,
                     sp_gate, sp_t, sp_opre, scal, cc_in, cc_out, ident_f, replica, ph)
        elif ph == "attn_b":
            emit_attn_b(b, st, cur_src, dst, waiv_bf, wmix_bf, db_sb, bmix_sb, sp_csw,
                        sp_cd, ident_f, onesf)
        cur_src = dst

    return b


def host_inputs(cfg, core_x, weights):
    import ml_dtypes
    E = cfg["E"]
    bf = ml_dtypes.bfloat16
    m = {
        "xT": np.ascontiguousarray(core_x.T.astype(np.float32)),
        "w_attn_T": np.ascontiguousarray(weights["w_attn"].T.astype(np.float32)),
        "w_proj_T": np.ascontiguousarray(weights["w_proj"].T.astype(np.float32)),
        "W_aiv": np.ascontiguousarray(weights["W_aiv"].astype(np.float32)),
        "W_mix": np.ascontiguousarray(weights["W_mix"].astype(np.float32)),
        "decay_bias": np.ascontiguousarray(weights["decay_bias"].astype(np.float32).reshape(E, 1)),
        "b_mix": np.ascontiguousarray(weights["b_mix"].astype(np.float32).reshape(E, 1)),
        "w_gate": np.ascontiguousarray(weights["w_gate"].astype(np.float32)),
        "w_up": np.ascontiguousarray(weights["w_up"].astype(np.float32)),
        "w_down": np.ascontiguousarray(weights["w_down"].astype(np.float32)),
        "identc_b": np.eye(128, dtype=np.float32).astype(bf),
        "identc_f": np.eye(128, dtype=np.float32),
        "maskTc": np.tril(np.ones((128, 128), np.float32)).T.astype(bf),
    }
    return m


from concourse.bass_utils import run_bass_kernel_spmd

_PROG_CACHE = {}

def kernel(L0_attn_a_x, L0_attn_a_w_attn, L0_attn_a_w_proj, L0_attn_b_W_aiv,
           L0_attn_b_decay_bias, L0_attn_b_W_mix, L0_attn_b_b_mix,
           L0_ffn_w_gate, L0_ffn_w_up, L0_ffn_w_down):
    X = np.asarray(L0_attn_a_x)
    Bn, T, E = X.shape
    cfg = default_cfg(T=T)
    assert Bn == cfg["n_cores"] and E == cfg["E"]
    key = (T, E)
    if key not in _PROG_CACHE:
        _PROG_CACHE[key] = build_program(cfg)
    b = _PROG_CACHE[key]
    weights = dict(
        w_attn=np.asarray(L0_attn_a_w_attn), w_proj=np.asarray(L0_attn_a_w_proj),
        W_aiv=np.asarray(L0_attn_b_W_aiv), W_mix=np.asarray(L0_attn_b_W_mix),
        decay_bias=np.asarray(L0_attn_b_decay_bias), b_mix=np.asarray(L0_attn_b_b_mix),
        w_gate=np.asarray(L0_ffn_w_gate), w_up=np.asarray(L0_ffn_w_up),
        w_down=np.asarray(L0_ffn_w_down),
    )
    in_maps = [host_inputs(cfg, X[c], weights) for c in range(Bn)]
    res = run_bass_kernel_spmd(b.nc, in_maps, core_ids=list(range(Bn)))
    out = np.empty((Bn, T, E), np.float32)
    for c in range(Bn):
        out[c] = np.ascontiguousarray(np.asarray(res.results[c]["outT"]).T)
    return out


# version 15 (final)
# speedup vs baseline: 1.0586x; 1.0586x over previous
from contextlib import ExitStack
import numpy as np
import concourse.bass as bass
import concourse.mybir as mybir

F32 = mybir.dt.float32
BF16 = mybir.dt.bfloat16
I16 = mybir.dt.int16
U16 = mybir.dt.uint16
ALU = mybir.AluOpType
AF = mybir.ActivationFunctionType
AX = mybir.AxisListType

MAGIC = 12582912.0
C0 = float(np.float32(np.log(np.float32(1e-10))))


class Sem:
    __slots__ = ("h", "v")
    def __init__(self, nc, name):
        self.h = nc._stack.enter_context(nc.semaphore(name))
        self.v = 0
    def inc(self, n=1):
        self.v += n
        return self.h, n
    def tail(self):
        return self.h, self.v


class Pipe:
    def __init__(self, b, name, nbuf, pq=1, cq=1, dma_p=False, dma_c=False):
        self.nbuf = nbuf
        self.pq, self.cq = pq, cq
        self.ps = [b.sem(f"{name}_p{i}" if dma_p else f"{name}_p")
                   for i in range(nbuf if dma_p else 1)]
        self.cs = [b.sem(f"{name}_c{i}" if dma_c else f"{name}_c")
                   for i in range(nbuf if dma_c else 1)]
        if not dma_p:
            self.ps = self.ps * nbuf if False else self.ps
        self.pb = [s.v for s in self.ps]
        self.cb = [s.v for s in self.cs]
        self._pn = len(self.ps)
        self._cn = len(self.cs)
    def pwait(self, eng, i):
        if self._cn == 1:
            eng.wait_ge(self.cs[0].h, self.cb[0] + max(0, i - self.nbuf + 1) * self.cq)
        else:
            j = i % self._cn
            rounds = max(0, (i - self.nbuf) // self._cn + 1)
            eng.wait_ge(self.cs[j].h, self.cb[j] + rounds * self.cq)
    def pinc(self, i):
        return self.ps[i % self._pn].inc(self.pq)
    def pinc_part(self, i, n=16):
        return self.ps[i % self._pn].inc(n)
    def cwait(self, eng, i):
        j = i % self._pn
        eng.wait_ge(self.ps[j].h, self.pb[j] + (i // self._pn + 1) * self.pq)
    def cinc(self, i):
        return self.cs[i % self._cn].inc(self.cq)


class B:
    def __init__(self, cfg):
        self.cfg = cfg
        nc = bass.Bass("TRN2", target_bir_lowering=False, debug=False)
        nc._stack = ExitStack()
        self.nc = nc
        self.sems = {}
    def sem(self, name):
        if name not in self.sems:
            self.sems[name] = Sem(self.nc, name)
        return self.sems[name]
    def sbuf(self, name, shape, dtype):
        return self.nc._stack.enter_context(self.nc.sbuf_tensor(name, shape, dtype))
    def psum(self, name, shape, dtype=F32):
        return self.nc._stack.enter_context(self.nc.psum_tensor(name, shape, dtype))
    def dram(self, name, shape, dtype, kind="Internal"):
        return self.nc.dram_tensor(name, shape, dtype, kind=kind)


def default_cfg(T=4096):
    return dict(E=1024, T=T, G=512, H=16, D=64, ACH=128, SCH=64, I=3072, n_cores=8)


from concourse.bass import BassScalarEngine as _BSE
if not hasattr(_BSE, "engine_nop"):
    _BSE.engine_nop = lambda self: self.drain()
try:
    from concourse.bass import BassGpSimd as _BGS
    if not hasattr(_BGS, "engine_nop"):
        _BGS.engine_nop = lambda self: self.drain()
except Exception:
    pass


from contextlib import ExitStack



class Scope:
    def __init__(self, b):
        self.b = b
        self.es = ExitStack()
    def sbuf(self, name, shape, dtype):
        return self.es.enter_context(self.b.nc.sbuf_tensor(name, shape, dtype))
    def psum(self, name, shape, dtype=F32):
        return self.es.enter_context(self.b.nc.psum_tensor(name, shape, dtype))
    def banks(self, name, n):
        return [self.psum(f"{name}{i}", [128, 512], F32) for i in range(n)]
    def close(self):
        self.es.close()


def emit_weight_prep(b, w, wT_out, ident_b, tag):
    nc = b.nc
    O, Iin = w.shape
    OT, ITn = O // 128, Iin // 128
    nb = Iin // 32
    sc = Scope(b)
    wtmp = sc.sbuf(f"{tag}_tmp", [128, Iin], F32)
    absb = sc.sbuf(f"{tag}_abs", [128, nb], F32)
    sblk = sc.sbuf(f"{tag}_s", [128, nb], F32)
    rblk = sc.sbuf(f"{tag}_rs", [128, nb], F32)
    q16 = sc.sbuf(f"{tag}_q", [128, Iin], I16)
    qf = sc.sbuf(f"{tag}_qf", [128, Iin], F32)
    wq = sc.sbuf(f"{tag}_wq", [128, Iin], BF16)
    pst = [sc.psum(f"{tag}_pst{i}", [128, 1024], BF16) for i in range(2)]
    wtr = sc.sbuf(f"{tag}_tr", [128, ITn * 128], BF16)
    ld = b.sem("g_ldx")
    for ot in range(OT):
        with nc.Block() as blk:
            @blk.sync
            def _(s):
                s.dma_start(out=wtmp[:, :], in_=w[ot * 128:(ot + 1) * 128, :]).then_inc(*ld.inc(16))
                s.wait_ge(*ld.tail())

            @blk.vector
            def _(v):
                v.wait_ge(*ld.tail())
                v.tensor_reduce(absb.ap().rearrange("p (bk o) -> p bk o", o=1),
                                wtmp.ap().rearrange("p (bk k) -> p bk k", k=32),
                                AX.X, ALU.max, apply_absolute_value=True)
                v.drain()
                v.tensor_scalar(sblk[:, :], absb[:, :], 1.0 / 7.0, 1e-12, ALU.mult, ALU.max)
                v.drain()
                v.reciprocal(rblk[:, :], sblk[:, :])
                v.drain()
                v.tensor_tensor(qf.ap().rearrange("p (bk k) -> p bk k", k=32),
                                wtmp.ap().rearrange("p (bk k) -> p bk k", k=32),
                                bass.AP(rblk, 0, [[nb, 128], [1, nb], [0, 32]]), ALU.mult)
                v.drain()
                v.tensor_single_scalar(qf[:, :], qf[:, :], MAGIC, ALU.add)
                v.drain()
                v.tensor_single_scalar(q16[:, :], qf[:, :], MAGIC, ALU.subtract)
                v.drain()
                v.tensor_tensor(wq.ap().rearrange("p (bk k) -> p bk k", k=32),
                                q16.ap().rearrange("p (bk k) -> p bk k", k=32),
                                bass.AP(sblk, 0, [[nb, 128], [1, nb], [0, 32]]), ALU.mult)

        with nc.Block() as blk:
            tp = Pipe(b, "g_tp", 2)
            ep = Pipe(b, "wp_ep", 1, cq=16, dma_c=True)

            @blk.tensor
            def _(te):
                for it in range(ITn):
                    tp.pwait(te, it)
                    te.transpose(pst[it % 2][:, 0:128],
                                 wq[:, it * 128:(it + 1) * 128], ident_b[:, :]).then_inc(*tp.pinc(it))

            @blk.vector
            def _(v):
                ep.pwait(v, 0)
                for it in range(ITn):
                    tp.cwait(v, it)
                    v.tensor_copy(wtr[:, it * 128:(it + 1) * 128],
                                  pst[it % 2][:, 0:128]).then_inc(*tp.cinc(it))
                v.engine_nop().then_inc(*ep.pinc(0))

            @blk.sync
            def _(s):
                ep.cwait(s, 0)
                s.dma_start(out=bass.AP(wT_out, ot * 128, [[O, 128], [128 * O, ITn], [1, 128]]),
                            in_=bass.AP(wtr, 0, [[ITn * 128, 128], [128, ITn], [1, 128]])).then_inc(*ep.cinc(0))
    with nc.Block() as blk:
        @blk.sync
        def _(s):
            for sm in b.sem("wp_ep_c0"), b.sem("wp_ep_c1"):
                s.wait_ge(*sm.tail())
    sc.close()


def emit_cast_bf16(b, w, w_bf, tag):
    nc = b.nc
    R, Ccols = w.shape
    RT = R // 128
    sc = Scope(b)
    stg = [sc.sbuf(f"{tag}_stg{i}", [128, Ccols], F32) for i in range(2)]
    ob = [sc.sbuf(f"{tag}_ob{i}", [128, Ccols], BF16) for i in range(2)]
    with nc.Block() as blk:
        lp = Pipe(b, "g_bl", 2, pq=16, dma_p=True)
        epipe = Pipe(b, "g_bo", 2, cq=16, dma_c=True)

        @blk.sync
        def _(s):
            for rt in range(RT):
                lp.pwait(s, rt)
                s.dma_start(out=stg[rt % 2][:, :], in_=w[rt * 128:(rt + 1) * 128, :]).then_inc(*lp.pinc(rt))

        @blk.vector
        def _(v):
            for rt in range(RT):
                lp.cwait(v, rt)
                epipe.pwait(v, rt)
                v.tensor_copy(ob[rt % 2][:, :], stg[rt % 2][:, :])
                v.engine_nop().then_inc(*lp.cinc(rt))
                v.engine_nop().then_inc(*epipe.pinc(rt))

        @blk.gpsimd
        def _(g):
            for rt in range(RT):
                epipe.cwait(g, rt)
                g.dma_start(out=w_bf[rt * 128:(rt + 1) * 128, :], in_=ob[rt % 2][:, :]).then_inc(*epipe.cinc(rt))
            for sm in epipe.cs:
                g.wait_ge(*sm.tail())
    sc.close()


def emit_ln_group(b, blk, st, banks, xg, xb, sqb, xnc_sb, rb_sb, rcol_sb, want_rcol,
                  x2n_out=None):
    cfg = b.cfg
    E, G = cfg["E"], cfg["G"]
    ET, TPG = E // 128, G // 128
    s_mm = b.sem("ln_mm")
    s_ev = b.sem("ln_ev")
    ps_st, ps_st2, ps_mb, ps_rb, ps_rc = banks[0], banks[1], banks[2], banks[3], banks[4]

    lnc = Pipe(b, "ln_c", 2)

    @blk.scalar
    def _(s):
        for e in range(ET):
            lnc.pwait(s, e)
            sl = slice((e % 2) * G, (e % 2) * G + G)
            s.activation(xb[:, sl], xg[:, e * G:(e + 1) * G], AF.Copy)
            s.drain()
            s.activation(sqb[:, sl], xg[:, e * G:(e + 1) * G], AF.Square)
            s.engine_nop().then_inc(*lnc.pinc(e))

    @blk.tensor
    def _(te):
        for e in range(ET):
            lnc.cwait(te, e)
            sl = slice((e % 2) * G, (e % 2) * G + G)
            mm1 = te.matmul(ps_st[0:1, 0:G], st["ones_col_b"][:, :], xb[:, sl],
                            start=(e == 0), stop=(e == ET - 1))
            mm2 = te.matmul(ps_st2[0:1, 0:G], st["ones_col_b"][:, :], sqb[:, sl],
                            start=(e == 0), stop=(e == ET - 1))
            if e == ET - 1:
                mm1.then_inc(*lnc.cinc(e))
                mm2.then_inc(*s_mm.inc(2))
            else:
                mm2.then_inc(*lnc.cinc(e))
    k_stat = s_mm.v

    @blk.scalar
    def _(s):
        s.wait_ge(s_mm.h, k_stat)
        s.activation(st["msq_row"][0:1, 0:G], ps_st[0:1, 0:G], AF.Square, scale=1.0 / E)
        s.drain()
        s.activation(st["m_row"][0:1, 0:G], ps_st[0:1, 0:G], AF.Copy, scale=1.0 / E)
        s.engine_nop().then_inc(*s_ev.inc())
    k_msq = s_ev.v

    @blk.vector
    def _(v):
        v.wait_ge(s_ev.h, k_msq)
        v.scalar_tensor_tensor(st["var_row"][0:1, 0:G], ps_st2[0:1, 0:G], 1.0 / E,
                               st["msq_row"][0:1, 0:G], ALU.mult, ALU.subtract)
        v.engine_nop().then_inc(*s_ev.inc())
    k_var = s_ev.v

    @blk.scalar
    def _(s):
        s.wait_ge(s_ev.h, k_var)
        s.activation(st["lnv_row"][0:1, 0:G], st["var_row"][0:1, 0:G], AF.Ln, bias=st["eps_t"][0:1, 0:1])
        s.drain()
        s.activation(st["r_row"][0:1, 0:G], st["lnv_row"][0:1, 0:G], AF.Exp, scale=-0.5)
        s.engine_nop().then_inc(*s_ev.inc())
    k_r = s_ev.v

    @blk.tensor
    def _(te):
        te.wait_ge(s_ev.h, k_r)
        mm3 = te.matmul(ps_mb[:, 0:G], st["ones_row_f"][0:1, :], st["m_row"][0:1, 0:G], start=True, stop=True)
        mm3 = te.matmul(ps_rb[:, 0:G], st["ones_row_f"][0:1, :], st["r_row"][0:1, 0:G], start=True, stop=True)
        if want_rcol:
            for tt in range(TPG):
                mm3 = te.matmul(ps_rc[:, tt:tt + 1], st["r_row"][0:1, tt * 128:(tt + 1) * 128],
                                st["one_f"][0:1, 0:1], start=True, stop=True)
        mm3.then_inc(*s_mm.inc())
    k_bc = s_mm.v

    @blk.vector
    def _(v):
        v.wait_ge(s_mm.h, k_bc)
        v.tensor_copy(rb_sb[:, 0:G], ps_rb[:, 0:G])
        if want_rcol:
            v.tensor_copy(rcol_sb[:, 0:TPG], ps_rc[:, 0:TPG])
        if xnc_sb is not None:
            for e in range(ET):
                v.tensor_tensor(xnc_sb[:, e * G:(e + 1) * G], xg[:, e * G:(e + 1) * G],
                                ps_mb[:, 0:G], ALU.subtract)
            v.drain()
        if x2n_out is not None:
            for e in range(ET):
                v.tensor_tensor(x2n_out[:, e * G:(e + 1) * G], xg[:, e * G:(e + 1) * G],
                                ps_mb[:, 0:G], ALU.subtract)
            v.drain()
            for e in range(ET):
                v.tensor_tensor(x2n_out[:, e * G:(e + 1) * G], x2n_out[:, e * G:(e + 1) * G],
                                rb_sb[:, 0:G], ALU.mult)
            v.drain()
        v.engine_nop().then_inc(*s_ev.inc())
    return s_ev.v


def make_stats(b):
    G = b.cfg["G"]
    return dict(
        m_row=b.sbuf("st_m", [1, G], F32),
        msq_row=b.sbuf("st_msq", [1, G], F32),
        var_row=b.sbuf("st_var", [1, G], F32),
        lnv_row=b.sbuf("st_lnv", [1, G], F32),
        r_row=b.sbuf("st_r", [1, G], F32),
        magic_col=b.sbuf("st_mag", [128, 1], F32),
    )


def emit_attn_a(b, st, src, dst, wa_bf, wp_bf, ident_f, maskT_b):
    cfg = b.cfg
    nc = b.nc
    E, T, G, H, D, ACH = cfg["E"], cfg["T"], cfg["G"], cfg["H"], cfg["D"], cfg["ACH"]
    ET, TPG, NG, CPG = E // 128, G // 128, T // G, G // ACH
    HT = H // 2
    assert ACH == 128 and TPG == CPG and D == 64

    sc = Scope(b)
    xg = [sc.sbuf(f"a_xg{i}", [128, ET * G], F32) for i in range(2)]
    ubuf = sc.sbuf("a_ubuf", [128, 2 * ET * G], BF16)
    lnb = sc.sbuf("a_lnb", [128, 4 * G], BF16)
    xnc = sc.sbuf("a_xnc", [128, ET * G], BF16)
    rb_sb = sc.sbuf("a_rb", [128, G], F32)
    rcol = sc.sbuf("a_rcol", [128, TPG], F32)
    wstr = sc.sbuf("a_wstr", [128, 2 * ET * 128], BF16)
    wvstr = sc.sbuf("a_wvstr", [128, 2 * ET * 512], BF16)
    f32a = sc.sbuf("a_f32a", [128, 2 * G], F32)
    f32b = sc.sbuf("a_f32b", [128, 2 * G], F32)
    f32c = sc.sbuf("a_f32c", [128, 2 * G], F32)
    vbuf = sc.sbuf("a_vbuf", [128, TPG * H * 65], BF16)
    kbuf = sc.sbuf("a_kbuf", [128, TPG * E], BF16)
    scT = sc.sbuf("a_scT", [128, H * ACH], BF16)
    den2 = sc.sbuf("a_den2", [128, H], F32)
    Ssb = sc.sbuf("a_Ssb", [128, HT * 65], BF16)
    ysb = sc.sbuf("a_ysb", [128, TPG * E], F32)
    yT = sc.sbuf("a_yT", [128, ET * G], BF16)
    resg = sc.sbuf("a_resg", [128, 2 * G], F32)
    bk = sc.banks("a_bk", 7)
    qkb = ubuf

    def num_slot(h):
        if h < 7:
            return bk[2], h * 65
        if h < 14:
            return bk[3], (h - 7) * 65
        return bk[4], (h - 14) * 65

    def S_slot(h):
        hp = h // 2
        bank = bk[5] if hp < 7 else bk[6]
        return bank, (hp % 7) * 65, (h % 2) * 64

    ld_x = [b.sem("g_ldxA"), b.sem("g_ldxB")]
    s_st = b.sem("g_stres")

    with nc.Block() as blk:
        @blk.vector
        def _(v):
            v.memset(bass.AP(vbuf, 64, [[TPG * H * 65, 128], [65, TPG * H], [1, 1]]), 1.0)
            v.memset(bk[5][:, :], 0.0)
            v.memset(bk[6][:, :], 0.0)

    for g in range(NG):
        with nc.Block() as blk:
            @blk.sync
            def _(s):
                if g == 0:
                    s.dma_start(out=bass.AP(xg[0], 0, [[ET * G, 128], [G, ET], [1, G]]),
                                in_=bass.AP(src, 0, [[T, 128], [128 * T, ET], [1, G]])).then_inc(*ld_x[0].inc(16))
                if g + 1 < NG:
                    s.dma_start(out=bass.AP(xg[(g + 1) % 2], 0, [[ET * G, 128], [G, ET], [1, G]]),
                                in_=bass.AP(src, (g + 1) * G, [[T, 128], [128 * T, ET], [1, G]])).then_inc(*ld_x[(g + 1) % 2].inc(16))
                s.wait_ge(*ld_x[g % 2].tail())
        cur = xg[g % 2]

        with nc.Block() as blk:
            emit_ln_group(b, blk, st, bk, cur, lnb[:, 0:2 * G], lnb[:, 2 * G:4 * G],
                          xnc, rb_sb, rcol, True)

        with nc.Block() as blk:
            wpipe = Pipe(b, "g_w", 2, pq=16, dma_p=True)
            mpipe = Pipe(b, "g_m", 2)
            spipe = Pipe(b, "g_s", 2)
            NQK = 2 * ET

            @blk.sync
            def _(s):
                for i in range(NQK):
                    s.wait_ge(mpipe.cs[0].h, mpipe.cb[0] + max(0, i - 1))
                    sec, f = (0, i) if i < ET else (1, i - ET)
                    s.dma_start(out=bass.AP(wstr, (i % 2) * ET * 128,
                                            [[2 * ET * 128, 128], [128, ET], [1, 128]]),
                                in_=bass.AP(wa_bf, sec * E + f * 128,
                                            [[3 * E, 128], [128 * 3 * E, ET], [1, 128]])).then_inc(*wpipe.pinc_part(i))

            @blk.tensor
            def _(te):
                for i in range(NQK):
                    wpipe.cwait(te, i)
                    mpipe.pwait(te, i)
                    buf = wstr[:, (i % 2) * ET * 128:(i % 2) * ET * 128 + ET * 128]
                    for e in range(ET):
                        mm = te.matmul(bk[i % 2][:, 0:G], buf[:, e * 128:(e + 1) * 128],
                                       xnc[:, e * G:(e + 1) * G],
                                       start=(e == 0), stop=(e == ET - 1))
                    mm.then_inc(*mpipe.pinc(i))

            @blk.scalar
            def _(s):
                for i in range(NQK):
                    spipe.cwait(s, i)
                    sl = slice((i % 2) * G, (i % 2) * G + G)
                    s.activation(f32c[:, sl], f32b[:, sl], AF.Exp).then_inc(*spipe.cinc(i))

            @blk.vector
            def _(v):
                for i in range(NQK + 1):
                    if i < NQK:
                        mpipe.cwait(v, i)
                        sl = slice((i % 2) * G, (i % 2) * G + G)
                        v.tensor_tensor(f32a[:, sl], bk[i % 2][:, 0:G], rb_sb[:, 0:G], ALU.mult)
                        v.engine_nop().then_inc(*mpipe.cinc(i))
                        v.drain()
                        v.tensor_single_scalar(f32b[:, sl], f32a[:, sl], 0.0, ALU.min)
                        v.engine_nop().then_inc(*spipe.pinc(i))
                    if i >= 1:
                        j = i - 1
                        v.wait_ge(spipe.cs[0].h, spipe.cb[0] + j + 1)
                        sj = slice((j % 2) * G, (j % 2) * G + G)
                        v.scalar_tensor_tensor(qkb[:, j * G:(j + 1) * G], f32a[:, sj], 0.0,
                                               f32c[:, sj], ALU.max, ALU.add)
                        v.drain()

        HW_ = min(512, E)
        NH = E // HW_
        with nc.Block() as blk:
            wv = Pipe(b, "g_wv", 2, pq=16, dma_p=True)
            mv = Pipe(b, "g_mv", 1)
            kp = Pipe(b, "g_kp", 2)
            ke = Pipe(b, "g_ke", 2)

            @blk.sync
            def _(s):
                for r in range(2 * NH):
                    sec = 2 if r < NH else 1
                    half = r % NH
                    s.wait_ge(mv.cs[0].h, mv.cb[0] + max(0, r - 1))
                    s.dma_start(out=bass.AP(wvstr, (r % 2) * ET * 512,
                                            [[2 * ET * 512, 128], [512, ET], [1, HW_]]),
                                in_=bass.AP(wa_bf, sec * E + half * HW_,
                                            [[3 * E, 128], [128 * 3 * E, ET], [1, HW_]])).then_inc(*wv.pinc_part(r))

            @blk.tensor
            def _(te):
                for r in range(2 * NH):
                    mv.pwait(te, r)
                    wv.cwait(te, r)
                    for e in range(ET):
                        for tt in range(TPG):
                            mm = te.matmul(bk[tt][:, 0:HW_],
                                           xnc[:, e * G + tt * 128:e * G + (tt + 1) * 128],
                                           wvstr[:, ((r % 2) * ET + e) * 512:((r % 2) * ET + e) * 512 + HW_],
                                           start=(e == 0), stop=(e == ET - 1))
                    mm.then_inc(*mv.pinc(r))

            @blk.scalar
            def _(s):
                NJ = NH * TPG
                jx = 0
                for r in range(2 * NH):
                    mv.cwait(s, r)
                    half = r % NH
                    hph = HW_ // 64
                    for tt in range(TPG):
                        if r < NH:
                            out_ap = bass.AP(vbuf, tt * H * 65 + half * hph * 65,
                                             [[TPG * H * 65, 128], [65, hph], [1, 64]])
                            s.activation(out_ap, bk[tt][:, 0:HW_], AF.Copy, scale=rcol[:, tt:tt + 1])
                        else:
                            j = (r - NH) * TPG + tt
                            kp.pwait(s, j)
                            s.activation(f32a[:, (j % 2) * G:(j % 2) * G + HW_],
                                         bk[tt][:, 0:HW_], AF.Copy, scale=rcol[:, tt:tt + 1])
                            s.engine_nop().then_inc(*kp.pinc(j))
                            if j >= 1:
                                ke.cwait(s, j - 1)
                                slx = slice(((j - 1) % 2) * G, ((j - 1) % 2) * G + HW_)
                                s.activation(f32c[:, slx], f32b[:, slx], AF.Exp).then_inc(*ke.cinc(j - 1))
                    s.engine_nop().then_inc(*mv.cinc(r))
                ke.cwait(s, NJ - 1)
                slx = slice(((NJ - 1) % 2) * G, ((NJ - 1) % 2) * G + HW_)
                s.activation(f32c[:, slx], f32b[:, slx], AF.Exp).then_inc(*ke.cinc(NJ - 1))

            @blk.vector
            def _(v):
                for j in range(NH * TPG + 1):
                    if j < NH * TPG:
                        kp.cwait(v, j)
                        ke.pwait(v, j)
                        sl = slice((j % 2) * G, (j % 2) * G + HW_)
                        v.tensor_single_scalar(f32b[:, sl], f32a[:, sl], 0.0, ALU.min)
                        v.engine_nop().then_inc(*ke.pinc(j))
                    if j >= 1:
                        jj = j - 1
                        v.wait_ge(ke.cs[0].h, ke.cb[0] + jj + 1)
                        sjl = slice((jj % 2) * G, (jj % 2) * G + HW_)
                        half2, tt2 = divmod(jj, TPG)
                        v.scalar_tensor_tensor(
                            kbuf[:, tt2 * E + half2 * HW_:tt2 * E + (half2 + 1) * HW_],
                            f32a[:, sjl], 0.0, f32c[:, sjl], ALU.max, ALU.add)
                        v.drain()
                        v.engine_nop().then_inc(*kp.cinc(jj))

        with nc.Block() as blk:
            scp = Pipe(b, "g_scp", 1)
            nmp = Pipe(b, "g_nmp", 1)
            sup = Pipe(b, "g_sup", 1)
            msk = Pipe(b, "g_msk", 1)
            dnp = Pipe(b, "g_dn", 1)

            @blk.tensor
            def _(te):
                for c in range(CPG):
                    gc = g * CPG + c
                    for hp in range(HT):
                        i = c * HT + hp
                        scp.pwait(te, i)
                        for par in range(2):
                            kT = qkb[64 * par:64 * par + 64,
                                     (ET + hp) * G + c * 128:(ET + hp) * G + (c + 1) * 128]
                            qT = qkb[64 * par:64 * par + 64,
                                     hp * G + c * 128:hp * G + (c + 1) * 128]
                            mm = te.matmul(bk[par][:, 0:128], kT, qT, start=True, stop=True,
                                           tile_position=(64 * par, 0))
                        mm.then_inc(*scp.pinc(i))
                    msk.cwait(te, c)
                    nmp.pwait(te, c)
                    te.wait_ge(sup.cs[0].h, sup.cb[0] + c)
                    for h in range(H):
                        bankn, off = num_slot(h)
                        mmn = te.matmul(bankn[:, off:off + 65],
                                        scT[:, h * 128:(h + 1) * 128],
                                        vbuf[:, c * H * 65 + h * 65:c * H * 65 + (h + 1) * 65],
                                        start=True, stop=(gc == 0))
                        if gc > 0:
                            par = h % 2
                            mmn = te.matmul(bankn[:, off:off + 65],
                                            qkb[64 * par:64 * par + 64,
                                                (h // 2) * G + c * 128:(h // 2) * G + (c + 1) * 128],
                                            Ssb[64 * par:64 * par + 64,
                                                (h // 2) * 65:(h // 2) * 65 + 65],
                                            start=False, stop=True, tile_position=(64 * par, 0))
                    mmn.then_inc(*nmp.pinc(c))
                    sup.pwait(te, c)
                    for h in range(H):
                        bs, off, pbase = S_slot(h)
                        mms = te.matmul(bs[pbase:pbase + 64, off:off + 65],
                                        kbuf[:, c * E + h * 64:c * E + (h + 1) * 64],
                                        vbuf[:, c * H * 65 + h * 65:c * H * 65 + (h + 1) * 65],
                                        start=False, stop=(gc == NG * CPG - 1),
                                        skip_group_check=True, tile_position=(0, pbase))
                    mms.then_inc(*sup.pinc(c))

            @blk.vector
            def _(v):
                for c in range(CPG):
                    v.wait_ge(nmp.ps[0].h, nmp.pb[0] + c)
                    for hp in range(HT):
                        i = c * HT + hp
                        scp.cwait(v, i)
                        for par in range(2):
                            h = 2 * hp + par
                            v.tensor_tensor(scT[:, h * 128:(h + 1) * 128], bk[par][:, 0:128],
                                            maskT_b[:, :], ALU.mult)
                        v.engine_nop().then_inc(*scp.cinc(i))
                    v.engine_nop().then_inc(*msk.pinc(c))
                    nmp.cwait(v, c)
                    dnp.pwait(v, c)
                    nh = min(H, 7)
                    v.tensor_single_scalar(den2[:, 0:nh], bass.AP(bk[2], 64, [[512, 128], [65, nh]]),
                                           1e-6, ALU.add)
                    if H > 7:
                        v.tensor_single_scalar(den2[:, 7:14], bass.AP(bk[3], 64, [[512, 128], [65, 7]]),
                                               1e-6, ALU.add)
                    if H > 14:
                        v.tensor_single_scalar(den2[:, 14:16], bass.AP(bk[4], 64, [[512, 128], [65, 2]]),
                                               1e-6, ALU.add)
                    v.drain()
                    v.reciprocal(den2[:, 0:H], den2[:, 0:H])
                    v.drain()
                    v.engine_nop().then_inc(*dnp.pinc(c))

            @blk.scalar
            def _(s):
                for c in range(CPG):
                    dnp.cwait(s, c)
                    for h in range(H):
                        bankn, off = num_slot(h)
                        s.activation(ysb[:, c * E + h * 64:c * E + (h + 1) * 64],
                                     bankn[:, off:off + 64], AF.Copy, scale=den2[:, h:h + 1])
                    s.engine_nop().then_inc(*dnp.cinc(c))
                    s.engine_nop().then_inc(*nmp.cinc(c))
                    sup.cwait(s, c)
                    s.activation(Ssb[:, 0:min(HT, 7) * 65], bk[5][:, 0:min(HT, 7) * 65], AF.Copy)
                    if HT > 7:
                        s.activation(Ssb[:, 455:455 + (HT - 7) * 65], bk[6][:, 0:(HT - 7) * 65], AF.Copy)
                    s.engine_nop().then_inc(*sup.cinc(c))

        with nc.Block() as blk:
            tp = Pipe(b, "g_tp", 2)
            pj = Pipe(b, "g_pj", 2)
            rs = Pipe(b, "g_rs", 2, cq=16, dma_c=True)
            wp2 = Pipe(b, "g_w2", 2, pq=16, dma_p=True)

            @blk.sync
            def _(s):
                for eo in range(ET):
                    s.wait_ge(pj.cs[0].h, pj.cb[0] + max(0, eo - 1))
                    s.dma_start(out=bass.AP(wstr, (eo % 2) * ET * 128,
                                            [[2 * ET * 128, 128], [128, ET], [1, 128]]),
                                in_=bass.AP(wp_bf, eo * 128,
                                            [[E, 128], [128 * E, ET], [1, 128]])).then_inc(*wp2.pinc_part(eo))

            @blk.tensor
            def _(te):
                for i in range(TPG * ET):
                    tt, e = divmod(i, ET)
                    tp.pwait(te, i)
                    te.transpose(bk[i % 2][:, 0:128],
                                 ysb[:, tt * E + e * 128:tt * E + (e + 1) * 128],
                                 ident_f[:, :]).then_inc(*tp.pinc(i))
                te.wait_ge(tp.cs[0].h, tp.cb[0] + TPG * ET)
                for eo in range(ET):
                    wp2.cwait(te, eo)
                    pj.pwait(te, eo)
                    buf = wstr[:, (eo % 2) * ET * 128:(eo % 2) * ET * 128 + ET * 128]
                    for e in range(ET):
                        mm = te.matmul(bk[eo % 2][:, 0:G], buf[:, e * 128:(e + 1) * 128],
                                       yT[:, e * G:(e + 1) * G],
                                       start=(e == 0), stop=(e == ET - 1))
                    mm.then_inc(*pj.pinc(eo))

            @blk.vector
            def _(v):
                for i in range(TPG * ET):
                    tt, e = divmod(i, ET)
                    tp.cwait(v, i)
                    v.tensor_copy(yT[:, e * G + tt * 128:e * G + (tt + 1) * 128],
                                  bk[i % 2][:, 0:128]).then_inc(*tp.cinc(i))
                for eo in range(ET):
                    pj.cwait(v, eo)
                    rs.pwait(v, eo)
                    v.tensor_tensor(resg[:, (eo % 2) * G:(eo % 2) * G + G], bk[eo % 2][:, 0:G],
                                    cur[:, eo * G:(eo + 1) * G], ALU.add)
                    v.engine_nop().then_inc(*pj.cinc(eo))
                    v.engine_nop().then_inc(*rs.pinc(eo))

            @blk.gpsimd
            def _(gp):
                for eo in range(ET):
                    rs.cwait(gp, eo)
                    gp.dma_start(out=dst[eo * 128:(eo + 1) * 128, g * G:(g + 1) * G],
                                 in_=resg[:, (eo % 2) * G:(eo % 2) * G + G]).then_inc(*rs.cinc(eo))

    with nc.Block() as blk:
        @blk.sync
        def _(s):
            for sm in b.sem("g_rs_c0"), b.sem("g_rs_c1"):
                s.wait_ge(*sm.tail())
    sc.close()


def emit_allreduce_max(b, maxv, nv, grow, bank, ident_f, cc_in, cc_out, replica, tag):
    nc = b.nc
    red = b.sem("g_r")
    dm = b.sem("ar_dm")
    with nc.Block() as blk:
        @blk.tensor
        def _(te):
            te.matmul(bank[0:nv, 0:128], maxv[:, 0:nv], ident_f[:, :], start=True,
                      stop=True).then_inc(*red.inc())
        k_tr = red.v

        @blk.vector
        def _(v):
            v.wait_ge(red.h, k_tr)
            v.tensor_reduce(maxv[0:nv, nv:nv + 1], bank[0:nv, 0:128], AX.X, ALU.max)
            v.engine_nop().then_inc(*red.inc())
        k_red = red.v

        @blk.gpsimd
        def _(g):
            ccs = b.sem("ar_cc")
            g.wait_ge(red.h, k_red)
            g.dma_start(out=cc_in[0:1, 0:nv], in_=maxv[0:nv, nv:nv + 1]).then_inc(*dm.inc(16))
            g.wait_ge(*dm.tail())
            g.collective_compute("AllReduce", ALU.max, replica_groups=replica,
                                 ins=[cc_in[0:1, 0:nv]], outs=[cc_out[0:1, 0:nv]]).then_inc(*ccs.inc())
            g.wait_ge(*ccs.tail())
            g.dma_start(out=grow[0:1, 0:nv], in_=cc_out[0:1, 0:nv]).then_inc(*dm.inc(16))
            g.wait_ge(*dm.tail())


def emit_scales(b, st, grow, nv, scal, c0, bank, tag):
    nc = b.nc
    sv = b.sem("g_r")
    with nc.Block() as blk:
        @blk.vector
        def _(v):
            v.tensor_scalar(grow[0:1, 0:nv], grow[0:1, 0:nv], 1.0 / 32767.0, 1e-12,
                            ALU.mult, ALU.max)
            v.drain()
            v.reciprocal(grow[0:1, nv:2 * nv], grow[0:1, 0:nv])
            v.engine_nop().then_inc(*sv.inc())
        k_v = sv.v

        @blk.tensor
        def _(te):
            te.wait_ge(sv.h, k_v)
            te.matmul(bank[:, 0:2 * nv], st["ones_row_f"][0:1, :], grow[0:1, 0:2 * nv],
                      start=True, stop=True).then_inc(*sv.inc())
        k_t = sv.v

        @blk.vector
        def _(v):
            v.wait_ge(sv.h, k_t)
            for k in range(nv):
                v.tensor_copy(scal[:, c0 + 2 * k:c0 + 2 * k + 1], bank[:, k:k + 1])
                v.tensor_copy(scal[:, c0 + 2 * k + 1:c0 + 2 * k + 2], bank[:, nv + k:nv + k + 1])
            v.drain()


def emit_ffn(b, st, src, dst, wu_bf, wg_bf, wd_bf, sp_x2n, sp_up, sp_gate, sp_t,
             sp_opre, scal, cc_in, cc_out, ident_f, replica, tag):
    cfg = b.cfg
    nc = b.nc
    E, T, G, I = cfg["E"], cfg["T"], cfg["G"], cfg["I"]
    ET, IT, NG, TPG = E // 128, I // 128, T // G, G // 128
    CH = min(512, T)
    NCH = T // CH

    N = IT * NCH
    ITG = IT * G

    sc = Scope(b)
    xg = [sc.sbuf(f"{tag}_xg{i}", [128, ET * G], F32) for i in range(2)]
    maxb = sc.sbuf(f"{tag}_maxb", [128, 2 * IT * NG], F32)
    maxv = sc.sbuf(f"{tag}_maxv", [128, 8], F32)
    grow = sc.sbuf(f"{tag}_grow", [1, 8], F32)
    resg = sc.sbuf(f"{tag}_resg", [128, 2 * G], F32)
    bk = sc.banks(f"{tag}_bk", 7)

    sc1 = Scope(b)
    lnb = sc1.sbuf(f"{tag}_lnb", [128, 4 * G], BF16)
    rb_sb = sc1.sbuf(f"{tag}_rb", [128, G], F32)
    x2ng = sc1.sbuf(f"{tag}_x2n", [128, ET * G], F32)
    iq = sc1.sbuf(f"{tag}_iq", [128, ET * G], I16)
    xq = sc1.sbuf(f"{tag}_xq", [128, ET * T], BF16)
    wstr = sc1.sbuf(f"{tag}_wstr", [128, 2 * ET * 128], BF16)
    spst = sc1.sbuf(f"{tag}_spst", [128, 4 * 512], BF16)

    ld_x = [b.sem("g_ldxA"), b.sem("g_ldxB")]
    st_sp = [b.sem("g_stspA"), b.sem("g_stspB")]
    s_st = b.sem("g_stres")

    for g in range(NG):
        with nc.Block() as blk:
            @blk.sync
            def _(s):
                if g == 0:
                    s.dma_start(out=bass.AP(xg[0], 0, [[ET * G, 128], [G, ET], [1, G]]),
                                in_=bass.AP(src, 0, [[T, 128], [128 * T, ET], [1, G]])).then_inc(*ld_x[0].inc(16))
                if g + 1 < NG:
                    s.dma_start(out=bass.AP(xg[(g + 1) % 2], 0, [[ET * G, 128], [G, ET], [1, G]]),
                                in_=bass.AP(src, (g + 1) * G, [[T, 128], [128 * T, ET], [1, G]])).then_inc(*ld_x[(g + 1) % 2].inc(16))
                s.wait_ge(*ld_x[g % 2].tail())
        cur = xg[g % 2]
        x2c = x2ng[:, 0:ET * G]
        spill_guard = (st_sp[0].v, st_sp[1].v)
        with nc.Block() as blk:
            @blk.vector
            def _(v):
                v.wait_ge(st_sp[0].h, spill_guard[0])
                v.wait_ge(st_sp[1].h, spill_guard[1])
            emit_ln_group(b, blk, st, bk, cur, lnb[:, 0:2 * G], lnb[:, 2 * G:4 * G],
                          None, rb_sb, None, False, x2n_out=x2c)
            s_r = b.sem("g_r")

            @blk.vector
            def _(v):
                for e in range(ET):
                    v.tensor_reduce(maxv[:, 0:1] if False else maxb[:, g * ET + e:g * ET + e + 1],
                                    x2c[:, e * G:(e + 1) * G], AX.X, ALU.max,
                                    apply_absolute_value=True)
                v.engine_nop().then_inc(*s_r.inc())
            k_r = s_r.v

            @blk.gpsimd
            def _(gp):
                gp.wait_ge(s_r.h, k_r)
                gp.dma_start(out=bass.AP(sp_x2n, g * G, [[T, 128], [128 * T, ET], [1, G]]),
                             in_=bass.AP(x2ng, 0, [[ET * G, 128], [G, ET], [1, G]])).then_inc(*st_sp[g % 2].inc(16))
    with nc.Block() as blk:
        @blk.vector
        def _(v):
            v.tensor_reduce(maxv[:, 0:1], maxb[:, 0:NG * ET], AX.X, ALU.max)
            v.drain()
    emit_allreduce_max(b, maxv, 1, grow, bk[5], ident_f, cc_in, cc_out, replica, tag)
    emit_scales(b, st, grow, 1, scal, 0, bk[6], tag)

    with nc.Block() as blk:
        lp = Pipe(b, "g_lsp", 1, pq=16, dma_p=True)
        qp = Pipe(b, "g_qp", 1)

        @blk.sync
        def _(s):
            s.wait_ge(*st_sp[0].tail())
            s.wait_ge(*st_sp[1].tail())
            for g in range(NG):
                lp.pwait(s, g)
                s.dma_start(out=bass.AP(x2ng, 0, [[ET * G, 128], [G, ET], [1, G]]),
                            in_=bass.AP(sp_x2n, g * G, [[T, 128], [128 * T, ET], [1, G]])).then_inc(*lp.pinc(g))

        @blk.vector
        def _(v):
            for g in range(NG):
                lp.cwait(v, g)
                qp.pwait(v, g)
                v.tensor_scalar(x2ng[:, :], x2ng[:, :], scal[:, 1:2], MAGIC, ALU.mult, ALU.add)
                v.drain()
                v.tensor_single_scalar(iq[:, :], x2ng[:, :], MAGIC, ALU.subtract)
                v.engine_nop().then_inc(*qp.pinc(g))
                v.engine_nop().then_inc(*lp.cinc(g))

        @blk.scalar
        def _(s):
            for g in range(NG):
                qp.cwait(s, g)
                for e in range(ET):
                    s.activation(xq[:, e * T + g * G:e * T + (g + 1) * G],
                                 iq[:, e * G:(e + 1) * G], AF.Copy, scale=scal[:, 0:1])
                s.engine_nop().then_inc(*qp.cinc(g))

    with nc.Block() as blk:
        wp = Pipe(b, "g_w", 2, pq=16, dma_p=True)
        mp = Pipe(b, "g_mp", 4)
        ep = Pipe(b, "g_ep", 4, cq=16, dma_c=True)

        @blk.sync
        def _(s):
            for ft in range(2 * IT):
                s.wait_ge(mp.cs[0].h, mp.cb[0] + max(0, ft - 1) * NG)
                wsrc = wu_bf if ft < IT else wg_bf
                f = ft % IT
                s.dma_start(out=bass.AP(wstr, (ft % 2) * ET * 128,
                                        [[2 * ET * 128, 128], [128, ET], [1, 128]]),
                            in_=bass.AP(wsrc, f * 128,
                                        [[I, 128], [128 * I, ET], [1, 128]])).then_inc(*wp.pinc_part(ft))

        @blk.tensor
        def _(te):
            for ft in range(2 * IT):
                wp.cwait(te, ft)
                buf = wstr[:, (ft % 2) * ET * 128:(ft % 2) * ET * 128 + ET * 128]
                for g in range(NG):
                    i = ft * NG + g
                    mp.pwait(te, i)
                    for e in range(ET):
                        mm = te.matmul(bk[i % 4][:, 0:G], buf[:, e * 128:(e + 1) * 128],
                                       xq[:, e * T + g * G:e * T + (g + 1) * G],
                                       start=(e == 0), stop=(e == ET - 1))
                    mm.then_inc(*mp.pinc(i))

        @blk.scalar
        def _(s):
            srd = b.sem("g_rd")
            for i in range(2 * IT * NG):
                mp.cwait(s, i)
                ep.pwait(s, i)
                s.activation(spst[:, (i % 4) * 512:(i % 4) * 512 + G], bk[i % 4][:, 0:G],
                             AF.Copy)
                s.engine_nop().then_inc(*mp.cinc(i))
                s.engine_nop().then_inc(*srd.inc())

        @blk.vector
        def _(v):
            srd = b.sem("g_rd")
            base = srd.v - 2 * IT * NG
            for i in range(2 * IT * NG):
                v.wait_ge(srd.h, base + i + 1)
                v.tensor_reduce(maxb[:, i:i + 1], spst[:, (i % 4) * 512:(i % 4) * 512 + G],
                                AX.X, ALU.max, apply_absolute_value=True)
                v.engine_nop().then_inc(*ep.pinc(i))

        @blk.gpsimd
        def _(gp):
            for i in range(2 * IT * NG):
                ft, g = divmod(i, NG)
                spdst = sp_up if ft < IT else sp_gate
                f = ft % IT
                ep.cwait(gp, i)
                gp.dma_start(out=spdst[f * 128:(f + 1) * 128, g * G:(g + 1) * G],
                             in_=spst[:, (i % 4) * 512:(i % 4) * 512 + G]).then_inc(*ep.cinc(i))

    with nc.Block() as blk:
        @blk.vector
        def _(v):
            v.tensor_reduce(maxv[:, 0:1], maxb[:, 0:IT * NG], AX.X, ALU.max)
            v.tensor_reduce(maxv[:, 1:2], maxb[:, IT * NG:2 * IT * NG], AX.X, ALU.max)
            v.drain()
    emit_allreduce_max(b, maxv, 2, grow, bk[5], ident_f, cc_in, cc_out, replica, tag)
    emit_scales(b, st, grow, 2, scal, 2, bk[6], tag)
    with nc.Block() as blk:
        @blk.vector
        def _(v):
            v.tensor_single_scalar(scal[:, 10:11], scal[:, 4:5], 1.0 / 65536.0, ALU.mult)
            v.drain()

    sc1.close()
    sc2 = Scope(b)
    wstrD = sc2.sbuf(f"{tag}_wd", [128, ET * IT * 128], BF16)
    oT2 = sc2.sbuf(f"{tag}_oT", [128, 2 * ITG], BF16)
    fb1 = sc2.sbuf(f"{tag}_fb1", [128, 2 * CH], F32)
    fb2 = sc2.sbuf(f"{tag}_fb2", [128, 2 * CH], F32)
    fc1 = sc2.sbuf(f"{tag}_fc1", [128, 2 * CH], F32)
    prb = sc2.sbuf(f"{tag}_pr", [128, 2 * CH], F32)
    ib1 = sc2.sbuf(f"{tag}_ib1", [128, 4 * CH], I16)
    ib2 = sc2.sbuf(f"{tag}_ib2", [128, 2 * CH], I16)
    ub1 = sc2.sbuf(f"{tag}_ub1", [128, 2 * CH], U16)
    bb1 = sc2.sbuf(f"{tag}_bb1", [128, 3 * CH], BF16)
    gld = sc2.sbuf(f"{tag}_gld", [128, 3 * CH], BF16)
    uld = sc2.sbuf(f"{tag}_uld", [128, 3 * CH], BF16)
    magic = st["magic_col"]

    with nc.Block() as blk:
        cs = [b.sem(f"g_c{k}") for k in range(8)]
        cb = [x.v for x in cs]
        ld = [b.sem(n) for n in ("g_ldxA", "g_ldxB", "g_ldx")]
        ldb = [x.v for x in ld]
        sw = [b.sem(n) for n in ("g_ep_c0", "g_ep_c1", "g_ep_c2")]
        swb = [x.v for x in sw]

        @blk.sync
        def _(s):
            for k in range(4):
                s.wait_ge(*b.sem(f"g_ep_c{k}").tail())
            for i in range(N):
                it, ch = divmod(i, NCH)
                if i >= 3:
                    s.wait_ge(cs[0].h, cb[0] + i - 2)
                s.dma_start(out=gld[:, (i % 3) * CH:(i % 3) * CH + CH],
                            in_=sp_gate[it * 128:(it + 1) * 128, ch * CH:(ch + 1) * CH]).then_inc(*ld[i % 3].inc(16))

        @blk.scalar
        def _(s):
            def S1(i):
                s.wait_ge(ld[i % 3].h, ldb[i % 3] + (i // 3 + 1) * 16)
                if i >= 2:
                    s.wait_ge(cs[1].h, cb[1] + i - 1)
                s.activation(fb1[:, (i % 2) * CH:(i % 2) * CH + CH],
                             gld[:, (i % 3) * CH:(i % 3) * CH + CH],
                             AF.Identity, bias=magic[:, 0:1], scale=scal[:, 5:6]).then_inc(*cs[0].inc())
            def S3(j):
                s.wait_ge(cs[1].h, cb[1] + j + 1)
                if j >= 2:
                    s.wait_ge(cs[3].h, cb[3] + j - 1)
                s.activation(fb2[:, (j % 2) * CH:(j % 2) * CH + CH],
                             ib1[:, (j % 4) * CH:(j % 4) * CH + CH],
                             AF.Sigmoid, scale=scal[:, 4:5]).then_inc(*cs[2].inc())
            for i in range(N + 1):
                if i < N:
                    S1(i)
                if 0 <= i - 1 < N:
                    S3(i - 1)

        @blk.vector
        def _(v):
            def st2(i):
                v.wait_ge(cs[0].h, cb[0] + i + 1)
                if i >= 4:
                    v.wait_ge(cs[5].h, cb[5] + i - 3)
                v.tensor_single_scalar(ib1[:, (i % 4) * CH:(i % 4) * CH + CH],
                                       fb1[:, (i % 2) * CH:(i % 2) * CH + CH],
                                       MAGIC, ALU.subtract).then_inc(*cs[1].inc())
            def st4(j):
                v.wait_ge(cs[2].h, cb[2] + j + 1)
                if j >= 2:
                    v.wait_ge(cs[4].h, cb[4] + j - 1)
                v.tensor_scalar(fc1[:, (j % 2) * CH:(j % 2) * CH + CH],
                                fb2[:, (j % 2) * CH:(j % 2) * CH + CH],
                                65536.0, MAGIC, ALU.mult, ALU.add).then_inc(*cs[3].inc())
            def st5(k):
                v.wait_ge(cs[3].h, cb[3] + k + 1)
                if k >= 2:
                    v.wait_ge(cs[5].h, cb[5] + k - 1)
                v.tensor_scalar(ub1[:, (k % 2) * CH:(k % 2) * CH + CH],
                                fc1[:, (k % 2) * CH:(k % 2) * CH + CH],
                                MAGIC, 65535.0, ALU.subtract, ALU.min).then_inc(*cs[4].inc())
            def st6(l):
                v.wait_ge(cs[4].h, cb[4] + l + 1)
                if l >= 2:
                    v.wait_ge(cs[6].h, cb[6] + l - 1)
                v.tensor_tensor(prb[:, (l % 2) * CH:(l % 2) * CH + CH],
                                ib1[:, (l % 4) * CH:(l % 4) * CH + CH],
                                ub1[:, (l % 2) * CH:(l % 2) * CH + CH], ALU.mult).then_inc(*cs[5].inc())
            def st7(m):
                v.wait_ge(cs[5].h, cb[5] + m + 1)
                if m >= 3:
                    v.wait_ge(cs[7].h, cb[7] + m - 2)
                    v.wait_ge(sw[m % 3].h, swb[m % 3] + (m // 3) * 16)
                v.tensor_scalar(bb1[:, (m % 3) * CH:(m % 3) * CH + CH],
                                prb[:, (m % 2) * CH:(m % 2) * CH + CH],
                                scal[:, 10:11], None, ALU.mult).then_inc(*cs[6].inc())
            def st8(n):
                v.wait_ge(cs[6].h, cb[6] + n + 1)
                v.tensor_reduce(maxb[:, n:n + 1], bb1[:, (n % 3) * CH:(n % 3) * CH + CH],
                                AX.X, ALU.max, apply_absolute_value=True).then_inc(*cs[7].inc())
            for i in range(N + 5):
                if i < N:
                    st2(i)
                if 0 <= i - 1 < N:
                    st4(i - 1)
                if 0 <= i - 2 < N:
                    st5(i - 2)
                if 0 <= i - 3 < N:
                    st6(i - 3)
                if 0 <= i - 4 < N:
                    st7(i - 4)
                if 0 <= i - 5 < N:
                    st8(i - 5)

        @blk.gpsimd
        def _(gp):
            for i in range(N):
                it, ch = divmod(i, NCH)
                gp.wait_ge(cs[6].h, cb[6] + i + 1)
                gp.dma_start(out=sp_t[it * 128:(it + 1) * 128, ch * CH:(ch + 1) * CH],
                             in_=bb1[:, (i % 3) * CH:(i % 3) * CH + CH]).then_inc(*sw[i % 3].inc(16))

    with nc.Block() as blk:
        @blk.vector
        def _(v):
            v.tensor_reduce(maxv[:, 0:1], maxb[:, 0:IT * NCH], AX.X, ALU.max)
            v.drain()
    emit_allreduce_max(b, maxv, 1, grow, bk[5], ident_f, cc_in, cc_out, replica, tag)
    emit_scales(b, st, grow, 1, scal, 6, bk[6], tag)
    with nc.Block() as blk:
        @blk.vector
        def _(v):
            v.tensor_tensor(scal[:, 11:12], scal[:, 6:7], scal[:, 2:3], ALU.mult)
            v.drain()

    with nc.Block() as blk:
        cs = [b.sem(f"g_c{k}") for k in range(8)]
        cb = [x.v for x in cs]
        ld = [b.sem(n) for n in ("g_ldxA", "g_ldxB", "g_ldx")]
        ldb = [x.v for x in ld]
        lu = [b.sem(n) for n in ("g_w_p0", "g_w_p1", "g_w2_p0")]
        lub = [x.v for x in lu]
        sw = [b.sem(n) for n in ("g_ep_c0", "g_ep_c1", "g_ep_c2")]
        swb = [x.v for x in sw]

        @blk.sync
        def _(s):
            for k in range(3):
                s.wait_ge(sw[k].h, swb[k])
            for i in range(N):
                it, ch = divmod(i, NCH)
                if i >= 3:
                    s.wait_ge(cs[0].h, cb[0] + i - 2)
                s.dma_start(out=gld[:, (i % 3) * CH:(i % 3) * CH + CH],
                            in_=sp_t[it * 128:(it + 1) * 128, ch * CH:(ch + 1) * CH]).then_inc(*ld[i % 3].inc(16))
                if i >= 3:
                    s.wait_ge(cs[2].h, cb[2] + i - 2)
                s.dma_start(out=uld[:, (i % 3) * CH:(i % 3) * CH + CH],
                            in_=sp_up[it * 128:(it + 1) * 128, ch * CH:(ch + 1) * CH]).then_inc(*lu[i % 3].inc(16))

        @blk.scalar
        def _(s):
            for i in range(N):
                s.wait_ge(ld[i % 3].h, ldb[i % 3] + (i // 3 + 1) * 16)
                if i >= 2:
                    s.wait_ge(cs[1].h, cb[1] + i - 1)
                s.activation(fb1[:, (i % 2) * CH:(i % 2) * CH + CH],
                             gld[:, (i % 3) * CH:(i % 3) * CH + CH],
                             AF.Identity, bias=magic[:, 0:1], scale=scal[:, 7:8]).then_inc(*cs[0].inc())
                s.wait_ge(lu[i % 3].h, lub[i % 3] + (i // 3 + 1) * 16)
                if i >= 2:
                    s.wait_ge(cs[3].h, cb[3] + i - 1)
                s.activation(fb2[:, (i % 2) * CH:(i % 2) * CH + CH],
                             uld[:, (i % 3) * CH:(i % 3) * CH + CH],
                             AF.Identity, bias=magic[:, 0:1], scale=scal[:, 3:4]).then_inc(*cs[2].inc())

        @blk.vector
        def _(v):
            def st2(i):
                v.wait_ge(cs[0].h, cb[0] + i + 1)
                if i >= 2:
                    v.wait_ge(cs[4].h, cb[4] + i - 1)
                v.tensor_single_scalar(ib1[:, (i % 2) * CH:(i % 2) * CH + CH],
                                       fb1[:, (i % 2) * CH:(i % 2) * CH + CH],
                                       MAGIC, ALU.subtract).then_inc(*cs[1].inc())
            def st4(i):
                v.wait_ge(cs[2].h, cb[2] + i + 1)
                if i >= 2:
                    v.wait_ge(cs[4].h, cb[4] + i - 1)
                v.tensor_single_scalar(ib2[:, (i % 2) * CH:(i % 2) * CH + CH],
                                       fb2[:, (i % 2) * CH:(i % 2) * CH + CH],
                                       MAGIC, ALU.subtract).then_inc(*cs[3].inc())
            def st6(k):
                v.wait_ge(cs[4].h, cb[4] + k + 1)
                if k >= 3:
                    v.wait_ge(cs[6].h, cb[6] + k - 2)
                    v.wait_ge(sw[k % 3].h, swb[k % 3] + (k // 3) * 16)
                v.tensor_scalar(bb1[:, (k % 3) * CH:(k % 3) * CH + CH],
                                prb[:, (k % 2) * CH:(k % 2) * CH + CH],
                                scal[:, 11:12], None, ALU.mult).then_inc(*cs[5].inc())
            def st7(l):
                v.wait_ge(cs[5].h, cb[5] + l + 1)
                v.tensor_reduce(maxb[:, l:l + 1], bb1[:, (l % 3) * CH:(l % 3) * CH + CH],
                                AX.X, ALU.max, apply_absolute_value=True).then_inc(*cs[6].inc())
            def st5(j):
                v.wait_ge(cs[1].h, cb[1] + j + 1)
                v.wait_ge(cs[3].h, cb[3] + j + 1)
                if j >= 2:
                    v.wait_ge(cs[5].h, cb[5] + j - 1)
                v.tensor_tensor(prb[:, (j % 2) * CH:(j % 2) * CH + CH],
                                ib1[:, (j % 2) * CH:(j % 2) * CH + CH],
                                ib2[:, (j % 2) * CH:(j % 2) * CH + CH], ALU.mult).then_inc(*cs[4].inc())
            for i in range(N + 3):
                if i < N:
                    st2(i)
                    st4(i)
                if 0 <= i - 1 < N:
                    st5(i - 1)
                if 0 <= i - 2 < N:
                    st6(i - 2)
                if 0 <= i - 3 < N:
                    st7(i - 3)

        @blk.gpsimd
        def _(gp):
            for i in range(N):
                it, ch = divmod(i, NCH)
                gp.wait_ge(cs[5].h, cb[5] + i + 1)
                gp.dma_start(out=sp_opre[it * 128:(it + 1) * 128, ch * CH:(ch + 1) * CH],
                             in_=bb1[:, (i % 3) * CH:(i % 3) * CH + CH]).then_inc(*sw[i % 3].inc(16))

    with nc.Block() as blk:
        @blk.vector
        def _(v):
            v.tensor_reduce(maxv[:, 0:1], maxb[:, 0:IT * NCH], AX.X, ALU.max)
            v.drain()
    emit_allreduce_max(b, maxv, 1, grow, bk[5], ident_f, cc_in, cc_out, replica, tag)
    emit_scales(b, st, grow, 1, scal, 8, bk[6], tag)

    NB = NG * ET
    NN = NG * IT
    with nc.Block() as blk:
        cs = [b.sem(f"g_c{k}") for k in range(8)]
        cb = [x.v for x in cs]
        ld = [b.sem(n) for n in ("g_ldxA", "g_ldxB", "g_ldx")]
        ldb = [x.v for x in ld]
        sw = [b.sem(n) for n in ("g_ep_c0", "g_ep_c1", "g_ep_c2")]
        swb = [x.v for x in sw]
        rl = [b.sem(n) for n in ("g_wv_p0", "g_wv_p1")]
        rlb = [x.v for x in rl]
        wdl = [b.sem("g_w2_p1")]
        wdlb = [x.v for x in wdl]
        do = [b.sem(n) for n in ("g_rs_c0", "g_rs_c1")]
        dob = [x.v for x in do]
        sev = b.sem("g_ev")
        sevb = sev.v
        smm = b.sem("g_mm")
        smmb = smm.v

        @blk.sync
        def _(s):
            for k in range(3):
                s.wait_ge(sw[k].h, swb[k])
            s.dma_start(out=bass.AP(wstrD, 0,
                                    [[ET * IT * 128, 128], [IT * 128, ET], [128, IT], [1, 128]]),
                        in_=bass.AP(wd_bf, 0,
                                    [[E, 128], [128, ET], [128 * E, IT], [1, 128]])).then_inc(*wdl[0].inc(16))
            for j in range(NN):
                g, it = divmod(j, IT)
                if it == 0:
                    if g >= 2:
                        s.wait_ge(sev.h, sevb + (g - 1) * ET)
                    s.dma_start(out=bass.AP(xg[g % 2], 0, [[ET * G, 128], [G, ET], [1, G]]),
                                in_=bass.AP(src, g * G, [[T, 128], [128 * T, ET], [1, G]])).then_inc(*rl[g % 2].inc(16))
                if j >= 3:
                    s.wait_ge(cs[0].h, cb[0] + j - 2)
                s.dma_start(out=gld[:, (j % 3) * CH:(j % 3) * CH + G],
                            in_=sp_opre[it * 128:(it + 1) * 128, g * G:(g + 1) * G]).then_inc(*ld[j % 3].inc(16))

        @blk.scalar
        def _(s):
            def S3(j):
                g, it = divmod(j, IT)
                s.wait_ge(cs[1].h, cb[1] + j + 1)
                if it == 0 and g >= 2:
                    s.wait_ge(sev.h, sevb + (g - 1) * ET)
                s.activation(oT2[:, (g % 2) * ITG + it * G:(g % 2) * ITG + it * G + G],
                             ib1[:, (j % 2) * CH:(j % 2) * CH + G],
                             AF.Copy, scale=scal[:, 8:9]).then_inc(*cs[2].inc())
            for j in range(NN):
                S3(j)

        @blk.vector
        def _(v):
            def st1(j):
                v.wait_ge(ld[j % 3].h, ldb[j % 3] + (j // 3 + 1) * 16)
                if j >= 2:
                    v.wait_ge(cs[1].h, cb[1] + j - 1)
                v.tensor_scalar(fb1[:, (j % 2) * CH:(j % 2) * CH + G],
                                gld[:, (j % 3) * CH:(j % 3) * CH + G],
                                scal[:, 9:10], MAGIC, ALU.mult, ALU.add).then_inc(*cs[0].inc())
            def st2(j):
                v.wait_ge(cs[0].h, cb[0] + j + 1)
                if j >= 2:
                    v.wait_ge(cs[2].h, cb[2] + j - 1)
                v.tensor_single_scalar(ib1[:, (j % 2) * CH:(j % 2) * CH + G],
                                       fb1[:, (j % 2) * CH:(j % 2) * CH + G],
                                       MAGIC, ALU.subtract).then_inc(*cs[1].inc())
            def evac(idx):
                g, eo = divmod(idx, ET)
                v.wait_ge(smm.h, smmb + idx + 1)
                v.wait_ge(rl[g % 2].h, rlb[g % 2] + (g // 2 + 1) * 16)
                if idx >= 2:
                    v.wait_ge(do[idx % 2].h, dob[idx % 2] + (idx // 2) * 16)
                v.tensor_tensor(resg[:, (idx % 2) * G:(idx % 2) * G + G], bk[idx % 4][:, 0:G],
                                xg[g % 2][:, eo * G:(eo + 1) * G], ALU.add).then_inc(*sev.inc())
            nev = 0
            for j in range(NN + 1):
                if j < NN:
                    st1(j)
                if 0 <= j - 1 < NN:
                    st2(j - 1)
                if j >= IT and (j - IT) % 3 == 2 and nev < NB:
                    evac(nev)
                    nev += 1
            while nev < NB:
                evac(nev)
                nev += 1

        @blk.tensor
        def _(te):
            for g in range(NG):
                for eo in range(ET):
                    idx = g * ET + eo
                    if idx == 0:
                        te.wait_ge(wdl[0].h, wdlb[0] + 16)
                    if idx >= 4:
                        te.wait_ge(sev.h, sevb + idx - 3)
                    for it in range(IT):
                        if eo == 0:
                            te.wait_ge(cs[2].h, cb[2] + g * IT + it + 1)
                        mm = te.matmul(bk[idx % 4][:, 0:G],
                                       wstrD[:, (eo * IT + it) * 128:(eo * IT + it + 1) * 128],
                                       oT2[:, (g % 2) * ITG + it * G:(g % 2) * ITG + it * G + G],
                                       start=(it == 0), stop=(it == IT - 1))
                    mm.then_inc(*smm.inc())

        @blk.gpsimd
        def _(gp):
            for idx in range(NB):
                g, eo = divmod(idx, ET)
                gp.wait_ge(sev.h, sevb + idx + 1)
                gp.dma_start(out=dst[eo * 128:(eo + 1) * 128, g * G:(g + 1) * G],
                             in_=resg[:, (idx % 2) * G:(idx % 2) * G + G]).then_inc(*do[idx % 2].inc(16))

    with nc.Block() as blk:
        @blk.sync
        def _(s):
            for nm in ("g_rs_c0", "g_rs_c1"):
                s.wait_ge(*b.sem(nm).tail())
    sc2.close()
    sc.close()


def emit_attn_b(b, st, src, dst, waiv_bf, wmix_bf, db_sb, bmix_sb, sp_csw, sp_cd,
                ident_f, onesf):
    cfg = b.cfg
    nc = b.nc
    E, T, G, SCH = cfg["E"], cfg["T"], cfg["G"], cfg["SCH"]
    I3 = 3 * E
    ET, IT3, NG, TPG = E // 128, I3 // 128, T // G, G // 128
    SPG = G // SCH
    NSC = T // SCH

    sc = Scope(b)
    xg = [sc.sbuf(f"b_xg{i}", [128, ET * G], F32) for i in range(2)]
    lnb = sc.sbuf("b_lnb", [128, 4 * G], BF16)
    xnc = sc.sbuf("b_xnc", [128, ET * G], BF16)
    rb_sb = sc.sbuf("b_rb", [128, G], F32)
    wstr = sc.sbuf("b_wstr", [128, 2 * ET * 128], BF16)
    t_a = sc.sbuf("b_ta", [128, ET * G], F32)
    t_i = sc.sbuf("b_ti", [128, ET * G], F32)
    t_v = sc.sbuf("b_tv", [128, ET * G], F32)
    g1 = sc.sbuf("b_g1", [128, ET * G], F32)
    ctd = sc.sbuf("b_ctd", [128, ET * NSC], F32)
    cfs = sc.sbuf("b_cfs", [128, ET * NSC], F32)
    ccd = sc.sbuf("b_ccd", [128, ET * NSC], F32)
    wr2 = sc.sbuf("b_wr2", [128, ET * NSC], F32)
    cwc = sc.sbuf("b_cwc", [128, ET * NSC], F32)
    inc = sc.sbuf("b_inc", [128, ET * NSC], F32)
    hb2 = sc.sbuf("b_h", [128, 2 * ET * G], BF16)
    wmr = sc.sbuf("b_wmr", [128, ET * ET * 128], BF16)
    resg = sc.sbuf("b_resg", [128, 2 * G], F32)
    rstmask = sc.sbuf("b_rst", [128, G], F32)
    bk = sc.banks("b_bk", 6)

    with nc.Block() as blk:
        @blk.vector
        def _(v):
            v.memset(rstmask[:, :], 1.0)
            v.drain()
            v.memset(bass.AP(rstmask, 0, [[G, 128], [SCH, SPG]]), 0.0)

    ld_x = [b.sem("g_ldxA"), b.sem("g_ldxB")]
    st_sp = b.sem("g_stspA")
    s_st = b.sem("g_stres")

    for g in range(NG):
        with nc.Block() as blk:
            @blk.sync
            def _(s):
                if g == 0:
                    s.dma_start(out=bass.AP(xg[0], 0, [[ET * G, 128], [G, ET], [1, G]]),
                                in_=bass.AP(src, 0, [[T, 128], [128 * T, ET], [1, G]])).then_inc(*ld_x[0].inc(16))
                if g + 1 < NG:
                    s.dma_start(out=bass.AP(xg[(g + 1) % 2], 0, [[ET * G, 128], [G, ET], [1, G]]),
                                in_=bass.AP(src, (g + 1) * G, [[T, 128], [128 * T, ET], [1, G]])).then_inc(*ld_x[(g + 1) % 2].inc(16))
                s.wait_ge(*ld_x[g % 2].tail())
        cur = xg[g % 2]

        with nc.Block() as blk:
            emit_ln_group(b, blk, st, bk, cur, lnb[:, 0:2 * G], lnb[:, 2 * G:4 * G],
                          xnc, rb_sb, None, False)

        with nc.Block() as blk:
            wp = Pipe(b, "g_w", 2, pq=16, dma_p=True)
            mp = Pipe(b, "g_mp", 4)
            sg = Pipe(b, "g_sg", 4)

            @blk.sync
            def _(s):
                for ft in range(IT3):
                    s.wait_ge(mp.cs[0].h, mp.cb[0] + max(0, ft - 1))
                    s.dma_start(out=bass.AP(wstr, (ft % 2) * ET * 128,
                                            [[2 * ET * 128, 128], [128, ET], [1, 128]]),
                                in_=bass.AP(waiv_bf, ft * 128,
                                            [[3 * E, 128], [128 * 3 * E, ET], [1, 128]])).then_inc(*wp.pinc_part(ft))

            @blk.tensor
            def _(te):
                for ft in range(IT3):
                    wp.cwait(te, ft)
                    mp.pwait(te, ft)
                    buf = wstr[:, (ft % 2) * ET * 128:(ft % 2) * ET * 128 + ET * 128]
                    for e in range(ET):
                        mm = te.matmul(bk[ft % 4][:, 0:G], buf[:, e * 128:(e + 1) * 128],
                                       xnc[:, e * G:(e + 1) * G],
                                       start=(e == 0), stop=(e == ET - 1))
                    mm.then_inc(*mp.pinc(ft))

            @blk.vector
            def _(v):
                v.wait_ge(*st_sp.tail())
                for ft in range(IT3):
                    mp.cwait(v, ft)
                    if ft < ET:
                        dstb = t_a[:, ft * G:(ft + 1) * G]
                    elif ft < 2 * ET:
                        dstb = t_i[:, (ft - ET) * G:(ft - ET + 1) * G]
                    else:
                        dstb = t_v[:, (ft - 2 * ET) * G:(ft - 2 * ET + 1) * G]
                    v.tensor_tensor(dstb, bk[ft % 4][:, 0:G], rb_sb[:, 0:G], ALU.mult)
                    if ft < 2 * ET:
                        sg.pwait(v, ft)
                        v.engine_nop().then_inc(*sg.pinc(ft))
                    v.engine_nop().then_inc(*mp.cinc(ft))

            @blk.scalar
            def _(s):
                for ft in range(2 * ET):
                    sg.cwait(s, ft)
                    if ft < ET:
                        s.activation(t_a[:, ft * G:(ft + 1) * G], t_a[:, ft * G:(ft + 1) * G],
                                     AF.Sigmoid, bias=db_sb[:, ft:ft + 1])
                    else:
                        e = ft - ET
                        s.activation(t_i[:, e * G:(e + 1) * G], t_i[:, e * G:(e + 1) * G],
                                     AF.Sigmoid)
                    s.engine_nop().then_inc(*sg.cinc(ft))

        with nc.Block() as blk:
            R = ET
            import os as _os2
            _FRESH = "b" if _os2.environ.get("CHAIN_FRESH", "0") == "1" else "g"
            p1 = b.sem(f"{_FRESH}_c0"); v2 = b.sem(f"{_FRESH}_c1"); v3 = b.sem(f"{_FRESH}_c2")
            s4 = b.sem(f"{_FRESH}_c3"); p5 = b.sem(f"{_FRESH}_c4"); p6 = b.sem(f"{_FRESH}_c5")
            v7 = b.sem(f"{_FRESH}_s2"); s8 = b.sem(f"{_FRESH}_c7")
            v9 = b.sem(f"{_FRESH}_s1"); v10 = b.sem(f"{_FRESH}_c6")
            s11 = b.sem(f"{_FRESH}_l1"); s12 = b.sem(f"{_FRESH}_l2")
            p13 = b.sem(f"{_FRESH}_mv_p"); v14 = b.sem(f"{_FRESH}_mv_c")
            bs = {x: x.v for x in (p1, v2, v3, s4, p5, p6, v7, s8, v9, v10,
                                   s11, s12, p13, v14)}
            spb = st_sp.v
            def sl(r):
                return slice(r * G, (r + 1) * G)

            import os as _os
            _POOL = _os.environ.get("CHAIN_POOL", "1") == "1"

            @blk.gpsimd
            def _(gp):
                def R1(r):
                    gp.tensor_tensor(g1[:, sl(r)], t_a[:, sl(r)], t_a[:, sl(r)],
                                     ALU.mult).then_inc(*p1.inc())
                def R5(r):
                    gp.tensor_tensor(t_v[:, sl(r)], t_v[:, sl(r)], t_i[:, sl(r)],
                                     ALU.mult).then_inc(*p5.inc())
                def R6(r):
                    gp.wait_ge(p5.h, bs[p5] + r + 1)
                    gp.wait_ge(s4.h, bs[s4] + r + 1)
                    gp.tensor_tensor(t_v[:, sl(r)], t_v[:, sl(r)], g1[:, sl(r)],
                                     ALU.mult).then_inc(*p6.inc())
                def R13(r):
                    gp.wait_ge(p6.h, bs[p6] + r + 1)
                    gp.wait_ge(s12.h, bs[s12] + r + 1)
                    gp.tensor_tensor(t_v[:, sl(r)], t_v[:, sl(r)], t_a[:, sl(r)],
                                     ALU.mult).then_inc(*p13.inc())
                if _POOL:
                    for r in range(R):
                        R1(r)
                        R5(r)
                    for r in range(R):
                        R6(r)
                    for r in range(R):
                        R13(r)
                else:
                    for r in range(R):
                        R5(r)
                gp.wait_ge(v14.h, bs[v14] + R)
                gp.dma_start(out=bass.AP(sp_csw, g * G, [[T, 128], [128 * T, ET], [1, G]]),
                             in_=bass.AP(t_i, 0, [[ET * G, 128], [G, ET], [1, G]])).then_inc(*st_sp.inc(16))
                gp.wait_ge(s11.h, bs[s11] + R)
                gp.dma_start(out=bass.AP(sp_cd, g * G, [[T, 128], [128 * T, ET], [1, G]]),
                             in_=bass.AP(g1, 0, [[ET * G, 128], [G, ET], [1, G]])).then_inc(*st_sp.inc(16))

            @blk.vector
            def _(v):
                v.wait_ge(st_sp.h, spb)
                def R2(r):
                    v.wait_ge(p1.h, bs[p1] + r + 1)
                    v.tensor_scalar(g1[:, sl(r)], g1[:, sl(r)], -1.0, 1.0,
                                    ALU.mult, ALU.add).then_inc(*v2.inc())
                def R7(r):
                    v.wait_ge(p1.h, bs[p1] + r + 1)
                    v.tensor_single_scalar(t_a[:, sl(r)], t_a[:, sl(r)], 1e-10,
                                           ALU.max).then_inc(*v7.inc())
                def R3(r):
                    v.wait_ge(v2.h, bs[v2] + r + 1)
                    v.tensor_single_scalar(g1[:, sl(r)], g1[:, sl(r)], 1e-8,
                                           ALU.max).then_inc(*v3.inc())
                def R9(r):
                    v.wait_ge(s8.h, bs[s8] + r + 1)
                    v.wait_ge(p5.h, bs[p5] + r + 1)
                    v.tensor_tensor_scan(t_i[:, sl(r)], rstmask[:, 0:G], t_a[:, sl(r)],
                                         0.0, ALU.mult, ALU.add).then_inc(*v9.inc())
                def R10(r):
                    v.wait_ge(v9.h, bs[v9] + r + 1)
                    v.tensor_single_scalar(t_a[:, sl(r)], t_i[:, sl(r)], C0,
                                           ALU.max).then_inc(*v10.inc())
                def R14(r):
                    v.wait_ge(p13.h, bs[p13] + r + 1)
                    v.wait_ge(s11.h, bs[s11] + r + 1)
                    v.tensor_tensor_scan(t_i[:, sl(r)], rstmask[:, 0:G], t_v[:, sl(r)],
                                         0.0, ALU.mult, ALU.add).then_inc(*v14.inc())
                def R15(r):
                    v.wait_ge(v14.h, bs[v14] + r + 1)
                    v.wait_ge(s11.h, bs[s11] + r + 1)
                    v.tensor_copy(ctd[:, r * NSC + g * SPG:r * NSC + g * SPG + SPG],
                                  bass.AP(g1, r * G + SCH - 1, [[ET * G, 128], [SCH, SPG]]))
                    v.tensor_tensor(cfs[:, r * NSC + g * SPG:r * NSC + g * SPG + SPG],
                                    bass.AP(g1, r * G + SCH - 1, [[ET * G, 128], [SCH, SPG]]),
                                    bass.AP(t_i, r * G + SCH - 1, [[ET * G, 128], [SCH, SPG]]),
                                    ALU.mult)
                def R1v(r):
                    v.tensor_tensor(g1[:, sl(r)], t_a[:, sl(r)], t_a[:, sl(r)],
                                    ALU.mult).then_inc(*p1.inc())
                def R6v(r):
                    v.wait_ge(p5.h, bs[p5] + r + 1)
                    v.wait_ge(s4.h, bs[s4] + r + 1)
                    v.tensor_tensor(t_v[:, sl(r)], t_v[:, sl(r)], g1[:, sl(r)],
                                    ALU.mult).then_inc(*p6.inc())
                def R13v(r):
                    v.wait_ge(p6.h, bs[p6] + r + 1)
                    v.wait_ge(s12.h, bs[s12] + r + 1)
                    v.tensor_tensor(t_v[:, sl(r)], t_v[:, sl(r)], t_a[:, sl(r)],
                                    ALU.mult).then_inc(*p13.inc())
                for r in range(R + 4):
                    if r < R:
                        if not _POOL:
                            R1v(r)
                        R2(r)
                        R7(r)
                    if 0 <= r - 1 < R: R3(r - 1)
                    if not _POOL and 0 <= r - 2 < R: R6v(r - 2)
                    if 0 <= r - 2 < R: R9(r - 2)
                    if 0 <= r - 3 < R: R10(r - 3)
                for r in range(R):
                    if not _POOL:
                        R13v(r)
                    R14(r)
                    if r >= 1:
                        R15(r - 1)
                R15(R - 1)

            @blk.scalar
            def _(s):
                s4a = b.sem(f"{_FRESH}_r")
                b4a = s4a.v
                def R4a(r):
                    s.wait_ge(v3.h, bs[v3] + r + 1)
                    s.activation(g1[:, sl(r)], g1[:, sl(r)], AF.Ln).then_inc(*s4a.inc())
                def R4b(r):
                    s.wait_ge(s4a.h, b4a + r + 1)
                    s.activation(g1[:, sl(r)], g1[:, sl(r)], AF.Exp,
                                 scale=0.5).then_inc(*s4.inc())
                def R8(r):
                    s.wait_ge(v7.h, bs[v7] + r + 1)
                    s.activation(t_a[:, sl(r)], t_a[:, sl(r)], AF.Ln).then_inc(*s8.inc())
                def R11(r):
                    s.wait_ge(v9.h, bs[v9] + r + 1)
                    s.wait_ge(p6.h, bs[p6] + r + 1)
                    s.activation(g1[:, sl(r)], t_i[:, sl(r)], AF.Exp).then_inc(*s11.inc())
                def R12(r):
                    s.wait_ge(v10.h, bs[v10] + r + 1)
                    s.activation(t_a[:, sl(r)], t_a[:, sl(r)], AF.Exp,
                                 scale=-1.0).then_inc(*s12.inc())
                for r in range(R + 8):
                    if 0 <= r - 2 < R:
                        R4a(r - 2)
                        R8(r - 2)
                    if 0 <= r - 3 < R:
                        R4b(r - 3)
                    if 0 <= r - 8 < R:
                        R11(r - 8)
                        R12(r - 8)

    with nc.Block() as blk:
        l1 = b.sem("g_l1")
        l2 = b.sem("g_l2")

        @blk.vector
        def _(v):
            for e in range(ET):
                sl = slice(e * NSC, (e + 1) * NSC)
                v.tensor_single_scalar(wr2[:, sl], ctd[:, sl], 1e-10, ALU.max)
            v.engine_nop().then_inc(*l1.inc())
        k1 = l1.v

        @blk.scalar
        def _(s):
            s.wait_ge(l1.h, k1)
            for e in range(ET):
                sl = slice(e * NSC, (e + 1) * NSC)
                s.activation(wr2[:, sl], wr2[:, sl], AF.Ln)
            s.engine_nop().then_inc(*l2.inc())
        k2 = l2.v

        @blk.vector
        def _(v):
            v.wait_ge(l2.h, k2)
            for e in range(ET):
                sl = slice(e * NSC, (e + 1) * NSC)
                v.tensor_tensor_scan(ccd[:, sl], onesf[:, 0:NSC], wr2[:, sl], 0.0,
                                     ALU.mult, ALU.add)
            v.drain()
            for e in range(ET):
                sl = slice(e * NSC, (e + 1) * NSC)
                v.tensor_single_scalar(wr2[:, sl], ccd[:, sl], C0, ALU.max)
            v.engine_nop().then_inc(*l1.inc())
        k3 = l1.v

        @blk.scalar
        def _(s):
            s.wait_ge(l1.h, k3)
            for e in range(ET):
                sl = slice(e * NSC, (e + 1) * NSC)
                s.activation(ccd[:, sl], ccd[:, sl], AF.Exp)
            s.drain()
            for e in range(ET):
                sl = slice(e * NSC, (e + 1) * NSC)
                s.activation(wr2[:, sl], wr2[:, sl], AF.Exp, scale=-1.0)
            s.engine_nop().then_inc(*l2.inc())
        k4 = l2.v

        @blk.vector
        def _(v):
            v.wait_ge(l2.h, k4)
            for e in range(ET):
                sl = slice(e * NSC, (e + 1) * NSC)
                v.tensor_tensor(wr2[:, sl], cfs[:, sl], wr2[:, sl], ALU.mult)
            v.drain()
            for e in range(ET):
                sl = slice(e * NSC, (e + 1) * NSC)
                v.tensor_tensor_scan(cwc[:, sl], onesf[:, 0:NSC], wr2[:, sl], 0.0,
                                     ALU.mult, ALU.add)
            v.drain()
            v.memset(inc[:, :], 0.0)
            v.drain()
            for e in range(ET):
                v.tensor_tensor(inc[:, e * NSC + 1:(e + 1) * NSC],
                                ccd[:, e * NSC:(e + 1) * NSC - 1],
                                cwc[:, e * NSC:(e + 1) * NSC - 1], ALU.mult)
            v.drain()

    with nc.Block() as blk:
        lc = [b.sem("g_ldc"), b.sem("g_lsp_p0")]
        lcb = [x.v for x in lc]
        wl = b.sem("g_w2_p0")
        wlb = wl.v
        cst = b.sem("g_s1")
        cstb = cst.v
        cmm = b.sem("g_mm")
        cmmb = cmm.v
        cev = b.sem("g_ev")
        cevb = cev.v
        douts = [b.sem("g_rs_c0"), b.sem("g_rs_c1")]
        doutb = [x.v for x in douts]
        cdb = [t_a, t_v]
        cwb = [t_i, g1]
        NBm = NG * ET

        sx = [b.sem("g_w_p0"), b.sem("g_w_p1")]
        sxb = [x.v for x in sx]

        @blk.sync
        def _(s):
            s.wait_ge(*st_sp.tail())
            s.dma_start(out=bass.AP(wmr, 0,
                                    [[ET * ET * 128, 128], [ET * 128, ET], [128, ET], [1, 128]]),
                        in_=bass.AP(wmix_bf, 0,
                                    [[E, 128], [128, ET], [128 * E, ET], [1, 128]])).then_inc(*wl.inc(16))
            for g in range(NG):
                if g >= 2:
                    s.wait_ge(cst.h, cstb + g - 1)
                s.dma_start(out=bass.AP(cdb[g % 2], 0, [[ET * G, 128], [G, ET], [1, G]]),
                            in_=bass.AP(sp_cd, g * G, [[T, 128], [128 * T, ET], [1, G]])).then_inc(*lc[g % 2].inc(16))
                s.dma_start(out=bass.AP(cwb[g % 2], 0, [[ET * G, 128], [G, ET], [1, G]]),
                            in_=bass.AP(sp_csw, g * G, [[T, 128], [128 * T, ET], [1, G]])).then_inc(*lc[g % 2].inc(16))

        @blk.scalar
        def _(s):
            for g in range(NG):
                if g >= 2:
                    s.wait_ge(cev.h, cevb + (g - 1) * ET)
                s.dma_start(out=bass.AP(xg[g % 2], 0, [[ET * G, 128], [G, ET], [1, G]]),
                            in_=bass.AP(src, g * G, [[T, 128], [128 * T, ET], [1, G]])).then_inc(*sx[g % 2].inc(16))

        @blk.vector
        def _(v):
            def states(g):
                v.wait_ge(lc[g % 2].h, lcb[g % 2] + (g // 2 + 1) * 32)
                v.wait_ge(sx[g % 2].h, sxb[g % 2] + (g // 2 + 1) * 16)
                if g >= 2:
                    v.wait_ge(cmm.h, cmmb + (g - 1) * ET)
                op = None
                for e in range(ET):
                    for ch in range(SPG):
                        sl = slice(e * G + ch * SCH, e * G + (ch + 1) * SCH)
                        hsl = slice((g % 2) * ET * G + e * G + ch * SCH,
                                    (g % 2) * ET * G + e * G + (ch + 1) * SCH)
                        op = v.scalar_tensor_tensor(hb2[:, hsl], cwb[g % 2][:, sl],
                                                    inc[:, e * NSC + g * SPG + ch:e * NSC + g * SPG + ch + 1],
                                                    cdb[g % 2][:, sl], ALU.add, ALU.mult)
                op.then_inc(*cst.inc())
            def evac(idx):
                g, eo = divmod(idx, ET)
                v.wait_ge(cmm.h, cmmb + idx + 1)
                if idx >= 2:
                    v.wait_ge(douts[idx % 2].h, doutb[idx % 2] + (idx // 2) * 16)
                v.scalar_tensor_tensor(resg[:, (idx % 2) * G:(idx % 2) * G + G],
                                       bk[idx % 4][:, 0:G], bmix_sb[:, eo:eo + 1],
                                       xg[g % 2][:, eo * G:(eo + 1) * G],
                                       ALU.add, ALU.add).then_inc(*cev.inc())
            for g in range(NG):
                states(g)
                if g >= 1:
                    for eo in range(ET):
                        evac((g - 1) * ET + eo)
            for eo in range(ET):
                evac((NG - 1) * ET + eo)

        @blk.tensor
        def _(te):
            for g in range(NG):
                for eo in range(ET):
                    idx = g * ET + eo
                    if idx == 0:
                        te.wait_ge(wl.h, wlb + 16)
                    if eo == 0:
                        te.wait_ge(cst.h, cstb + g + 1)
                    if idx >= 4:
                        te.wait_ge(cev.h, cevb + idx - 3)
                    for e in range(ET):
                        mm = te.matmul(bk[idx % 4][:, 0:G],
                                       wmr[:, (eo * ET + e) * 128:(eo * ET + e + 1) * 128],
                                       hb2[:, (g % 2) * ET * G + e * G:(g % 2) * ET * G + (e + 1) * G],
                                       start=(e == 0), stop=(e == ET - 1))
                    mm.then_inc(*cmm.inc())

        @blk.gpsimd
        def _(gp):
            for idx in range(NBm):
                g, eo = divmod(idx, ET)
                gp.wait_ge(cev.h, cevb + idx + 1)
                gp.dma_start(out=dst[eo * 128:(eo + 1) * 128, g * G:(g + 1) * G],
                             in_=resg[:, (idx % 2) * G:(idx % 2) * G + G]).then_inc(*douts[idx % 2].inc(16))

    with nc.Block() as blk:
        @blk.sync
        def _(s):
            for sm in b.sem("g_rs_c0"), b.sem("g_rs_c1"):
                s.wait_ge(*sm.tail())
    sc.close()




def build_program(cfg, phases=('attn_a', 'ffn1', 'attn_b', 'ffn2')):
    E, T, G, I = cfg["E"], cfg["T"], cfg["G"], cfg["I"]
    ET = E // 128
    b = B(cfg)
    nc = b.nc
    replica = [list(range(cfg["n_cores"]))]

    xT = b.dram("xT", [E, T], F32, kind="ExternalInput")
    w_attn_T = b.dram("w_attn_T", [E, 3 * E], F32, kind="ExternalInput")
    w_proj_T = b.dram("w_proj_T", [E, E], F32, kind="ExternalInput")
    W_aiv = b.dram("W_aiv", [E, 3 * E], F32, kind="ExternalInput")
    W_mix = b.dram("W_mix", [E, E], F32, kind="ExternalInput")
    decay_bias = b.dram("decay_bias", [E, 1], F32, kind="ExternalInput")
    b_mix = b.dram("b_mix", [E, 1], F32, kind="ExternalInput")
    w_gate = b.dram("w_gate", [I, E], F32, kind="ExternalInput")
    w_up = b.dram("w_up", [I, E], F32, kind="ExternalInput")
    w_down = b.dram("w_down", [E, I], F32, kind="ExternalInput")
    identc_b = b.dram("identc_b", [128, 128], BF16, kind="ExternalInput")
    identc_f = b.dram("identc_f", [128, 128], F32, kind="ExternalInput")
    maskTc = b.dram("maskTc", [128, 128], BF16, kind="ExternalInput")
    outT = b.dram("outT", [E, T], F32, kind="ExternalOutput")

    res1 = b.dram("res1", [E, T], F32)
    res2 = b.dram("res2", [E, T], F32)
    res3 = b.dram("res3", [E, T], F32)
    sp_x2n = b.dram("sp_x2n", [E, T], F32)
    sp_up = b.dram("sp_up", [I, T], BF16)
    sp_gate = b.dram("sp_gate", [I, T], BF16)
    sp_t = b.dram("sp_t", [I, T], BF16)
    sp_opre = b.dram("sp_opre", [I, T], BF16)
    sp_csw = b.dram("sp_csw", [E, T], F32)
    sp_cd = b.dram("sp_cd", [E, T], F32)
    wa_bf = b.dram("wa_bf", [E, 3 * E], BF16)
    wp_bf = b.dram("wp_bf", [E, E], BF16)
    waiv_bf = b.dram("waiv_bf", [E, 3 * E], BF16)
    wmix_bf = b.dram("wmix_bf", [E, E], BF16)
    wdq_upT = b.dram("wdq_upT", [E, I], BF16)
    wdq_gateT = b.dram("wdq_gateT", [E, I], BF16)
    wdq_downT = b.dram("wdq_downT", [I, E], BF16)
    cc_in = b.dram("cc_in", [1, 2], F32)
    cc_out = b.dram("cc_out", [1, 2], F32)

    ones_col_b = b.sbuf("ones_col_b", [128, 1], BF16)
    ones_row_f = b.sbuf("ones_row_f", [1, 128], F32)
    one_f = b.sbuf("one_f", [1, 1], F32)
    onesf = b.sbuf("onesf", [128, 512], F32)
    ident_b = b.sbuf("ident_b", [128, 128], BF16)
    ident_f = b.sbuf("ident_f", [128, 128], F32)
    maskT_b = b.sbuf("maskT_b", [128, 128], BF16)
    db_sb = b.sbuf("db_sb", [128, ET], F32)
    bmix_sb = b.sbuf("bmix_sb", [128, ET], F32)
    eps_t = b.sbuf("eps_t", [1, 1], F32)
    scal = b.sbuf("scal", [128, 16], F32)
    st = make_stats(b)
    st.update(dict(ones_col_b=ones_col_b, ones_row_f=ones_row_f, one_f=one_f, eps_t=eps_t))

    with nc.Block() as blk:
        @blk.sync
        def _(s):
            sd = b.sem("c0")
            s.dma_start(out=ident_b[:, :], in_=identc_b[:, :]).then_inc(*sd.inc(16))
            s.dma_start(out=ident_f[:, :], in_=identc_f[:, :]).then_inc(*sd.inc(16))
            s.dma_start(out=maskT_b[:, :], in_=maskTc[:, :]).then_inc(*sd.inc(16))
            for e in range(ET):
                s.dma_start(out=db_sb[:, e:e + 1], in_=decay_bias[e * 128:(e + 1) * 128, :]).then_inc(*sd.inc(16))
                s.dma_start(out=bmix_sb[:, e:e + 1], in_=b_mix[e * 128:(e + 1) * 128, :]).then_inc(*sd.inc(16))
            s.wait_ge(*sd.tail())

        @blk.vector
        def _(v):
            v.memset(ones_col_b[:, :], 1.0)
            v.memset(ones_row_f[:, :], 1.0)
            v.memset(one_f[:, :], 1.0)
            v.memset(onesf[:, :], 1.0)
            v.memset(eps_t[:, :], 1e-5)
            v.memset(scal[:, :], 0.0)
            v.memset(st["magic_col"][:, :], MAGIC)

    emit_cast_bf16(b, w_attn_T, wa_bf, "ca")
    emit_cast_bf16(b, w_proj_T, wp_bf, "cp")
    emit_cast_bf16(b, W_aiv, waiv_bf, "cv")
    emit_cast_bf16(b, W_mix, wmix_bf, "cm")
    emit_weight_prep(b, w_up, wdq_upT, ident_b, "wu")
    emit_weight_prep(b, w_gate, wdq_gateT, ident_b, "wg")
    emit_weight_prep(b, w_down, wdq_downT, ident_b, "wd")

    chain = {"attn_a": res1, "ffn1": res2, "attn_b": res3, "ffn2": outT}
    last = None
    cur_src = xT
    for ph in phases:
        dst = chain[ph] if ph != phases[-1] else outT
        if ph == "attn_a":
            emit_attn_a(b, st, cur_src, dst, wa_bf, wp_bf, ident_f, maskT_b)
        elif ph in ("ffn1", "ffn2"):
            emit_ffn(b, st, cur_src, dst, wdq_upT, wdq_gateT, wdq_downT, sp_x2n, sp_up,
                     sp_gate, sp_t, sp_opre, scal, cc_in, cc_out, ident_f, replica, ph)
        elif ph == "attn_b":
            emit_attn_b(b, st, cur_src, dst, waiv_bf, wmix_bf, db_sb, bmix_sb, sp_csw,
                        sp_cd, ident_f, onesf)
        cur_src = dst

    return b


def host_inputs(cfg, core_x, weights):
    import ml_dtypes
    E = cfg["E"]
    bf = ml_dtypes.bfloat16
    m = {
        "xT": np.ascontiguousarray(core_x.T.astype(np.float32)),
        "w_attn_T": np.ascontiguousarray(weights["w_attn"].T.astype(np.float32)),
        "w_proj_T": np.ascontiguousarray(weights["w_proj"].T.astype(np.float32)),
        "W_aiv": np.ascontiguousarray(weights["W_aiv"].astype(np.float32)),
        "W_mix": np.ascontiguousarray(weights["W_mix"].astype(np.float32)),
        "decay_bias": np.ascontiguousarray(weights["decay_bias"].astype(np.float32).reshape(E, 1)),
        "b_mix": np.ascontiguousarray(weights["b_mix"].astype(np.float32).reshape(E, 1)),
        "w_gate": np.ascontiguousarray(weights["w_gate"].astype(np.float32)),
        "w_up": np.ascontiguousarray(weights["w_up"].astype(np.float32)),
        "w_down": np.ascontiguousarray(weights["w_down"].astype(np.float32)),
        "identc_b": np.eye(128, dtype=np.float32).astype(bf),
        "identc_f": np.eye(128, dtype=np.float32),
        "maskTc": np.tril(np.ones((128, 128), np.float32)).T.astype(bf),
    }
    return m


from concourse.bass_utils import run_bass_kernel_spmd

_PROG_CACHE = {}

def kernel(L0_attn_a_x, L0_attn_a_w_attn, L0_attn_a_w_proj, L0_attn_b_W_aiv,
           L0_attn_b_decay_bias, L0_attn_b_W_mix, L0_attn_b_b_mix,
           L0_ffn_w_gate, L0_ffn_w_up, L0_ffn_w_down):
    X = np.asarray(L0_attn_a_x)
    Bn, T, E = X.shape
    cfg = default_cfg(T=T)
    assert Bn == cfg["n_cores"] and E == cfg["E"]
    key = (T, E)
    if key not in _PROG_CACHE:
        _PROG_CACHE[key] = build_program(cfg)
    b = _PROG_CACHE[key]
    weights = dict(
        w_attn=np.asarray(L0_attn_a_w_attn), w_proj=np.asarray(L0_attn_a_w_proj),
        W_aiv=np.asarray(L0_attn_b_W_aiv), W_mix=np.asarray(L0_attn_b_W_mix),
        decay_bias=np.asarray(L0_attn_b_decay_bias), b_mix=np.asarray(L0_attn_b_b_mix),
        w_gate=np.asarray(L0_ffn_w_gate), w_up=np.asarray(L0_ffn_w_up),
        w_down=np.asarray(L0_ffn_w_down),
    )
    in_maps = [host_inputs(cfg, X[c], weights) for c in range(Bn)]
    res = run_bass_kernel_spmd(b.nc, in_maps, core_ids=list(range(Bn)))
    out = np.empty((Bn, T, E), np.float32)
    for c in range(Bn):
        out[c] = np.ascontiguousarray(np.asarray(res.results[c]["outT"]).T)
    return out


# version 17
# speedup vs baseline: 1.0647x; 1.0057x over previous
from contextlib import ExitStack
import numpy as np
import concourse.bass as bass
import concourse.mybir as mybir

F32 = mybir.dt.float32
BF16 = mybir.dt.bfloat16
I16 = mybir.dt.int16
U16 = mybir.dt.uint16
ALU = mybir.AluOpType
AF = mybir.ActivationFunctionType
AX = mybir.AxisListType

MAGIC = 12582912.0
C0 = float(np.float32(np.log(np.float32(1e-10))))


class Sem:
    __slots__ = ("h", "v")
    def __init__(self, nc, name):
        self.h = nc._stack.enter_context(nc.semaphore(name))
        self.v = 0
    def inc(self, n=1):
        self.v += n
        return self.h, n
    def tail(self):
        return self.h, self.v


class Pipe:
    def __init__(self, b, name, nbuf, pq=1, cq=1, dma_p=False, dma_c=False):
        self.nbuf = nbuf
        self.pq, self.cq = pq, cq
        self.ps = [b.sem(f"{name}_p{i}" if dma_p else f"{name}_p")
                   for i in range(nbuf if dma_p else 1)]
        self.cs = [b.sem(f"{name}_c{i}" if dma_c else f"{name}_c")
                   for i in range(nbuf if dma_c else 1)]
        if not dma_p:
            self.ps = self.ps * nbuf if False else self.ps
        self.pb = [s.v for s in self.ps]
        self.cb = [s.v for s in self.cs]
        self._pn = len(self.ps)
        self._cn = len(self.cs)
    def pwait(self, eng, i):
        if self._cn == 1:
            eng.wait_ge(self.cs[0].h, self.cb[0] + max(0, i - self.nbuf + 1) * self.cq)
        else:
            j = i % self._cn
            rounds = max(0, (i - self.nbuf) // self._cn + 1)
            eng.wait_ge(self.cs[j].h, self.cb[j] + rounds * self.cq)
    def pinc(self, i):
        return self.ps[i % self._pn].inc(self.pq)
    def pinc_part(self, i, n=16):
        return self.ps[i % self._pn].inc(n)
    def cwait(self, eng, i):
        j = i % self._pn
        eng.wait_ge(self.ps[j].h, self.pb[j] + (i // self._pn + 1) * self.pq)
    def cinc(self, i):
        return self.cs[i % self._cn].inc(self.cq)


class B:
    def __init__(self, cfg):
        self.cfg = cfg
        nc = bass.Bass("TRN2", target_bir_lowering=False, debug=False)
        nc._stack = ExitStack()
        self.nc = nc
        self.sems = {}
    def sem(self, name):
        if name not in self.sems:
            self.sems[name] = Sem(self.nc, name)
        return self.sems[name]
    def sbuf(self, name, shape, dtype):
        return self.nc._stack.enter_context(self.nc.sbuf_tensor(name, shape, dtype))
    def psum(self, name, shape, dtype=F32):
        return self.nc._stack.enter_context(self.nc.psum_tensor(name, shape, dtype))
    def dram(self, name, shape, dtype, kind="Internal"):
        return self.nc.dram_tensor(name, shape, dtype, kind=kind)


def default_cfg(T=4096):
    return dict(E=1024, T=T, G=512, H=16, D=64, ACH=128, SCH=64, I=3072, n_cores=8)


from concourse.bass import BassScalarEngine as _BSE
if not hasattr(_BSE, "engine_nop"):
    _BSE.engine_nop = lambda self: self.drain()
try:
    from concourse.bass import BassGpSimd as _BGS
    if not hasattr(_BGS, "engine_nop"):
        _BGS.engine_nop = lambda self: self.drain()
except Exception:
    pass


from contextlib import ExitStack



class Scope:
    def __init__(self, b):
        self.b = b
        self.es = ExitStack()
    def sbuf(self, name, shape, dtype):
        return self.es.enter_context(self.b.nc.sbuf_tensor(name, shape, dtype))
    def psum(self, name, shape, dtype=F32):
        return self.es.enter_context(self.b.nc.psum_tensor(name, shape, dtype))
    def banks(self, name, n):
        return [self.psum(f"{name}{i}", [128, 512], F32) for i in range(n)]
    def close(self):
        self.es.close()


def emit_weight_prep(b, w, wT_out, ident_b, tag):
    nc = b.nc
    O, Iin = w.shape
    OT, ITn = O // 128, Iin // 128
    nb = Iin // 32
    sc = Scope(b)
    wtmp = sc.sbuf(f"{tag}_tmp", [128, Iin], F32)
    absb = sc.sbuf(f"{tag}_abs", [128, nb], F32)
    sblk = sc.sbuf(f"{tag}_s", [128, nb], F32)
    rblk = sc.sbuf(f"{tag}_rs", [128, nb], F32)
    q16 = sc.sbuf(f"{tag}_q", [128, Iin], I16)
    qf = sc.sbuf(f"{tag}_qf", [128, Iin], F32)
    wq = sc.sbuf(f"{tag}_wq", [128, Iin], BF16)
    pst = [sc.psum(f"{tag}_pst{i}", [128, 1024], BF16) for i in range(2)]
    wtr = sc.sbuf(f"{tag}_tr", [128, ITn * 128], BF16)
    ld = b.sem("g_ldx")
    for ot in range(OT):
        with nc.Block() as blk:
            @blk.sync
            def _(s):
                s.dma_start(out=wtmp[:, :], in_=w[ot * 128:(ot + 1) * 128, :]).then_inc(*ld.inc(16))
                s.wait_ge(*ld.tail())

            @blk.vector
            def _(v):
                v.wait_ge(*ld.tail())
                v.tensor_reduce(absb.ap().rearrange("p (bk o) -> p bk o", o=1),
                                wtmp.ap().rearrange("p (bk k) -> p bk k", k=32),
                                AX.X, ALU.max, apply_absolute_value=True)
                v.drain()
                v.tensor_scalar(sblk[:, :], absb[:, :], 1.0 / 7.0, 1e-12, ALU.mult, ALU.max)
                v.drain()
                v.reciprocal(rblk[:, :], sblk[:, :])
                v.drain()
                v.tensor_tensor(qf.ap().rearrange("p (bk k) -> p bk k", k=32),
                                wtmp.ap().rearrange("p (bk k) -> p bk k", k=32),
                                bass.AP(rblk, 0, [[nb, 128], [1, nb], [0, 32]]), ALU.mult)
                v.drain()
                v.tensor_single_scalar(qf[:, :], qf[:, :], MAGIC, ALU.add)
                v.drain()
                v.tensor_single_scalar(q16[:, :], qf[:, :], MAGIC, ALU.subtract)
                v.drain()
                v.tensor_tensor(wq.ap().rearrange("p (bk k) -> p bk k", k=32),
                                q16.ap().rearrange("p (bk k) -> p bk k", k=32),
                                bass.AP(sblk, 0, [[nb, 128], [1, nb], [0, 32]]), ALU.mult)

        with nc.Block() as blk:
            tp = Pipe(b, "g_tp", 2)
            ep = Pipe(b, "wp_ep", 1, cq=16, dma_c=True)

            @blk.tensor
            def _(te):
                for it in range(ITn):
                    tp.pwait(te, it)
                    te.transpose(pst[it % 2][:, 0:128],
                                 wq[:, it * 128:(it + 1) * 128], ident_b[:, :]).then_inc(*tp.pinc(it))

            @blk.vector
            def _(v):
                ep.pwait(v, 0)
                for it in range(ITn):
                    tp.cwait(v, it)
                    v.tensor_copy(wtr[:, it * 128:(it + 1) * 128],
                                  pst[it % 2][:, 0:128]).then_inc(*tp.cinc(it))
                v.engine_nop().then_inc(*ep.pinc(0))

            @blk.sync
            def _(s):
                ep.cwait(s, 0)
                s.dma_start(out=bass.AP(wT_out, ot * 128, [[O, 128], [128 * O, ITn], [1, 128]]),
                            in_=bass.AP(wtr, 0, [[ITn * 128, 128], [128, ITn], [1, 128]])).then_inc(*ep.cinc(0))
    with nc.Block() as blk:
        @blk.sync
        def _(s):
            for sm in b.sem("wp_ep_c0"), b.sem("wp_ep_c1"):
                s.wait_ge(*sm.tail())
    sc.close()


def emit_cast_bf16(b, w, w_bf, tag):
    nc = b.nc
    R, Ccols = w.shape
    RT = R // 128
    sc = Scope(b)
    stg = [sc.sbuf(f"{tag}_stg{i}", [128, Ccols], F32) for i in range(2)]
    ob = [sc.sbuf(f"{tag}_ob{i}", [128, Ccols], BF16) for i in range(2)]
    with nc.Block() as blk:
        lp = Pipe(b, "g_bl", 2, pq=16, dma_p=True)
        epipe = Pipe(b, "g_bo", 2, cq=16, dma_c=True)

        @blk.sync
        def _(s):
            for rt in range(RT):
                lp.pwait(s, rt)
                s.dma_start(out=stg[rt % 2][:, :], in_=w[rt * 128:(rt + 1) * 128, :]).then_inc(*lp.pinc(rt))

        @blk.vector
        def _(v):
            for rt in range(RT):
                lp.cwait(v, rt)
                epipe.pwait(v, rt)
                v.tensor_copy(ob[rt % 2][:, :], stg[rt % 2][:, :])
                v.engine_nop().then_inc(*lp.cinc(rt))
                v.engine_nop().then_inc(*epipe.pinc(rt))

        @blk.gpsimd
        def _(g):
            for rt in range(RT):
                epipe.cwait(g, rt)
                g.dma_start(out=w_bf[rt * 128:(rt + 1) * 128, :], in_=ob[rt % 2][:, :]).then_inc(*epipe.cinc(rt))
            for sm in epipe.cs:
                g.wait_ge(*sm.tail())
    sc.close()


def emit_ln_group(b, blk, st, banks, xg, xb, sqb, xnc_sb, rb_sb, rcol_sb, want_rcol,
                  x2n_out=None):
    cfg = b.cfg
    E, G = cfg["E"], cfg["G"]
    ET, TPG = E // 128, G // 128
    s_mm = b.sem("ln_mm")
    s_ev = b.sem("ln_ev")
    ps_st, ps_st2, ps_mb, ps_rb, ps_rc = banks[0], banks[1], banks[2], banks[3], banks[4]

    lnc = Pipe(b, "ln_c", 2)

    @blk.scalar
    def _(s):
        for e in range(ET):
            lnc.pwait(s, e)
            sl = slice((e % 2) * G, (e % 2) * G + G)
            s.activation(xb[:, sl], xg[:, e * G:(e + 1) * G], AF.Copy)
            s.drain()
            s.activation(sqb[:, sl], xg[:, e * G:(e + 1) * G], AF.Square)
            s.engine_nop().then_inc(*lnc.pinc(e))

    @blk.tensor
    def _(te):
        for e in range(ET):
            lnc.cwait(te, e)
            sl = slice((e % 2) * G, (e % 2) * G + G)
            mm1 = te.matmul(ps_st[0:1, 0:G], st["ones_col_b"][:, :], xb[:, sl],
                            start=(e == 0), stop=(e == ET - 1))
            mm2 = te.matmul(ps_st2[0:1, 0:G], st["ones_col_b"][:, :], sqb[:, sl],
                            start=(e == 0), stop=(e == ET - 1))
            if e == ET - 1:
                mm1.then_inc(*lnc.cinc(e))
                mm2.then_inc(*s_mm.inc(2))
            else:
                mm2.then_inc(*lnc.cinc(e))
    k_stat = s_mm.v

    @blk.scalar
    def _(s):
        s.wait_ge(s_mm.h, k_stat)
        s.activation(st["msq_row"][0:1, 0:G], ps_st[0:1, 0:G], AF.Square, scale=1.0 / E)
        s.drain()
        s.activation(st["m_row"][0:1, 0:G], ps_st[0:1, 0:G], AF.Copy, scale=1.0 / E)
        s.engine_nop().then_inc(*s_ev.inc())
    k_msq = s_ev.v

    @blk.vector
    def _(v):
        v.wait_ge(s_ev.h, k_msq)
        v.scalar_tensor_tensor(st["var_row"][0:1, 0:G], ps_st2[0:1, 0:G], 1.0 / E,
                               st["msq_row"][0:1, 0:G], ALU.mult, ALU.subtract)
        v.engine_nop().then_inc(*s_ev.inc())
    k_var = s_ev.v

    @blk.scalar
    def _(s):
        s.wait_ge(s_ev.h, k_var)
        s.activation(st["lnv_row"][0:1, 0:G], st["var_row"][0:1, 0:G], AF.Ln, bias=st["eps_t"][0:1, 0:1])
        s.drain()
        s.activation(st["r_row"][0:1, 0:G], st["lnv_row"][0:1, 0:G], AF.Exp, scale=-0.5)
        s.engine_nop().then_inc(*s_ev.inc())
    k_r = s_ev.v

    @blk.tensor
    def _(te):
        te.wait_ge(s_ev.h, k_r)
        mm3 = te.matmul(ps_mb[:, 0:G], st["ones_row_f"][0:1, :], st["m_row"][0:1, 0:G], start=True, stop=True)
        mm3 = te.matmul(ps_rb[:, 0:G], st["ones_row_f"][0:1, :], st["r_row"][0:1, 0:G], start=True, stop=True)
        if want_rcol:
            for tt in range(TPG):
                mm3 = te.matmul(ps_rc[:, tt:tt + 1], st["r_row"][0:1, tt * 128:(tt + 1) * 128],
                                st["one_f"][0:1, 0:1], start=True, stop=True)
        mm3.then_inc(*s_mm.inc())
    k_bc = s_mm.v

    @blk.vector
    def _(v):
        v.wait_ge(s_mm.h, k_bc)
        v.tensor_copy(rb_sb[:, 0:G], ps_rb[:, 0:G])
        if want_rcol:
            v.tensor_copy(rcol_sb[:, 0:TPG], ps_rc[:, 0:TPG])
        if xnc_sb is not None:
            for e in range(ET):
                v.tensor_tensor(xnc_sb[:, e * G:(e + 1) * G], xg[:, e * G:(e + 1) * G],
                                ps_mb[:, 0:G], ALU.subtract)
            v.drain()
        if x2n_out is not None:
            for e in range(ET):
                v.tensor_tensor(x2n_out[:, e * G:(e + 1) * G], xg[:, e * G:(e + 1) * G],
                                ps_mb[:, 0:G], ALU.subtract)
            v.drain()
            for e in range(ET):
                v.tensor_tensor(x2n_out[:, e * G:(e + 1) * G], x2n_out[:, e * G:(e + 1) * G],
                                rb_sb[:, 0:G], ALU.mult)
            v.drain()
        v.engine_nop().then_inc(*s_ev.inc())
    return s_ev.v


def make_stats(b):
    G = b.cfg["G"]
    return dict(
        m_row=b.sbuf("st_m", [1, G], F32),
        msq_row=b.sbuf("st_msq", [1, G], F32),
        var_row=b.sbuf("st_var", [1, G], F32),
        lnv_row=b.sbuf("st_lnv", [1, G], F32),
        r_row=b.sbuf("st_r", [1, G], F32),
        magic_col=b.sbuf("st_mag", [128, 1], F32),
    )


def emit_attn_a(b, st, src, dst, wa_bf, wp_bf, ident_f, maskT_b):
    cfg = b.cfg
    nc = b.nc
    E, T, G, H, D, ACH = cfg["E"], cfg["T"], cfg["G"], cfg["H"], cfg["D"], cfg["ACH"]
    ET, TPG, NG, CPG = E // 128, G // 128, T // G, G // ACH
    HT = H // 2
    assert ACH == 128 and TPG == CPG and D == 64

    sc = Scope(b)
    xg = [sc.sbuf(f"a_xg{i}", [128, ET * G], F32) for i in range(2)]
    ubuf = sc.sbuf("a_ubuf", [128, 2 * ET * G], BF16)
    lnb = sc.sbuf("a_lnb", [128, 4 * G], BF16)
    xnc = sc.sbuf("a_xnc", [128, ET * G], BF16)
    rb_sb = sc.sbuf("a_rb", [128, G], F32)
    rcol = sc.sbuf("a_rcol", [128, TPG], F32)
    wstr = sc.sbuf("a_wstr", [128, 2 * ET * 128], BF16)
    wvstr = sc.sbuf("a_wvstr", [128, 2 * ET * 512], BF16)
    f32a = sc.sbuf("a_f32a", [128, 2 * G], F32)
    f32b = sc.sbuf("a_f32b", [128, 2 * G], F32)
    f32c = sc.sbuf("a_f32c", [128, 2 * G], F32)
    vbuf = sc.sbuf("a_vbuf", [128, TPG * H * 65], BF16)
    kbuf = sc.sbuf("a_kbuf", [128, TPG * E], BF16)
    scT = sc.sbuf("a_scT", [128, H * ACH], BF16)
    den2 = sc.sbuf("a_den2", [128, H], F32)
    Ssb = sc.sbuf("a_Ssb", [128, HT * 65], BF16)
    ysb = sc.sbuf("a_ysb", [128, TPG * E], F32)
    yT = sc.sbuf("a_yT", [128, ET * G], BF16)
    resg = sc.sbuf("a_resg", [128, 2 * G], F32)
    bk = sc.banks("a_bk", 7)
    qkb = ubuf

    def num_slot(h):
        if h < 7:
            return bk[2], h * 65
        if h < 14:
            return bk[3], (h - 7) * 65
        return bk[4], (h - 14) * 65

    def S_slot(h):
        hp = h // 2
        bank = bk[5] if hp < 7 else bk[6]
        return bank, (hp % 7) * 65, (h % 2) * 64

    ld_x = [b.sem("g_ldxA"), b.sem("g_ldxB")]
    s_st = b.sem("g_stres")

    with nc.Block() as blk:
        @blk.vector
        def _(v):
            v.memset(bass.AP(vbuf, 64, [[TPG * H * 65, 128], [65, TPG * H], [1, 1]]), 1.0)
            v.memset(bk[5][:, :], 0.0)
            v.memset(bk[6][:, :], 0.0)

    for g in range(NG):
        with nc.Block() as blk:
            @blk.sync
            def _(s):
                if g == 0:
                    s.dma_start(out=bass.AP(xg[0], 0, [[ET * G, 128], [G, ET], [1, G]]),
                                in_=bass.AP(src, 0, [[T, 128], [128 * T, ET], [1, G]])).then_inc(*ld_x[0].inc(16))
                if g + 1 < NG:
                    s.dma_start(out=bass.AP(xg[(g + 1) % 2], 0, [[ET * G, 128], [G, ET], [1, G]]),
                                in_=bass.AP(src, (g + 1) * G, [[T, 128], [128 * T, ET], [1, G]])).then_inc(*ld_x[(g + 1) % 2].inc(16))
                s.wait_ge(*ld_x[g % 2].tail())
        cur = xg[g % 2]

        with nc.Block() as blk:
            emit_ln_group(b, blk, st, bk, cur, lnb[:, 0:2 * G], lnb[:, 2 * G:4 * G],
                          xnc, rb_sb, rcol, True)

        with nc.Block() as blk:
            wpipe = Pipe(b, "g_w", 2, pq=16, dma_p=True)
            mpipe = Pipe(b, "g_m", 2)
            spipe = Pipe(b, "g_s", 2)
            NQK = 2 * ET

            @blk.sync
            def _(s):
                for i in range(NQK):
                    s.wait_ge(mpipe.cs[0].h, mpipe.cb[0] + max(0, i - 1))
                    sec, f = (0, i) if i < ET else (1, i - ET)
                    s.dma_start(out=bass.AP(wstr, (i % 2) * ET * 128,
                                            [[2 * ET * 128, 128], [128, ET], [1, 128]]),
                                in_=bass.AP(wa_bf, sec * E + f * 128,
                                            [[3 * E, 128], [128 * 3 * E, ET], [1, 128]])).then_inc(*wpipe.pinc_part(i))

            @blk.tensor
            def _(te):
                for i in range(NQK):
                    wpipe.cwait(te, i)
                    mpipe.pwait(te, i)
                    buf = wstr[:, (i % 2) * ET * 128:(i % 2) * ET * 128 + ET * 128]
                    for e in range(ET):
                        mm = te.matmul(bk[i % 2][:, 0:G], buf[:, e * 128:(e + 1) * 128],
                                       xnc[:, e * G:(e + 1) * G],
                                       start=(e == 0), stop=(e == ET - 1))
                    mm.then_inc(*mpipe.pinc(i))

            @blk.scalar
            def _(s):
                for i in range(NQK):
                    spipe.cwait(s, i)
                    sl = slice((i % 2) * G, (i % 2) * G + G)
                    s.activation(f32c[:, sl], f32b[:, sl], AF.Exp).then_inc(*spipe.cinc(i))

            @blk.vector
            def _(v):
                for i in range(NQK + 1):
                    if i < NQK:
                        mpipe.cwait(v, i)
                        sl = slice((i % 2) * G, (i % 2) * G + G)
                        v.tensor_tensor(f32a[:, sl], bk[i % 2][:, 0:G], rb_sb[:, 0:G], ALU.mult)
                        v.engine_nop().then_inc(*mpipe.cinc(i))
                        v.drain()
                        v.tensor_single_scalar(f32b[:, sl], f32a[:, sl], 0.0, ALU.min)
                        v.engine_nop().then_inc(*spipe.pinc(i))
                    if i >= 1:
                        j = i - 1
                        v.wait_ge(spipe.cs[0].h, spipe.cb[0] + j + 1)
                        sj = slice((j % 2) * G, (j % 2) * G + G)
                        v.scalar_tensor_tensor(qkb[:, j * G:(j + 1) * G], f32a[:, sj], 0.0,
                                               f32c[:, sj], ALU.max, ALU.add)
                        v.drain()

        HW_ = min(512, E)
        NH = E // HW_
        with nc.Block() as blk:
            wv = Pipe(b, "g_wv", 2, pq=16, dma_p=True)
            mv = Pipe(b, "g_mv", 1)
            kp = Pipe(b, "g_kp", 2)
            ke = Pipe(b, "g_ke", 2)

            @blk.sync
            def _(s):
                for r in range(2 * NH):
                    sec = 2 if r < NH else 1
                    half = r % NH
                    s.wait_ge(mv.cs[0].h, mv.cb[0] + max(0, r - 1))
                    s.dma_start(out=bass.AP(wvstr, (r % 2) * ET * 512,
                                            [[2 * ET * 512, 128], [512, ET], [1, HW_]]),
                                in_=bass.AP(wa_bf, sec * E + half * HW_,
                                            [[3 * E, 128], [128 * 3 * E, ET], [1, HW_]])).then_inc(*wv.pinc_part(r))

            @blk.tensor
            def _(te):
                for r in range(2 * NH):
                    mv.pwait(te, r)
                    wv.cwait(te, r)
                    for e in range(ET):
                        for tt in range(TPG):
                            mm = te.matmul(bk[tt][:, 0:HW_],
                                           xnc[:, e * G + tt * 128:e * G + (tt + 1) * 128],
                                           wvstr[:, ((r % 2) * ET + e) * 512:((r % 2) * ET + e) * 512 + HW_],
                                           start=(e == 0), stop=(e == ET - 1))
                    mm.then_inc(*mv.pinc(r))

            @blk.scalar
            def _(s):
                NJ = NH * TPG
                jx = 0
                for r in range(2 * NH):
                    mv.cwait(s, r)
                    half = r % NH
                    hph = HW_ // 64
                    for tt in range(TPG):
                        if r < NH:
                            out_ap = bass.AP(vbuf, tt * H * 65 + half * hph * 65,
                                             [[TPG * H * 65, 128], [65, hph], [1, 64]])
                            s.activation(out_ap, bk[tt][:, 0:HW_], AF.Copy, scale=rcol[:, tt:tt + 1])
                        else:
                            j = (r - NH) * TPG + tt
                            kp.pwait(s, j)
                            s.activation(f32a[:, (j % 2) * G:(j % 2) * G + HW_],
                                         bk[tt][:, 0:HW_], AF.Copy, scale=rcol[:, tt:tt + 1])
                            s.engine_nop().then_inc(*kp.pinc(j))
                            if j >= 1:
                                ke.cwait(s, j - 1)
                                slx = slice(((j - 1) % 2) * G, ((j - 1) % 2) * G + HW_)
                                s.activation(f32c[:, slx], f32b[:, slx], AF.Exp).then_inc(*ke.cinc(j - 1))
                    s.engine_nop().then_inc(*mv.cinc(r))
                ke.cwait(s, NJ - 1)
                slx = slice(((NJ - 1) % 2) * G, ((NJ - 1) % 2) * G + HW_)
                s.activation(f32c[:, slx], f32b[:, slx], AF.Exp).then_inc(*ke.cinc(NJ - 1))

            @blk.vector
            def _(v):
                for j in range(NH * TPG + 1):
                    if j < NH * TPG:
                        kp.cwait(v, j)
                        ke.pwait(v, j)
                        sl = slice((j % 2) * G, (j % 2) * G + HW_)
                        v.tensor_single_scalar(f32b[:, sl], f32a[:, sl], 0.0, ALU.min)
                        v.engine_nop().then_inc(*ke.pinc(j))
                    if j >= 1:
                        jj = j - 1
                        v.wait_ge(ke.cs[0].h, ke.cb[0] + jj + 1)
                        sjl = slice((jj % 2) * G, (jj % 2) * G + HW_)
                        half2, tt2 = divmod(jj, TPG)
                        v.scalar_tensor_tensor(
                            kbuf[:, tt2 * E + half2 * HW_:tt2 * E + (half2 + 1) * HW_],
                            f32a[:, sjl], 0.0, f32c[:, sjl], ALU.max, ALU.add)
                        v.drain()
                        v.engine_nop().then_inc(*kp.cinc(jj))

        with nc.Block() as blk:
            scp = Pipe(b, "g_scp", 1)
            nmp = Pipe(b, "g_nmp", 1)
            sup = Pipe(b, "g_sup", 1)
            msk = Pipe(b, "g_msk", 1)
            dnp = Pipe(b, "g_dn", 1)

            @blk.tensor
            def _(te):
                for c in range(CPG):
                    gc = g * CPG + c
                    for hp in range(HT):
                        i = c * HT + hp
                        scp.pwait(te, i)
                        for par in range(2):
                            kT = qkb[64 * par:64 * par + 64,
                                     (ET + hp) * G + c * 128:(ET + hp) * G + (c + 1) * 128]
                            qT = qkb[64 * par:64 * par + 64,
                                     hp * G + c * 128:hp * G + (c + 1) * 128]
                            mm = te.matmul(bk[par][:, 0:128], kT, qT, start=True, stop=True,
                                           tile_position=(64 * par, 0))
                        mm.then_inc(*scp.pinc(i))
                    msk.cwait(te, c)
                    nmp.pwait(te, c)
                    te.wait_ge(sup.cs[0].h, sup.cb[0] + c)
                    for h in range(H):
                        bankn, off = num_slot(h)
                        mmn = te.matmul(bankn[:, off:off + 65],
                                        scT[:, h * 128:(h + 1) * 128],
                                        vbuf[:, c * H * 65 + h * 65:c * H * 65 + (h + 1) * 65],
                                        start=True, stop=(gc == 0))
                        if gc > 0:
                            par = h % 2
                            mmn = te.matmul(bankn[:, off:off + 65],
                                            qkb[64 * par:64 * par + 64,
                                                (h // 2) * G + c * 128:(h // 2) * G + (c + 1) * 128],
                                            Ssb[64 * par:64 * par + 64,
                                                (h // 2) * 65:(h // 2) * 65 + 65],
                                            start=False, stop=True, tile_position=(64 * par, 0))
                    mmn.then_inc(*nmp.pinc(c))
                    sup.pwait(te, c)
                    for h in range(H):
                        bs, off, pbase = S_slot(h)
                        mms = te.matmul(bs[pbase:pbase + 64, off:off + 65],
                                        kbuf[:, c * E + h * 64:c * E + (h + 1) * 64],
                                        vbuf[:, c * H * 65 + h * 65:c * H * 65 + (h + 1) * 65],
                                        start=False, stop=(gc == NG * CPG - 1),
                                        skip_group_check=True, tile_position=(0, pbase))
                    mms.then_inc(*sup.pinc(c))

            @blk.vector
            def _(v):
                for c in range(CPG):
                    v.wait_ge(nmp.ps[0].h, nmp.pb[0] + c)
                    for hp in range(HT):
                        i = c * HT + hp
                        scp.cwait(v, i)
                        for par in range(2):
                            h = 2 * hp + par
                            v.tensor_tensor(scT[:, h * 128:(h + 1) * 128], bk[par][:, 0:128],
                                            maskT_b[:, :], ALU.mult)
                        v.engine_nop().then_inc(*scp.cinc(i))
                    v.engine_nop().then_inc(*msk.pinc(c))
                    nmp.cwait(v, c)
                    dnp.pwait(v, c)
                    nh = min(H, 7)
                    v.tensor_single_scalar(den2[:, 0:nh], bass.AP(bk[2], 64, [[512, 128], [65, nh]]),
                                           1e-6, ALU.add)
                    if H > 7:
                        v.tensor_single_scalar(den2[:, 7:14], bass.AP(bk[3], 64, [[512, 128], [65, 7]]),
                                               1e-6, ALU.add)
                    if H > 14:
                        v.tensor_single_scalar(den2[:, 14:16], bass.AP(bk[4], 64, [[512, 128], [65, 2]]),
                                               1e-6, ALU.add)
                    v.drain()
                    v.reciprocal(den2[:, 0:H], den2[:, 0:H])
                    v.drain()
                    v.engine_nop().then_inc(*dnp.pinc(c))

            @blk.scalar
            def _(s):
                for c in range(CPG):
                    dnp.cwait(s, c)
                    for h in range(H):
                        bankn, off = num_slot(h)
                        s.activation(ysb[:, c * E + h * 64:c * E + (h + 1) * 64],
                                     bankn[:, off:off + 64], AF.Copy, scale=den2[:, h:h + 1])
                    s.engine_nop().then_inc(*dnp.cinc(c))
                    s.engine_nop().then_inc(*nmp.cinc(c))
                    sup.cwait(s, c)
                    s.activation(Ssb[:, 0:min(HT, 7) * 65], bk[5][:, 0:min(HT, 7) * 65], AF.Copy)
                    if HT > 7:
                        s.activation(Ssb[:, 455:455 + (HT - 7) * 65], bk[6][:, 0:(HT - 7) * 65], AF.Copy)
                    s.engine_nop().then_inc(*sup.cinc(c))

        with nc.Block() as blk:
            tp = Pipe(b, "g_tp", 2)
            pj = Pipe(b, "g_pj", 2)
            rs = Pipe(b, "g_rs", 2, cq=16, dma_c=True)
            wp2 = Pipe(b, "g_w2", 2, pq=16, dma_p=True)

            @blk.sync
            def _(s):
                for eo in range(ET):
                    s.wait_ge(pj.cs[0].h, pj.cb[0] + max(0, eo - 1))
                    s.dma_start(out=bass.AP(wstr, (eo % 2) * ET * 128,
                                            [[2 * ET * 128, 128], [128, ET], [1, 128]]),
                                in_=bass.AP(wp_bf, eo * 128,
                                            [[E, 128], [128 * E, ET], [1, 128]])).then_inc(*wp2.pinc_part(eo))

            @blk.tensor
            def _(te):
                for i in range(TPG * ET):
                    tt, e = divmod(i, ET)
                    tp.pwait(te, i)
                    te.transpose(bk[i % 2][:, 0:128],
                                 ysb[:, tt * E + e * 128:tt * E + (e + 1) * 128],
                                 ident_f[:, :]).then_inc(*tp.pinc(i))
                te.wait_ge(tp.cs[0].h, tp.cb[0] + TPG * ET)
                for eo in range(ET):
                    wp2.cwait(te, eo)
                    pj.pwait(te, eo)
                    buf = wstr[:, (eo % 2) * ET * 128:(eo % 2) * ET * 128 + ET * 128]
                    for e in range(ET):
                        mm = te.matmul(bk[eo % 2][:, 0:G], buf[:, e * 128:(e + 1) * 128],
                                       yT[:, e * G:(e + 1) * G],
                                       start=(e == 0), stop=(e == ET - 1))
                    mm.then_inc(*pj.pinc(eo))

            @blk.vector
            def _(v):
                for i in range(TPG * ET):
                    tt, e = divmod(i, ET)
                    tp.cwait(v, i)
                    v.tensor_copy(yT[:, e * G + tt * 128:e * G + (tt + 1) * 128],
                                  bk[i % 2][:, 0:128]).then_inc(*tp.cinc(i))
                for eo in range(ET):
                    pj.cwait(v, eo)
                    rs.pwait(v, eo)
                    v.tensor_tensor(resg[:, (eo % 2) * G:(eo % 2) * G + G], bk[eo % 2][:, 0:G],
                                    cur[:, eo * G:(eo + 1) * G], ALU.add)
                    v.engine_nop().then_inc(*pj.cinc(eo))
                    v.engine_nop().then_inc(*rs.pinc(eo))

            @blk.gpsimd
            def _(gp):
                for eo in range(ET):
                    rs.cwait(gp, eo)
                    gp.dma_start(out=dst[eo * 128:(eo + 1) * 128, g * G:(g + 1) * G],
                                 in_=resg[:, (eo % 2) * G:(eo % 2) * G + G]).then_inc(*rs.cinc(eo))

    with nc.Block() as blk:
        @blk.sync
        def _(s):
            for sm in b.sem("g_rs_c0"), b.sem("g_rs_c1"):
                s.wait_ge(*sm.tail())
    sc.close()


def emit_allreduce_max(b, maxv, nv, grow, bank, ident_f, cc_in, cc_out, replica, tag):
    nc = b.nc
    red = b.sem("g_r")
    dm = b.sem("ar_dm")
    with nc.Block() as blk:
        @blk.tensor
        def _(te):
            te.matmul(bank[0:nv, 0:128], maxv[:, 0:nv], ident_f[:, :], start=True,
                      stop=True).then_inc(*red.inc())
        k_tr = red.v

        @blk.vector
        def _(v):
            v.wait_ge(red.h, k_tr)
            v.tensor_reduce(maxv[0:nv, nv:nv + 1], bank[0:nv, 0:128], AX.X, ALU.max)
            v.engine_nop().then_inc(*red.inc())
        k_red = red.v

        @blk.gpsimd
        def _(g):
            ccs = b.sem("ar_cc")
            g.wait_ge(red.h, k_red)
            g.dma_start(out=cc_in[0:1, 0:nv], in_=maxv[0:nv, nv:nv + 1]).then_inc(*dm.inc(16))
            g.wait_ge(*dm.tail())
            g.collective_compute("AllReduce", ALU.max, replica_groups=replica,
                                 ins=[cc_in[0:1, 0:nv]], outs=[cc_out[0:1, 0:nv]]).then_inc(*ccs.inc())
            g.wait_ge(*ccs.tail())
            g.dma_start(out=grow[0:1, 0:nv], in_=cc_out[0:1, 0:nv]).then_inc(*dm.inc(16))
            g.wait_ge(*dm.tail())


def emit_scales(b, st, grow, nv, scal, c0, bank, tag):
    nc = b.nc
    sv = b.sem("g_r")
    with nc.Block() as blk:
        @blk.vector
        def _(v):
            v.tensor_scalar(grow[0:1, 0:nv], grow[0:1, 0:nv], 1.0 / 32767.0, 1e-12,
                            ALU.mult, ALU.max)
            v.drain()
            v.reciprocal(grow[0:1, nv:2 * nv], grow[0:1, 0:nv])
            v.engine_nop().then_inc(*sv.inc())
        k_v = sv.v

        @blk.tensor
        def _(te):
            te.wait_ge(sv.h, k_v)
            te.matmul(bank[:, 0:2 * nv], st["ones_row_f"][0:1, :], grow[0:1, 0:2 * nv],
                      start=True, stop=True).then_inc(*sv.inc())
        k_t = sv.v

        @blk.vector
        def _(v):
            v.wait_ge(sv.h, k_t)
            for k in range(nv):
                v.tensor_copy(scal[:, c0 + 2 * k:c0 + 2 * k + 1], bank[:, k:k + 1])
                v.tensor_copy(scal[:, c0 + 2 * k + 1:c0 + 2 * k + 2], bank[:, nv + k:nv + k + 1])
            v.drain()


def emit_ffn(b, st, src, dst, wu_bf, wg_bf, wd_bf, sp_x2n, sp_up, sp_gate, sp_t,
             sp_opre, scal, cc_in, cc_out, ident_f, replica, tag):
    cfg = b.cfg
    nc = b.nc
    E, T, G, I = cfg["E"], cfg["T"], cfg["G"], cfg["I"]
    ET, IT, NG, TPG = E // 128, I // 128, T // G, G // 128
    CH = min(512, T)
    NCH = T // CH

    N = IT * NCH
    ITG = IT * G

    sc = Scope(b)
    xg = [sc.sbuf(f"{tag}_xg{i}", [128, ET * G], F32) for i in range(2)]
    maxb = sc.sbuf(f"{tag}_maxb", [128, 2 * IT * NG], F32)
    maxv = sc.sbuf(f"{tag}_maxv", [128, 8], F32)
    grow = sc.sbuf(f"{tag}_grow", [1, 8], F32)
    resg = sc.sbuf(f"{tag}_resg", [128, 2 * G], F32)
    bk = sc.banks(f"{tag}_bk", 7)

    sc1 = Scope(b)
    lnb = sc1.sbuf(f"{tag}_lnb", [128, 4 * G], BF16)
    rb_sb = sc1.sbuf(f"{tag}_rb", [128, G], F32)
    x2ng = sc1.sbuf(f"{tag}_x2n", [128, ET * G], F32)
    iq = sc1.sbuf(f"{tag}_iq", [128, ET * G], I16)
    xq = sc1.sbuf(f"{tag}_xq", [128, ET * T], BF16)
    wstr = sc1.sbuf(f"{tag}_wstr", [128, 2 * ET * 128], BF16)
    spst = sc1.sbuf(f"{tag}_spst", [128, 4 * 512], BF16)

    ld_x = [b.sem("g_ldxA"), b.sem("g_ldxB")]
    st_sp = [b.sem("g_stspA"), b.sem("g_stspB")]
    s_st = b.sem("g_stres")

    for g in range(NG):
        with nc.Block() as blk:
            @blk.sync
            def _(s):
                if g == 0:
                    s.dma_start(out=bass.AP(xg[0], 0, [[ET * G, 128], [G, ET], [1, G]]),
                                in_=bass.AP(src, 0, [[T, 128], [128 * T, ET], [1, G]])).then_inc(*ld_x[0].inc(16))
                if g + 1 < NG:
                    s.dma_start(out=bass.AP(xg[(g + 1) % 2], 0, [[ET * G, 128], [G, ET], [1, G]]),
                                in_=bass.AP(src, (g + 1) * G, [[T, 128], [128 * T, ET], [1, G]])).then_inc(*ld_x[(g + 1) % 2].inc(16))
                s.wait_ge(*ld_x[g % 2].tail())
        cur = xg[g % 2]
        x2c = x2ng[:, 0:ET * G]
        spill_guard = (st_sp[0].v, st_sp[1].v)
        with nc.Block() as blk:
            @blk.vector
            def _(v):
                v.wait_ge(st_sp[0].h, spill_guard[0])
                v.wait_ge(st_sp[1].h, spill_guard[1])
            emit_ln_group(b, blk, st, bk, cur, lnb[:, 0:2 * G], lnb[:, 2 * G:4 * G],
                          None, rb_sb, None, False, x2n_out=x2c)
            s_r = b.sem("g_r")

            @blk.vector
            def _(v):
                for e in range(ET):
                    v.tensor_reduce(maxv[:, 0:1] if False else maxb[:, g * ET + e:g * ET + e + 1],
                                    x2c[:, e * G:(e + 1) * G], AX.X, ALU.max,
                                    apply_absolute_value=True)
                v.engine_nop().then_inc(*s_r.inc())
            k_r = s_r.v

            @blk.gpsimd
            def _(gp):
                gp.wait_ge(s_r.h, k_r)
                gp.dma_start(out=bass.AP(sp_x2n, g * G, [[T, 128], [128 * T, ET], [1, G]]),
                             in_=bass.AP(x2ng, 0, [[ET * G, 128], [G, ET], [1, G]])).then_inc(*st_sp[g % 2].inc(16))
    with nc.Block() as blk:
        @blk.vector
        def _(v):
            v.tensor_reduce(maxv[:, 0:1], maxb[:, 0:NG * ET], AX.X, ALU.max)
            v.drain()
    emit_allreduce_max(b, maxv, 1, grow, bk[5], ident_f, cc_in, cc_out, replica, tag)
    emit_scales(b, st, grow, 1, scal, 0, bk[6], tag)

    with nc.Block() as blk:
        lp = Pipe(b, "g_lsp", 1, pq=16, dma_p=True)
        qp = Pipe(b, "g_qp", 1)

        @blk.sync
        def _(s):
            s.wait_ge(*st_sp[0].tail())
            s.wait_ge(*st_sp[1].tail())
            for g in range(NG):
                lp.pwait(s, g)
                s.dma_start(out=bass.AP(x2ng, 0, [[ET * G, 128], [G, ET], [1, G]]),
                            in_=bass.AP(sp_x2n, g * G, [[T, 128], [128 * T, ET], [1, G]])).then_inc(*lp.pinc(g))

        @blk.vector
        def _(v):
            for g in range(NG):
                lp.cwait(v, g)
                qp.pwait(v, g)
                v.tensor_scalar(x2ng[:, :], x2ng[:, :], scal[:, 1:2], MAGIC, ALU.mult, ALU.add)
                v.drain()
                v.tensor_single_scalar(iq[:, :], x2ng[:, :], MAGIC, ALU.subtract)
                v.engine_nop().then_inc(*qp.pinc(g))
                v.engine_nop().then_inc(*lp.cinc(g))

        @blk.scalar
        def _(s):
            for g in range(NG):
                qp.cwait(s, g)
                for e in range(ET):
                    s.activation(xq[:, e * T + g * G:e * T + (g + 1) * G],
                                 iq[:, e * G:(e + 1) * G], AF.Copy, scale=scal[:, 0:1])
                s.engine_nop().then_inc(*qp.cinc(g))

    with nc.Block() as blk:
        wp = Pipe(b, "g_w", 2, pq=16, dma_p=True)
        mp = Pipe(b, "g_mp", 4)
        ep = Pipe(b, "g_ep", 4, cq=16, dma_c=True)

        @blk.sync
        def _(s):
            for ft in range(2 * IT):
                s.wait_ge(mp.cs[0].h, mp.cb[0] + max(0, ft - 1) * NG)
                wsrc = wu_bf if ft < IT else wg_bf
                f = ft % IT
                s.dma_start(out=bass.AP(wstr, (ft % 2) * ET * 128,
                                        [[2 * ET * 128, 128], [128, ET], [1, 128]]),
                            in_=bass.AP(wsrc, f * 128,
                                        [[I, 128], [128 * I, ET], [1, 128]])).then_inc(*wp.pinc_part(ft))

        @blk.tensor
        def _(te):
            for ft in range(2 * IT):
                wp.cwait(te, ft)
                buf = wstr[:, (ft % 2) * ET * 128:(ft % 2) * ET * 128 + ET * 128]
                for g in range(NG):
                    i = ft * NG + g
                    mp.pwait(te, i)
                    for e in range(ET):
                        mm = te.matmul(bk[i % 4][:, 0:G], buf[:, e * 128:(e + 1) * 128],
                                       xq[:, e * T + g * G:e * T + (g + 1) * G],
                                       start=(e == 0), stop=(e == ET - 1))
                    mm.then_inc(*mp.pinc(i))

        @blk.scalar
        def _(s):
            srd = b.sem("g_rd")
            for i in range(2 * IT * NG):
                mp.cwait(s, i)
                ep.pwait(s, i)
                s.activation(spst[:, (i % 4) * 512:(i % 4) * 512 + G], bk[i % 4][:, 0:G],
                             AF.Copy)
                s.engine_nop().then_inc(*mp.cinc(i))
                s.engine_nop().then_inc(*srd.inc())

        @blk.vector
        def _(v):
            srd = b.sem("g_rd")
            base = srd.v - 2 * IT * NG
            for i in range(2 * IT * NG):
                v.wait_ge(srd.h, base + i + 1)
                v.tensor_reduce(maxb[:, i:i + 1], spst[:, (i % 4) * 512:(i % 4) * 512 + G],
                                AX.X, ALU.max, apply_absolute_value=True)
                v.engine_nop().then_inc(*ep.pinc(i))

        @blk.gpsimd
        def _(gp):
            for i in range(2 * IT * NG):
                ft, g = divmod(i, NG)
                spdst = sp_up if ft < IT else sp_gate
                f = ft % IT
                ep.cwait(gp, i)
                gp.dma_start(out=spdst[f * 128:(f + 1) * 128, g * G:(g + 1) * G],
                             in_=spst[:, (i % 4) * 512:(i % 4) * 512 + G]).then_inc(*ep.cinc(i))

    with nc.Block() as blk:
        @blk.vector
        def _(v):
            v.tensor_reduce(maxv[:, 0:1], maxb[:, 0:IT * NG], AX.X, ALU.max)
            v.tensor_reduce(maxv[:, 1:2], maxb[:, IT * NG:2 * IT * NG], AX.X, ALU.max)
            v.drain()
    emit_allreduce_max(b, maxv, 2, grow, bk[5], ident_f, cc_in, cc_out, replica, tag)
    emit_scales(b, st, grow, 2, scal, 2, bk[6], tag)
    with nc.Block() as blk:
        @blk.vector
        def _(v):
            v.tensor_single_scalar(scal[:, 10:11], scal[:, 4:5], 1.0 / 65536.0, ALU.mult)
            v.drain()

    sc1.close()
    sc2 = Scope(b)
    wstrD = sc2.sbuf(f"{tag}_wd", [128, ET * IT * 128], BF16)
    oT2 = sc2.sbuf(f"{tag}_oT", [128, 2 * ITG], BF16)
    fb1 = sc2.sbuf(f"{tag}_fb1", [128, 2 * CH], F32)
    fb2 = sc2.sbuf(f"{tag}_fb2", [128, 2 * CH], F32)
    fc1 = sc2.sbuf(f"{tag}_fc1", [128, 2 * CH], F32)
    prb = sc2.sbuf(f"{tag}_pr", [128, 2 * CH], F32)
    ib1 = sc2.sbuf(f"{tag}_ib1", [128, 4 * CH], I16)
    ib2 = sc2.sbuf(f"{tag}_ib2", [128, 2 * CH], I16)
    ub1 = sc2.sbuf(f"{tag}_ub1", [128, 2 * CH], U16)
    bb1 = sc2.sbuf(f"{tag}_bb1", [128, 3 * CH], BF16)
    gld = sc2.sbuf(f"{tag}_gld", [128, 3 * CH], BF16)
    uld = sc2.sbuf(f"{tag}_uld", [128, 3 * CH], BF16)
    magic = st["magic_col"]

    with nc.Block() as blk:
        cs = [b.sem(f"g_c{k}") for k in range(8)]
        cb = [x.v for x in cs]
        ld = [b.sem(n) for n in ("g_ldxA", "g_ldxB", "g_ldx")]
        ldb = [x.v for x in ld]
        sw = [b.sem(n) for n in ("g_ep_c0", "g_ep_c1", "g_ep_c2")]
        swb = [x.v for x in sw]

        @blk.sync
        def _(s):
            for k in range(4):
                s.wait_ge(*b.sem(f"g_ep_c{k}").tail())
            for i in range(N):
                it, ch = divmod(i, NCH)
                if i >= 3:
                    s.wait_ge(cs[0].h, cb[0] + i - 2)
                s.dma_start(out=gld[:, (i % 3) * CH:(i % 3) * CH + CH],
                            in_=sp_gate[it * 128:(it + 1) * 128, ch * CH:(ch + 1) * CH]).then_inc(*ld[i % 3].inc(16))

        @blk.scalar
        def _(s):
            def S1(i):
                s.wait_ge(ld[i % 3].h, ldb[i % 3] + (i // 3 + 1) * 16)
                if i >= 2:
                    s.wait_ge(cs[1].h, cb[1] + i - 1)
                s.activation(fb1[:, (i % 2) * CH:(i % 2) * CH + CH],
                             gld[:, (i % 3) * CH:(i % 3) * CH + CH],
                             AF.Identity, bias=magic[:, 0:1], scale=scal[:, 5:6]).then_inc(*cs[0].inc())
            def S3(j):
                s.wait_ge(cs[1].h, cb[1] + j + 1)
                if j >= 2:
                    s.wait_ge(cs[3].h, cb[3] + j - 1)
                s.activation(fb2[:, (j % 2) * CH:(j % 2) * CH + CH],
                             ib1[:, (j % 4) * CH:(j % 4) * CH + CH],
                             AF.Sigmoid, scale=scal[:, 4:5]).then_inc(*cs[2].inc())
            for i in range(N + 1):
                if i < N:
                    S1(i)
                if 0 <= i - 1 < N:
                    S3(i - 1)

        @blk.vector
        def _(v):
            def st2(i):
                v.wait_ge(cs[0].h, cb[0] + i + 1)
                if i >= 4:
                    v.wait_ge(cs[5].h, cb[5] + i - 3)
                v.tensor_single_scalar(ib1[:, (i % 4) * CH:(i % 4) * CH + CH],
                                       fb1[:, (i % 2) * CH:(i % 2) * CH + CH],
                                       MAGIC, ALU.subtract).then_inc(*cs[1].inc())
            def st4(j):
                v.wait_ge(cs[2].h, cb[2] + j + 1)
                if j >= 2:
                    v.wait_ge(cs[4].h, cb[4] + j - 1)
                v.tensor_scalar(fc1[:, (j % 2) * CH:(j % 2) * CH + CH],
                                fb2[:, (j % 2) * CH:(j % 2) * CH + CH],
                                65536.0, MAGIC, ALU.mult, ALU.add).then_inc(*cs[3].inc())
            def st5(k):
                v.wait_ge(cs[3].h, cb[3] + k + 1)
                if k >= 2:
                    v.wait_ge(cs[5].h, cb[5] + k - 1)
                v.tensor_scalar(ub1[:, (k % 2) * CH:(k % 2) * CH + CH],
                                fc1[:, (k % 2) * CH:(k % 2) * CH + CH],
                                MAGIC, 65535.0, ALU.subtract, ALU.min).then_inc(*cs[4].inc())
            def st6(l):
                v.wait_ge(cs[4].h, cb[4] + l + 1)
                if l >= 2:
                    v.wait_ge(cs[6].h, cb[6] + l - 1)
                v.tensor_tensor(prb[:, (l % 2) * CH:(l % 2) * CH + CH],
                                ib1[:, (l % 4) * CH:(l % 4) * CH + CH],
                                ub1[:, (l % 2) * CH:(l % 2) * CH + CH], ALU.mult).then_inc(*cs[5].inc())
            def st7(m):
                v.wait_ge(cs[5].h, cb[5] + m + 1)
                if m >= 3:
                    v.wait_ge(cs[7].h, cb[7] + m - 2)
                    v.wait_ge(sw[m % 3].h, swb[m % 3] + (m // 3) * 16)
                v.tensor_scalar(bb1[:, (m % 3) * CH:(m % 3) * CH + CH],
                                prb[:, (m % 2) * CH:(m % 2) * CH + CH],
                                scal[:, 10:11], None, ALU.mult).then_inc(*cs[6].inc())
            def st8(n):
                v.wait_ge(cs[6].h, cb[6] + n + 1)
                v.tensor_reduce(maxb[:, n:n + 1], bb1[:, (n % 3) * CH:(n % 3) * CH + CH],
                                AX.X, ALU.max, apply_absolute_value=True).then_inc(*cs[7].inc())
            for i in range(N + 5):
                if i < N:
                    st2(i)
                if 0 <= i - 1 < N:
                    st4(i - 1)
                if 0 <= i - 2 < N:
                    st5(i - 2)
                if 0 <= i - 3 < N:
                    st6(i - 3)
                if 0 <= i - 4 < N:
                    st7(i - 4)
                if 0 <= i - 5 < N:
                    st8(i - 5)

        @blk.gpsimd
        def _(gp):
            for i in range(N):
                it, ch = divmod(i, NCH)
                gp.wait_ge(cs[6].h, cb[6] + i + 1)
                gp.dma_start(out=sp_t[it * 128:(it + 1) * 128, ch * CH:(ch + 1) * CH],
                             in_=bb1[:, (i % 3) * CH:(i % 3) * CH + CH]).then_inc(*sw[i % 3].inc(16))

    with nc.Block() as blk:
        @blk.vector
        def _(v):
            v.tensor_reduce(maxv[:, 0:1], maxb[:, 0:IT * NCH], AX.X, ALU.max)
            v.drain()
    emit_allreduce_max(b, maxv, 1, grow, bk[5], ident_f, cc_in, cc_out, replica, tag)
    emit_scales(b, st, grow, 1, scal, 6, bk[6], tag)
    with nc.Block() as blk:
        @blk.vector
        def _(v):
            v.tensor_tensor(scal[:, 11:12], scal[:, 6:7], scal[:, 2:3], ALU.mult)
            v.drain()

    with nc.Block() as blk:
        cs = [b.sem(f"g_c{k}") for k in range(8)]
        cb = [x.v for x in cs]
        ld = [b.sem(n) for n in ("g_ldxA", "g_ldxB", "g_ldx")]
        ldb = [x.v for x in ld]
        lu = [b.sem(n) for n in ("g_w_p0", "g_w_p1", "g_w2_p0")]
        lub = [x.v for x in lu]
        sw = [b.sem(n) for n in ("g_ep_c0", "g_ep_c1", "g_ep_c2")]
        swb = [x.v for x in sw]

        @blk.sync
        def _(s):
            for k in range(3):
                s.wait_ge(sw[k].h, swb[k])
            for i in range(N):
                it, ch = divmod(i, NCH)
                if i >= 3:
                    s.wait_ge(cs[0].h, cb[0] + i - 2)
                s.dma_start(out=gld[:, (i % 3) * CH:(i % 3) * CH + CH],
                            in_=sp_t[it * 128:(it + 1) * 128, ch * CH:(ch + 1) * CH]).then_inc(*ld[i % 3].inc(16))
                if i >= 3:
                    s.wait_ge(cs[2].h, cb[2] + i - 2)
                s.dma_start(out=uld[:, (i % 3) * CH:(i % 3) * CH + CH],
                            in_=sp_up[it * 128:(it + 1) * 128, ch * CH:(ch + 1) * CH]).then_inc(*lu[i % 3].inc(16))

        @blk.scalar
        def _(s):
            for i in range(N):
                s.wait_ge(ld[i % 3].h, ldb[i % 3] + (i // 3 + 1) * 16)
                if i >= 2:
                    s.wait_ge(cs[1].h, cb[1] + i - 1)
                s.activation(fb1[:, (i % 2) * CH:(i % 2) * CH + CH],
                             gld[:, (i % 3) * CH:(i % 3) * CH + CH],
                             AF.Identity, bias=magic[:, 0:1], scale=scal[:, 7:8]).then_inc(*cs[0].inc())
                s.wait_ge(lu[i % 3].h, lub[i % 3] + (i // 3 + 1) * 16)
                if i >= 2:
                    s.wait_ge(cs[3].h, cb[3] + i - 1)
                s.activation(fb2[:, (i % 2) * CH:(i % 2) * CH + CH],
                             uld[:, (i % 3) * CH:(i % 3) * CH + CH],
                             AF.Identity, bias=magic[:, 0:1], scale=scal[:, 3:4]).then_inc(*cs[2].inc())

        @blk.vector
        def _(v):
            def st2(i):
                v.wait_ge(cs[0].h, cb[0] + i + 1)
                if i >= 2:
                    v.wait_ge(cs[4].h, cb[4] + i - 1)
                v.tensor_single_scalar(ib1[:, (i % 2) * CH:(i % 2) * CH + CH],
                                       fb1[:, (i % 2) * CH:(i % 2) * CH + CH],
                                       MAGIC, ALU.subtract).then_inc(*cs[1].inc())
            def st4(i):
                v.wait_ge(cs[2].h, cb[2] + i + 1)
                if i >= 2:
                    v.wait_ge(cs[4].h, cb[4] + i - 1)
                v.tensor_single_scalar(ib2[:, (i % 2) * CH:(i % 2) * CH + CH],
                                       fb2[:, (i % 2) * CH:(i % 2) * CH + CH],
                                       MAGIC, ALU.subtract).then_inc(*cs[3].inc())
            def st6(k):
                v.wait_ge(cs[4].h, cb[4] + k + 1)
                if k >= 3:
                    v.wait_ge(cs[6].h, cb[6] + k - 2)
                    v.wait_ge(sw[k % 3].h, swb[k % 3] + (k // 3) * 16)
                v.tensor_scalar(bb1[:, (k % 3) * CH:(k % 3) * CH + CH],
                                prb[:, (k % 2) * CH:(k % 2) * CH + CH],
                                scal[:, 11:12], None, ALU.mult).then_inc(*cs[5].inc())
            def st7(l):
                v.wait_ge(cs[5].h, cb[5] + l + 1)
                v.tensor_reduce(maxb[:, l:l + 1], bb1[:, (l % 3) * CH:(l % 3) * CH + CH],
                                AX.X, ALU.max, apply_absolute_value=True).then_inc(*cs[6].inc())
            def st5(j):
                v.wait_ge(cs[1].h, cb[1] + j + 1)
                v.wait_ge(cs[3].h, cb[3] + j + 1)
                if j >= 2:
                    v.wait_ge(cs[5].h, cb[5] + j - 1)
                v.tensor_tensor(prb[:, (j % 2) * CH:(j % 2) * CH + CH],
                                ib1[:, (j % 2) * CH:(j % 2) * CH + CH],
                                ib2[:, (j % 2) * CH:(j % 2) * CH + CH], ALU.mult).then_inc(*cs[4].inc())
            for i in range(N + 3):
                if i < N:
                    st2(i)
                    st4(i)
                if 0 <= i - 1 < N:
                    st5(i - 1)
                if 0 <= i - 2 < N:
                    st6(i - 2)
                if 0 <= i - 3 < N:
                    st7(i - 3)

        @blk.gpsimd
        def _(gp):
            for i in range(N):
                it, ch = divmod(i, NCH)
                gp.wait_ge(cs[5].h, cb[5] + i + 1)
                gp.dma_start(out=sp_opre[it * 128:(it + 1) * 128, ch * CH:(ch + 1) * CH],
                             in_=bb1[:, (i % 3) * CH:(i % 3) * CH + CH]).then_inc(*sw[i % 3].inc(16))

    with nc.Block() as blk:
        @blk.vector
        def _(v):
            v.tensor_reduce(maxv[:, 0:1], maxb[:, 0:IT * NCH], AX.X, ALU.max)
            v.drain()
    emit_allreduce_max(b, maxv, 1, grow, bk[5], ident_f, cc_in, cc_out, replica, tag)
    emit_scales(b, st, grow, 1, scal, 8, bk[6], tag)

    NB = NG * ET
    NN = NG * IT
    with nc.Block() as blk:
        cs = [b.sem(f"g_c{k}") for k in range(8)]
        cb = [x.v for x in cs]
        ld = [b.sem(n) for n in ("g_ldxA", "g_ldxB", "g_ldx")]
        ldb = [x.v for x in ld]
        sw = [b.sem(n) for n in ("g_ep_c0", "g_ep_c1", "g_ep_c2")]
        swb = [x.v for x in sw]
        rl = [b.sem(n) for n in ("g_wv_p0", "g_wv_p1")]
        rlb = [x.v for x in rl]
        wdl = [b.sem("g_w2_p1")]
        wdlb = [x.v for x in wdl]
        do = [b.sem(n) for n in ("g_rs_c0", "g_rs_c1")]
        dob = [x.v for x in do]
        sev = b.sem("g_ev")
        sevb = sev.v
        smm = b.sem("g_mm")
        smmb = smm.v

        @blk.sync
        def _(s):
            for k in range(3):
                s.wait_ge(sw[k].h, swb[k])
            s.dma_start(out=bass.AP(wstrD, 0,
                                    [[ET * IT * 128, 128], [IT * 128, ET], [128, IT], [1, 128]]),
                        in_=bass.AP(wd_bf, 0,
                                    [[E, 128], [128, ET], [128 * E, IT], [1, 128]])).then_inc(*wdl[0].inc(16))
            for j in range(NN):
                g, it = divmod(j, IT)
                if j >= 3:
                    s.wait_ge(cs[0].h, cb[0] + j - 2)
                s.dma_start(out=gld[:, (j % 3) * CH:(j % 3) * CH + G],
                            in_=sp_opre[it * 128:(it + 1) * 128, g * G:(g + 1) * G]).then_inc(*ld[j % 3].inc(16))

        @blk.scalar
        def _(s):
            def XLD(g):
                if g >= 2:
                    s.wait_ge(sev.h, sevb + (g - 1) * ET)
                s.dma_start(out=bass.AP(xg[g % 2], 0, [[ET * G, 128], [G, ET], [1, G]]),
                            in_=bass.AP(src, g * G, [[T, 128], [128 * T, ET], [1, G]])).then_inc(*rl[g % 2].inc(16))
            def S3(j):
                g, it = divmod(j, IT)
                s.wait_ge(cs[1].h, cb[1] + j + 1)
                if it == 0 and g >= 2:
                    s.wait_ge(sev.h, sevb + (g - 1) * ET)
                s.activation(oT2[:, (g % 2) * ITG + it * G:(g % 2) * ITG + it * G + G],
                             ib1[:, (j % 2) * CH:(j % 2) * CH + G],
                             AF.Copy, scale=scal[:, 8:9]).then_inc(*cs[2].inc())
            for j in range(NN):
                if j % IT == 0:
                    XLD(j // IT)
                S3(j)

        @blk.vector
        def _(v):
            def st1(j):
                v.wait_ge(ld[j % 3].h, ldb[j % 3] + (j // 3 + 1) * 16)
                if j >= 2:
                    v.wait_ge(cs[1].h, cb[1] + j - 1)
                v.tensor_scalar(fb1[:, (j % 2) * CH:(j % 2) * CH + G],
                                gld[:, (j % 3) * CH:(j % 3) * CH + G],
                                scal[:, 9:10], MAGIC, ALU.mult, ALU.add).then_inc(*cs[0].inc())
            def st2(j):
                v.wait_ge(cs[0].h, cb[0] + j + 1)
                if j >= 2:
                    v.wait_ge(cs[2].h, cb[2] + j - 1)
                v.tensor_single_scalar(ib1[:, (j % 2) * CH:(j % 2) * CH + G],
                                       fb1[:, (j % 2) * CH:(j % 2) * CH + G],
                                       MAGIC, ALU.subtract).then_inc(*cs[1].inc())
            def evac(idx):
                g, eo = divmod(idx, ET)
                v.wait_ge(smm.h, smmb + idx + 1)
                v.wait_ge(rl[g % 2].h, rlb[g % 2] + (g // 2 + 1) * 16)
                if idx >= 2:
                    v.wait_ge(do[idx % 2].h, dob[idx % 2] + (idx // 2) * 16)
                v.tensor_tensor(resg[:, (idx % 2) * G:(idx % 2) * G + G], bk[idx % 4][:, 0:G],
                                xg[g % 2][:, eo * G:(eo + 1) * G], ALU.add).then_inc(*sev.inc())
            nev = 0
            for j in range(NN + 1):
                if j < NN:
                    st1(j)
                if 0 <= j - 1 < NN:
                    st2(j - 1)
                if j >= IT and (j - IT) % 3 == 2 and nev < NB:
                    evac(nev)
                    nev += 1
            while nev < NB:
                evac(nev)
                nev += 1

        @blk.tensor
        def _(te):
            for g in range(NG):
                for eo in range(ET):
                    idx = g * ET + eo
                    if idx == 0:
                        te.wait_ge(wdl[0].h, wdlb[0] + 16)
                    if idx >= 4:
                        te.wait_ge(sev.h, sevb + idx - 3)
                    for it in range(IT):
                        if eo == 0:
                            te.wait_ge(cs[2].h, cb[2] + g * IT + it + 1)
                        mm = te.matmul(bk[idx % 4][:, 0:G],
                                       wstrD[:, (eo * IT + it) * 128:(eo * IT + it + 1) * 128],
                                       oT2[:, (g % 2) * ITG + it * G:(g % 2) * ITG + it * G + G],
                                       start=(it == 0), stop=(it == IT - 1))
                    mm.then_inc(*smm.inc())

        @blk.gpsimd
        def _(gp):
            for idx in range(NB):
                g, eo = divmod(idx, ET)
                gp.wait_ge(sev.h, sevb + idx + 1)
                gp.dma_start(out=dst[eo * 128:(eo + 1) * 128, g * G:(g + 1) * G],
                             in_=resg[:, (idx % 2) * G:(idx % 2) * G + G]).then_inc(*do[idx % 2].inc(16))

    with nc.Block() as blk:
        @blk.sync
        def _(s):
            for nm in ("g_rs_c0", "g_rs_c1"):
                s.wait_ge(*b.sem(nm).tail())
    sc2.close()
    sc.close()


def emit_attn_b(b, st, src, dst, waiv_bf, wmix_bf, db_sb, bmix_sb, sp_csw, sp_cd,
                ident_f, onesf):
    cfg = b.cfg
    nc = b.nc
    E, T, G, SCH = cfg["E"], cfg["T"], cfg["G"], cfg["SCH"]
    I3 = 3 * E
    ET, IT3, NG, TPG = E // 128, I3 // 128, T // G, G // 128
    SPG = G // SCH
    NSC = T // SCH

    sc = Scope(b)
    xg = [sc.sbuf(f"b_xg{i}", [128, ET * G], F32) for i in range(2)]
    lnb = sc.sbuf("b_lnb", [128, 4 * G], BF16)
    xnc = sc.sbuf("b_xnc", [128, ET * G], BF16)
    rb_sb = sc.sbuf("b_rb", [128, G], F32)
    wstr = sc.sbuf("b_wstr", [128, 2 * ET * 128], BF16)
    t_a = sc.sbuf("b_ta", [128, ET * G], F32)
    t_i = sc.sbuf("b_ti", [128, ET * G], F32)
    t_v = sc.sbuf("b_tv", [128, ET * G], F32)
    g1 = sc.sbuf("b_g1", [128, ET * G], F32)
    ctd = sc.sbuf("b_ctd", [128, ET * NSC], F32)
    cfs = sc.sbuf("b_cfs", [128, ET * NSC], F32)
    ccd = sc.sbuf("b_ccd", [128, ET * NSC], F32)
    wr2 = sc.sbuf("b_wr2", [128, ET * NSC], F32)
    cwc = sc.sbuf("b_cwc", [128, ET * NSC], F32)
    inc = sc.sbuf("b_inc", [128, ET * NSC], F32)
    hb2 = sc.sbuf("b_h", [128, 2 * ET * G], BF16)
    wmr = sc.sbuf("b_wmr", [128, ET * ET * 128], BF16)
    resg = sc.sbuf("b_resg", [128, 2 * G], F32)
    rstmask = sc.sbuf("b_rst", [128, G], F32)
    bk = sc.banks("b_bk", 6)

    with nc.Block() as blk:
        @blk.vector
        def _(v):
            v.memset(rstmask[:, :], 1.0)
            v.drain()
            v.memset(bass.AP(rstmask, 0, [[G, 128], [SCH, SPG]]), 0.0)

    ld_x = [b.sem("g_ldxA"), b.sem("g_ldxB")]
    st_sp = b.sem("g_stspA")
    s_st = b.sem("g_stres")

    for g in range(NG):
        with nc.Block() as blk:
            @blk.sync
            def _(s):
                if g == 0:
                    s.dma_start(out=bass.AP(xg[0], 0, [[ET * G, 128], [G, ET], [1, G]]),
                                in_=bass.AP(src, 0, [[T, 128], [128 * T, ET], [1, G]])).then_inc(*ld_x[0].inc(16))
                if g + 1 < NG:
                    s.dma_start(out=bass.AP(xg[(g + 1) % 2], 0, [[ET * G, 128], [G, ET], [1, G]]),
                                in_=bass.AP(src, (g + 1) * G, [[T, 128], [128 * T, ET], [1, G]])).then_inc(*ld_x[(g + 1) % 2].inc(16))
                s.wait_ge(*ld_x[g % 2].tail())
        cur = xg[g % 2]

        with nc.Block() as blk:
            emit_ln_group(b, blk, st, bk, cur, lnb[:, 0:2 * G], lnb[:, 2 * G:4 * G],
                          xnc, rb_sb, None, False)

        with nc.Block() as blk:
            wp = Pipe(b, "g_w", 2, pq=16, dma_p=True)
            mp = Pipe(b, "g_mp", 4)
            sg = Pipe(b, "g_sg", 4)

            @blk.sync
            def _(s):
                for ft in range(IT3):
                    s.wait_ge(mp.cs[0].h, mp.cb[0] + max(0, ft - 1))
                    s.dma_start(out=bass.AP(wstr, (ft % 2) * ET * 128,
                                            [[2 * ET * 128, 128], [128, ET], [1, 128]]),
                                in_=bass.AP(waiv_bf, ft * 128,
                                            [[3 * E, 128], [128 * 3 * E, ET], [1, 128]])).then_inc(*wp.pinc_part(ft))

            @blk.tensor
            def _(te):
                for ft in range(IT3):
                    wp.cwait(te, ft)
                    mp.pwait(te, ft)
                    buf = wstr[:, (ft % 2) * ET * 128:(ft % 2) * ET * 128 + ET * 128]
                    for e in range(ET):
                        mm = te.matmul(bk[ft % 4][:, 0:G], buf[:, e * 128:(e + 1) * 128],
                                       xnc[:, e * G:(e + 1) * G],
                                       start=(e == 0), stop=(e == ET - 1))
                    mm.then_inc(*mp.pinc(ft))

            @blk.vector
            def _(v):
                v.wait_ge(*st_sp.tail())
                for ft in range(IT3):
                    mp.cwait(v, ft)
                    if ft < ET:
                        dstb = t_a[:, ft * G:(ft + 1) * G]
                    elif ft < 2 * ET:
                        dstb = t_i[:, (ft - ET) * G:(ft - ET + 1) * G]
                    else:
                        dstb = t_v[:, (ft - 2 * ET) * G:(ft - 2 * ET + 1) * G]
                    v.tensor_tensor(dstb, bk[ft % 4][:, 0:G], rb_sb[:, 0:G], ALU.mult)
                    if ft < 2 * ET:
                        sg.pwait(v, ft)
                        v.engine_nop().then_inc(*sg.pinc(ft))
                    v.engine_nop().then_inc(*mp.cinc(ft))

            @blk.scalar
            def _(s):
                for ft in range(2 * ET):
                    sg.cwait(s, ft)
                    if ft < ET:
                        s.activation(t_a[:, ft * G:(ft + 1) * G], t_a[:, ft * G:(ft + 1) * G],
                                     AF.Sigmoid, bias=db_sb[:, ft:ft + 1])
                    else:
                        e = ft - ET
                        s.activation(t_i[:, e * G:(e + 1) * G], t_i[:, e * G:(e + 1) * G],
                                     AF.Sigmoid)
                    s.engine_nop().then_inc(*sg.cinc(ft))

        with nc.Block() as blk:
            R = ET
            import os as _os2
            _FRESH = "b" if _os2.environ.get("CHAIN_FRESH", "0") == "1" else "g"
            p1 = b.sem(f"{_FRESH}_c0"); v2 = b.sem(f"{_FRESH}_c1"); v3 = b.sem(f"{_FRESH}_c2")
            s4 = b.sem(f"{_FRESH}_c3"); p5 = b.sem(f"{_FRESH}_c4"); p6 = b.sem(f"{_FRESH}_c5")
            v7 = b.sem(f"{_FRESH}_s2"); s8 = b.sem(f"{_FRESH}_c7")
            v9 = b.sem(f"{_FRESH}_s1"); v10 = b.sem(f"{_FRESH}_c6")
            s11 = b.sem(f"{_FRESH}_l1"); s12 = b.sem(f"{_FRESH}_l2")
            p13 = b.sem(f"{_FRESH}_mv_p"); v14 = b.sem(f"{_FRESH}_mv_c")
            bs = {x: x.v for x in (p1, v2, v3, s4, p5, p6, v7, s8, v9, v10,
                                   s11, s12, p13, v14)}
            spb = st_sp.v
            def sl(r):
                return slice(r * G, (r + 1) * G)

            import os as _os
            _POOL = _os.environ.get("CHAIN_POOL", "1") == "1"

            @blk.gpsimd
            def _(gp):
                def R1(r):
                    gp.tensor_tensor(g1[:, sl(r)], t_a[:, sl(r)], t_a[:, sl(r)],
                                     ALU.mult).then_inc(*p1.inc())
                def R5(r):
                    gp.tensor_tensor(t_v[:, sl(r)], t_v[:, sl(r)], t_i[:, sl(r)],
                                     ALU.mult).then_inc(*p5.inc())
                def R6(r):
                    gp.wait_ge(p5.h, bs[p5] + r + 1)
                    gp.wait_ge(s4.h, bs[s4] + r + 1)
                    gp.tensor_tensor(t_v[:, sl(r)], t_v[:, sl(r)], g1[:, sl(r)],
                                     ALU.mult).then_inc(*p6.inc())
                def R13(r):
                    gp.wait_ge(p6.h, bs[p6] + r + 1)
                    gp.wait_ge(s12.h, bs[s12] + r + 1)
                    gp.tensor_tensor(t_v[:, sl(r)], t_v[:, sl(r)], t_a[:, sl(r)],
                                     ALU.mult).then_inc(*p13.inc())
                if _POOL:
                    for r in range(R):
                        R1(r)
                        R5(r)
                    for r in range(R):
                        R6(r)
                    for r in range(R):
                        R13(r)
                else:
                    for r in range(R):
                        R5(r)
                gp.wait_ge(v14.h, bs[v14] + R)
                gp.dma_start(out=bass.AP(sp_csw, g * G, [[T, 128], [128 * T, ET], [1, G]]),
                             in_=bass.AP(t_i, 0, [[ET * G, 128], [G, ET], [1, G]])).then_inc(*st_sp.inc(16))
                gp.wait_ge(s11.h, bs[s11] + R)
                gp.dma_start(out=bass.AP(sp_cd, g * G, [[T, 128], [128 * T, ET], [1, G]]),
                             in_=bass.AP(g1, 0, [[ET * G, 128], [G, ET], [1, G]])).then_inc(*st_sp.inc(16))

            @blk.vector
            def _(v):
                v.wait_ge(st_sp.h, spb)
                def R2(r):
                    v.wait_ge(p1.h, bs[p1] + r + 1)
                    v.tensor_scalar(g1[:, sl(r)], g1[:, sl(r)], -1.0, 1.0,
                                    ALU.mult, ALU.add).then_inc(*v2.inc())
                def R7(r):
                    v.wait_ge(p1.h, bs[p1] + r + 1)
                    v.tensor_single_scalar(t_a[:, sl(r)], t_a[:, sl(r)], 1e-10,
                                           ALU.max).then_inc(*v7.inc())
                def R3(r):
                    v.wait_ge(v2.h, bs[v2] + r + 1)
                    v.tensor_single_scalar(g1[:, sl(r)], g1[:, sl(r)], 1e-8,
                                           ALU.max).then_inc(*v3.inc())
                def R9(r):
                    v.wait_ge(s8.h, bs[s8] + r + 1)
                    v.wait_ge(p5.h, bs[p5] + r + 1)
                    v.tensor_tensor_scan(t_i[:, sl(r)], rstmask[:, 0:G], t_a[:, sl(r)],
                                         0.0, ALU.mult, ALU.add).then_inc(*v9.inc())
                def R10(r):
                    v.wait_ge(v9.h, bs[v9] + r + 1)
                    v.tensor_single_scalar(t_a[:, sl(r)], t_i[:, sl(r)], C0,
                                           ALU.max).then_inc(*v10.inc())
                def R14(r):
                    v.wait_ge(p13.h, bs[p13] + r + 1)
                    v.wait_ge(s11.h, bs[s11] + r + 1)
                    v.tensor_tensor_scan(t_i[:, sl(r)], rstmask[:, 0:G], t_v[:, sl(r)],
                                         0.0, ALU.mult, ALU.add).then_inc(*v14.inc())
                def R15(r):
                    v.wait_ge(v14.h, bs[v14] + r + 1)
                    v.wait_ge(s11.h, bs[s11] + r + 1)
                    v.tensor_copy(ctd[:, r * NSC + g * SPG:r * NSC + g * SPG + SPG],
                                  bass.AP(g1, r * G + SCH - 1, [[ET * G, 128], [SCH, SPG]]))
                    v.tensor_tensor(cfs[:, r * NSC + g * SPG:r * NSC + g * SPG + SPG],
                                    bass.AP(g1, r * G + SCH - 1, [[ET * G, 128], [SCH, SPG]]),
                                    bass.AP(t_i, r * G + SCH - 1, [[ET * G, 128], [SCH, SPG]]),
                                    ALU.mult)
                def R1v(r):
                    v.tensor_tensor(g1[:, sl(r)], t_a[:, sl(r)], t_a[:, sl(r)],
                                    ALU.mult).then_inc(*p1.inc())
                def R6v(r):
                    v.wait_ge(p5.h, bs[p5] + r + 1)
                    v.wait_ge(s4.h, bs[s4] + r + 1)
                    v.tensor_tensor(t_v[:, sl(r)], t_v[:, sl(r)], g1[:, sl(r)],
                                    ALU.mult).then_inc(*p6.inc())
                def R13v(r):
                    v.wait_ge(p6.h, bs[p6] + r + 1)
                    v.wait_ge(s12.h, bs[s12] + r + 1)
                    v.tensor_tensor(t_v[:, sl(r)], t_v[:, sl(r)], t_a[:, sl(r)],
                                    ALU.mult).then_inc(*p13.inc())
                for r in range(R + 4):
                    if r < R:
                        if not _POOL:
                            R1v(r)
                        R2(r)
                        R7(r)
                    if 0 <= r - 1 < R: R3(r - 1)
                    if not _POOL and 0 <= r - 2 < R: R6v(r - 2)
                    if 0 <= r - 2 < R: R9(r - 2)
                    if 0 <= r - 3 < R: R10(r - 3)
                for r in range(R):
                    if not _POOL:
                        R13v(r)
                    R14(r)
                    if r >= 1:
                        R15(r - 1)
                R15(R - 1)

            @blk.scalar
            def _(s):
                s4a = b.sem(f"{_FRESH}_r")
                b4a = s4a.v
                def R4a(r):
                    s.wait_ge(v3.h, bs[v3] + r + 1)
                    s.activation(g1[:, sl(r)], g1[:, sl(r)], AF.Ln).then_inc(*s4a.inc())
                def R4b(r):
                    s.wait_ge(s4a.h, b4a + r + 1)
                    s.activation(g1[:, sl(r)], g1[:, sl(r)], AF.Exp,
                                 scale=0.5).then_inc(*s4.inc())
                def R8(r):
                    s.wait_ge(v7.h, bs[v7] + r + 1)
                    s.activation(t_a[:, sl(r)], t_a[:, sl(r)], AF.Ln).then_inc(*s8.inc())
                def R11(r):
                    s.wait_ge(v9.h, bs[v9] + r + 1)
                    s.wait_ge(p6.h, bs[p6] + r + 1)
                    s.activation(g1[:, sl(r)], t_i[:, sl(r)], AF.Exp).then_inc(*s11.inc())
                def R12(r):
                    s.wait_ge(v10.h, bs[v10] + r + 1)
                    s.activation(t_a[:, sl(r)], t_a[:, sl(r)], AF.Exp,
                                 scale=-1.0).then_inc(*s12.inc())
                for r in range(R + 8):
                    if 0 <= r - 2 < R:
                        R4a(r - 2)
                        R8(r - 2)
                    if 0 <= r - 3 < R:
                        R4b(r - 3)
                    if 0 <= r - 8 < R:
                        R11(r - 8)
                        R12(r - 8)

    with nc.Block() as blk:
        l1 = b.sem("g_l1")
        l2 = b.sem("g_l2")

        @blk.vector
        def _(v):
            for e in range(ET):
                sl = slice(e * NSC, (e + 1) * NSC)
                v.tensor_single_scalar(wr2[:, sl], ctd[:, sl], 1e-10, ALU.max)
            v.engine_nop().then_inc(*l1.inc())
        k1 = l1.v

        @blk.scalar
        def _(s):
            s.wait_ge(l1.h, k1)
            for e in range(ET):
                sl = slice(e * NSC, (e + 1) * NSC)
                s.activation(wr2[:, sl], wr2[:, sl], AF.Ln)
            s.engine_nop().then_inc(*l2.inc())
        k2 = l2.v

        @blk.vector
        def _(v):
            v.wait_ge(l2.h, k2)
            for e in range(ET):
                sl = slice(e * NSC, (e + 1) * NSC)
                v.tensor_tensor_scan(ccd[:, sl], onesf[:, 0:NSC], wr2[:, sl], 0.0,
                                     ALU.mult, ALU.add)
            v.drain()
            for e in range(ET):
                sl = slice(e * NSC, (e + 1) * NSC)
                v.tensor_single_scalar(wr2[:, sl], ccd[:, sl], C0, ALU.max)
            v.engine_nop().then_inc(*l1.inc())
        k3 = l1.v

        @blk.scalar
        def _(s):
            s.wait_ge(l1.h, k3)
            for e in range(ET):
                sl = slice(e * NSC, (e + 1) * NSC)
                s.activation(ccd[:, sl], ccd[:, sl], AF.Exp)
            s.drain()
            for e in range(ET):
                sl = slice(e * NSC, (e + 1) * NSC)
                s.activation(wr2[:, sl], wr2[:, sl], AF.Exp, scale=-1.0)
            s.engine_nop().then_inc(*l2.inc())
        k4 = l2.v

        @blk.vector
        def _(v):
            v.wait_ge(l2.h, k4)
            for e in range(ET):
                sl = slice(e * NSC, (e + 1) * NSC)
                v.tensor_tensor(wr2[:, sl], cfs[:, sl], wr2[:, sl], ALU.mult)
            v.drain()
            for e in range(ET):
                sl = slice(e * NSC, (e + 1) * NSC)
                v.tensor_tensor_scan(cwc[:, sl], onesf[:, 0:NSC], wr2[:, sl], 0.0,
                                     ALU.mult, ALU.add)
            v.drain()
            v.memset(inc[:, :], 0.0)
            v.drain()
            for e in range(ET):
                v.tensor_tensor(inc[:, e * NSC + 1:(e + 1) * NSC],
                                ccd[:, e * NSC:(e + 1) * NSC - 1],
                                cwc[:, e * NSC:(e + 1) * NSC - 1], ALU.mult)
            v.drain()

    with nc.Block() as blk:
        lc = [b.sem("g_ldc"), b.sem("g_lsp_p0")]
        lcb = [x.v for x in lc]
        wl = b.sem("g_w2_p0")
        wlb = wl.v
        cst = b.sem("g_s1")
        cstb = cst.v
        cmm = b.sem("g_mm")
        cmmb = cmm.v
        cev = b.sem("g_ev")
        cevb = cev.v
        douts = [b.sem("g_rs_c0"), b.sem("g_rs_c1")]
        doutb = [x.v for x in douts]
        cdb = [t_a, t_v]
        cwb = [t_i, g1]
        NBm = NG * ET

        sx = [b.sem("g_w_p0"), b.sem("g_w_p1")]
        sxb = [x.v for x in sx]

        @blk.sync
        def _(s):
            s.wait_ge(*st_sp.tail())
            s.dma_start(out=bass.AP(wmr, 0,
                                    [[ET * ET * 128, 128], [ET * 128, ET], [128, ET], [1, 128]]),
                        in_=bass.AP(wmix_bf, 0,
                                    [[E, 128], [128, ET], [128 * E, ET], [1, 128]])).then_inc(*wl.inc(16))
            for g in range(NG):
                if g >= 2:
                    s.wait_ge(cst.h, cstb + g - 1)
                s.dma_start(out=bass.AP(cdb[g % 2], 0, [[ET * G, 128], [G, ET], [1, G]]),
                            in_=bass.AP(sp_cd, g * G, [[T, 128], [128 * T, ET], [1, G]])).then_inc(*lc[g % 2].inc(16))
                s.dma_start(out=bass.AP(cwb[g % 2], 0, [[ET * G, 128], [G, ET], [1, G]]),
                            in_=bass.AP(sp_csw, g * G, [[T, 128], [128 * T, ET], [1, G]])).then_inc(*lc[g % 2].inc(16))

        @blk.scalar
        def _(s):
            for g in range(NG):
                if g >= 2:
                    s.wait_ge(cev.h, cevb + (g - 1) * ET)
                s.dma_start(out=bass.AP(xg[g % 2], 0, [[ET * G, 128], [G, ET], [1, G]]),
                            in_=bass.AP(src, g * G, [[T, 128], [128 * T, ET], [1, G]])).then_inc(*sx[g % 2].inc(16))

        @blk.vector
        def _(v):
            def states(g):
                v.wait_ge(lc[g % 2].h, lcb[g % 2] + (g // 2 + 1) * 32)
                v.wait_ge(sx[g % 2].h, sxb[g % 2] + (g // 2 + 1) * 16)
                if g >= 2:
                    v.wait_ge(cmm.h, cmmb + (g - 1) * ET)
                op = None
                for e in range(ET):
                    for ch in range(SPG):
                        sl = slice(e * G + ch * SCH, e * G + (ch + 1) * SCH)
                        hsl = slice((g % 2) * ET * G + e * G + ch * SCH,
                                    (g % 2) * ET * G + e * G + (ch + 1) * SCH)
                        op = v.scalar_tensor_tensor(hb2[:, hsl], cwb[g % 2][:, sl],
                                                    inc[:, e * NSC + g * SPG + ch:e * NSC + g * SPG + ch + 1],
                                                    cdb[g % 2][:, sl], ALU.add, ALU.mult)
                op.then_inc(*cst.inc())
            def evac(idx):
                g, eo = divmod(idx, ET)
                v.wait_ge(cmm.h, cmmb + idx + 1)
                if idx >= 2:
                    v.wait_ge(douts[idx % 2].h, doutb[idx % 2] + (idx // 2) * 16)
                v.scalar_tensor_tensor(resg[:, (idx % 2) * G:(idx % 2) * G + G],
                                       bk[idx % 4][:, 0:G], bmix_sb[:, eo:eo + 1],
                                       xg[g % 2][:, eo * G:(eo + 1) * G],
                                       ALU.add, ALU.add).then_inc(*cev.inc())
            for g in range(NG):
                states(g)
                if g >= 1:
                    for eo in range(ET):
                        evac((g - 1) * ET + eo)
            for eo in range(ET):
                evac((NG - 1) * ET + eo)

        @blk.tensor
        def _(te):
            for g in range(NG):
                for eo in range(ET):
                    idx = g * ET + eo
                    if idx == 0:
                        te.wait_ge(wl.h, wlb + 16)
                    if eo == 0:
                        te.wait_ge(cst.h, cstb + g + 1)
                    if idx >= 4:
                        te.wait_ge(cev.h, cevb + idx - 3)
                    for e in range(ET):
                        mm = te.matmul(bk[idx % 4][:, 0:G],
                                       wmr[:, (eo * ET + e) * 128:(eo * ET + e + 1) * 128],
                                       hb2[:, (g % 2) * ET * G + e * G:(g % 2) * ET * G + (e + 1) * G],
                                       start=(e == 0), stop=(e == ET - 1))
                    mm.then_inc(*cmm.inc())

        @blk.gpsimd
        def _(gp):
            for idx in range(NBm):
                g, eo = divmod(idx, ET)
                gp.wait_ge(cev.h, cevb + idx + 1)
                gp.dma_start(out=dst[eo * 128:(eo + 1) * 128, g * G:(g + 1) * G],
                             in_=resg[:, (idx % 2) * G:(idx % 2) * G + G]).then_inc(*douts[idx % 2].inc(16))

    with nc.Block() as blk:
        @blk.sync
        def _(s):
            for sm in b.sem("g_rs_c0"), b.sem("g_rs_c1"):
                s.wait_ge(*sm.tail())
    sc.close()




def build_program(cfg, phases=('attn_a', 'ffn1', 'attn_b', 'ffn2')):
    E, T, G, I = cfg["E"], cfg["T"], cfg["G"], cfg["I"]
    ET = E // 128
    b = B(cfg)
    nc = b.nc
    replica = [list(range(cfg["n_cores"]))]

    xT = b.dram("xT", [E, T], F32, kind="ExternalInput")
    w_attn_T = b.dram("w_attn_T", [E, 3 * E], F32, kind="ExternalInput")
    w_proj_T = b.dram("w_proj_T", [E, E], F32, kind="ExternalInput")
    W_aiv = b.dram("W_aiv", [E, 3 * E], F32, kind="ExternalInput")
    W_mix = b.dram("W_mix", [E, E], F32, kind="ExternalInput")
    decay_bias = b.dram("decay_bias", [E, 1], F32, kind="ExternalInput")
    b_mix = b.dram("b_mix", [E, 1], F32, kind="ExternalInput")
    w_gate = b.dram("w_gate", [I, E], F32, kind="ExternalInput")
    w_up = b.dram("w_up", [I, E], F32, kind="ExternalInput")
    w_down = b.dram("w_down", [E, I], F32, kind="ExternalInput")
    identc_b = b.dram("identc_b", [128, 128], BF16, kind="ExternalInput")
    identc_f = b.dram("identc_f", [128, 128], F32, kind="ExternalInput")
    maskTc = b.dram("maskTc", [128, 128], BF16, kind="ExternalInput")
    outT = b.dram("outT", [E, T], F32, kind="ExternalOutput")

    res1 = b.dram("res1", [E, T], F32)
    res2 = b.dram("res2", [E, T], F32)
    res3 = b.dram("res3", [E, T], F32)
    sp_x2n = b.dram("sp_x2n", [E, T], F32)
    sp_up = b.dram("sp_up", [I, T], BF16)
    sp_gate = b.dram("sp_gate", [I, T], BF16)
    sp_t = b.dram("sp_t", [I, T], BF16)
    sp_opre = b.dram("sp_opre", [I, T], BF16)
    sp_csw = b.dram("sp_csw", [E, T], F32)
    sp_cd = b.dram("sp_cd", [E, T], F32)
    wa_bf = b.dram("wa_bf", [E, 3 * E], BF16)
    wp_bf = b.dram("wp_bf", [E, E], BF16)
    waiv_bf = b.dram("waiv_bf", [E, 3 * E], BF16)
    wmix_bf = b.dram("wmix_bf", [E, E], BF16)
    wdq_upT = b.dram("wdq_upT", [E, I], BF16)
    wdq_gateT = b.dram("wdq_gateT", [E, I], BF16)
    wdq_downT = b.dram("wdq_downT", [I, E], BF16)
    cc_in = b.dram("cc_in", [1, 2], F32)
    cc_out = b.dram("cc_out", [1, 2], F32)

    ones_col_b = b.sbuf("ones_col_b", [128, 1], BF16)
    ones_row_f = b.sbuf("ones_row_f", [1, 128], F32)
    one_f = b.sbuf("one_f", [1, 1], F32)
    onesf = b.sbuf("onesf", [128, 512], F32)
    ident_b = b.sbuf("ident_b", [128, 128], BF16)
    ident_f = b.sbuf("ident_f", [128, 128], F32)
    maskT_b = b.sbuf("maskT_b", [128, 128], BF16)
    db_sb = b.sbuf("db_sb", [128, ET], F32)
    bmix_sb = b.sbuf("bmix_sb", [128, ET], F32)
    eps_t = b.sbuf("eps_t", [1, 1], F32)
    scal = b.sbuf("scal", [128, 16], F32)
    st = make_stats(b)
    st.update(dict(ones_col_b=ones_col_b, ones_row_f=ones_row_f, one_f=one_f, eps_t=eps_t))

    with nc.Block() as blk:
        @blk.sync
        def _(s):
            sd = b.sem("c0")
            s.dma_start(out=ident_b[:, :], in_=identc_b[:, :]).then_inc(*sd.inc(16))
            s.dma_start(out=ident_f[:, :], in_=identc_f[:, :]).then_inc(*sd.inc(16))
            s.dma_start(out=maskT_b[:, :], in_=maskTc[:, :]).then_inc(*sd.inc(16))
            for e in range(ET):
                s.dma_start(out=db_sb[:, e:e + 1], in_=decay_bias[e * 128:(e + 1) * 128, :]).then_inc(*sd.inc(16))
                s.dma_start(out=bmix_sb[:, e:e + 1], in_=b_mix[e * 128:(e + 1) * 128, :]).then_inc(*sd.inc(16))
            s.wait_ge(*sd.tail())

        @blk.vector
        def _(v):
            v.memset(ones_col_b[:, :], 1.0)
            v.memset(ones_row_f[:, :], 1.0)
            v.memset(one_f[:, :], 1.0)
            v.memset(onesf[:, :], 1.0)
            v.memset(eps_t[:, :], 1e-5)
            v.memset(scal[:, :], 0.0)
            v.memset(st["magic_col"][:, :], MAGIC)

    emit_cast_bf16(b, w_attn_T, wa_bf, "ca")
    emit_cast_bf16(b, w_proj_T, wp_bf, "cp")
    emit_cast_bf16(b, W_aiv, waiv_bf, "cv")
    emit_cast_bf16(b, W_mix, wmix_bf, "cm")
    emit_weight_prep(b, w_up, wdq_upT, ident_b, "wu")
    emit_weight_prep(b, w_gate, wdq_gateT, ident_b, "wg")
    emit_weight_prep(b, w_down, wdq_downT, ident_b, "wd")

    chain = {"attn_a": res1, "ffn1": res2, "attn_b": res3, "ffn2": outT}
    last = None
    cur_src = xT
    for ph in phases:
        dst = chain[ph] if ph != phases[-1] else outT
        if ph == "attn_a":
            emit_attn_a(b, st, cur_src, dst, wa_bf, wp_bf, ident_f, maskT_b)
        elif ph in ("ffn1", "ffn2"):
            emit_ffn(b, st, cur_src, dst, wdq_upT, wdq_gateT, wdq_downT, sp_x2n, sp_up,
                     sp_gate, sp_t, sp_opre, scal, cc_in, cc_out, ident_f, replica, ph)
        elif ph == "attn_b":
            emit_attn_b(b, st, cur_src, dst, waiv_bf, wmix_bf, db_sb, bmix_sb, sp_csw,
                        sp_cd, ident_f, onesf)
        cur_src = dst

    return b


def host_inputs(cfg, core_x, weights):
    import ml_dtypes
    E = cfg["E"]
    bf = ml_dtypes.bfloat16
    m = {
        "xT": np.ascontiguousarray(core_x.T.astype(np.float32)),
        "w_attn_T": np.ascontiguousarray(weights["w_attn"].T.astype(np.float32)),
        "w_proj_T": np.ascontiguousarray(weights["w_proj"].T.astype(np.float32)),
        "W_aiv": np.ascontiguousarray(weights["W_aiv"].astype(np.float32)),
        "W_mix": np.ascontiguousarray(weights["W_mix"].astype(np.float32)),
        "decay_bias": np.ascontiguousarray(weights["decay_bias"].astype(np.float32).reshape(E, 1)),
        "b_mix": np.ascontiguousarray(weights["b_mix"].astype(np.float32).reshape(E, 1)),
        "w_gate": np.ascontiguousarray(weights["w_gate"].astype(np.float32)),
        "w_up": np.ascontiguousarray(weights["w_up"].astype(np.float32)),
        "w_down": np.ascontiguousarray(weights["w_down"].astype(np.float32)),
        "identc_b": np.eye(128, dtype=np.float32).astype(bf),
        "identc_f": np.eye(128, dtype=np.float32),
        "maskTc": np.tril(np.ones((128, 128), np.float32)).T.astype(bf),
    }
    return m


from concourse.bass_utils import run_bass_kernel_spmd

_PROG_CACHE = {}

def kernel(L0_attn_a_x, L0_attn_a_w_attn, L0_attn_a_w_proj, L0_attn_b_W_aiv,
           L0_attn_b_decay_bias, L0_attn_b_W_mix, L0_attn_b_b_mix,
           L0_ffn_w_gate, L0_ffn_w_up, L0_ffn_w_down):
    X = np.asarray(L0_attn_a_x)
    Bn, T, E = X.shape
    cfg = default_cfg(T=T)
    assert Bn == cfg["n_cores"] and E == cfg["E"]
    key = (T, E)
    if key not in _PROG_CACHE:
        _PROG_CACHE[key] = build_program(cfg)
    b = _PROG_CACHE[key]
    weights = dict(
        w_attn=np.asarray(L0_attn_a_w_attn), w_proj=np.asarray(L0_attn_a_w_proj),
        W_aiv=np.asarray(L0_attn_b_W_aiv), W_mix=np.asarray(L0_attn_b_W_mix),
        decay_bias=np.asarray(L0_attn_b_decay_bias), b_mix=np.asarray(L0_attn_b_b_mix),
        w_gate=np.asarray(L0_ffn_w_gate), w_up=np.asarray(L0_ffn_w_up),
        w_down=np.asarray(L0_ffn_w_down),
    )
    in_maps = [host_inputs(cfg, X[c], weights) for c in range(Bn)]
    res = run_bass_kernel_spmd(b.nc, in_maps, core_ids=list(range(Bn)))
    out = np.empty((Bn, T, E), np.float32)
    for c in range(Bn):
        out[c] = np.ascontiguousarray(np.asarray(res.results[c]["outT"]).T)
    return out
